# version 14
# baseline (speedup 1.0000x reference)
import numpy as np
import ml_dtypes
from contextlib import ExitStack

import concourse.bass as bass
import concourse.mybir as mybir
from concourse.bass_utils import run_bass_kernel_spmd

F32 = mybir.dt.float32
BF16 = mybir.dt.bfloat16
AF = mybir.ActivationFunctionType
ALU = mybir.AluOpType
AX = mybir.AxisListType

C = 2048
CC = 16
NH = 16
TKV = 2048
TOWN = 1024
NE = 12
FC = 11
ISQ = float(1.0 / np.sqrt(128.0))
LN_EPS = 1e-5

DEBUG_TAPS = ()

ENGN = ("tensor", "vector", "scalar", "gpsimd", "sync")
SEMS = ("pe", "ve", "se", "po")


class Emit:

    def __init__(self):
        self.prog = {e: [] for e in ENGN}
        self.count = {s: 0 for s in SEMS}
        self.lastw = {e: {} for e in ENGN}

    def getsem(self, s):
        if s not in self.count:
            self.count[s] = 0

    def wait(self, eng, tok):
        if tok is None:
            return
        if isinstance(tok, (list, tuple)) and tok and isinstance(tok[0], tuple):
            for t in tok:
                self.wait(eng, t)
            return
        sem, v = tok
        assert v <= self.count[sem], f"wait on future token {sem}:{v}"
        if self.lastw[eng].get(sem, 0) >= v:
            return
        self.lastw[eng][sem] = v
        self.prog[eng].append(("w", sem, v))

    _CHAIN = {"vector": "ve", "scalar": "se", "gpsimd": "po"}

    def op(self, eng, fn, inc=None, inc_n=1, waits=()):
        ch = self._CHAIN.get(eng)
        if ch is not None and inc == ch and self.count[ch] > 0:
            self.wait(eng, (ch, self.count[ch]))
        for t in waits:
            self.wait(eng, t)
        tok = None
        if inc is not None:
            self.count[inc] += inc_n
            tok = (inc, self.count[inc])
        self.prog[eng].append(("i", fn, inc, inc_n))
        return tok

    def snapshot(self):
        return [(s, self.count[s]) for s in ("pe", "ve", "se", "po")
                if self.count[s] > 0]


def build_nc():
    nc = bass.Bass("TRN2", target_bir_lowering=False)

    d_xkv = nc.dram_tensor("x_kv", [TKV, C], F32, kind="ExternalInput").ap()
    d_xown = nc.dram_tensor("x_own", [TOWN, C], F32, kind="ExternalInput").ap()
    d_dmask = nc.dram_tensor("dmask", [128, 64], BF16, kind="ExternalInput").ap()
    d_ident = nc.dram_tensor("ident", [128, 128], BF16, kind="ExternalInput").ap()
    d_wqk = nc.dram_tensor("wqk", [NH, 128, CC * 256], F32, kind="ExternalInput").ap()
    d_wv4 = nc.dram_tensor("wv4", [4, 128, CC * 512], F32, kind="ExternalInput").ap()
    d_battn = nc.dram_tensor("battn", [128, 48], F32, kind="ExternalInput").ap()
    d_wp = nc.dram_tensor("wp", [4, 128, CC * 512], F32, kind="ExternalInput").ap()
    d_bp = nc.dram_tensor("bp_row", [1, C], F32, kind="ExternalInput").ap()
    d_wr = nc.dram_tensor("wr", [128, 128], F32, kind="ExternalInput").ap()
    d_wsg = nc.dram_tensor("wsg", [128, 16], F32, kind="ExternalInput").ap()
    d_wg = nc.dram_tensor("wg", [NE * FC, 128, CC * 128], F32, kind="ExternalInput").ap()
    d_wu = nc.dram_tensor("wu", [NE * FC, 128, CC * 128], F32, kind="ExternalInput").ap()
    d_wd = nc.dram_tensor("wd", [NE * 4, 128, FC * 512], F32, kind="ExternalInput").ap()
    d_y = nc.dram_tensor("y", [TOWN, C], F32, kind="ExternalOutput").ap()

    def row3(ap3, i):
        return ap3[i:i + 1, :, :].rearrange("o p f -> p (o f)")

    em = Emit()

    def mm(out, lhsT, rhs, start, stop, inc=None, waits=()):
        return em.op(
            "tensor",
            lambda E, o=out, l=lhsT, r=rhs, st=start, sp=stop: E.matmul(
                o, l, r, start=st, stop=sp),
            inc=inc, waits=waits)

    def pe_tr(out, in_, ident, inc=None, waits=()):
        return em.op(
            "tensor",
            lambda E, o=out, i=in_, d=ident: E.transpose(o, i, d),
            inc=inc, waits=waits)

    def act(out, in_, func, bias=0.0, scale=1.0, accum_out=None, waits=()):
        return em.op(
            "scalar",
            lambda E, o=out, i=in_, f=func, b=bias, s=scale, a=accum_out:
            E.activation(o, i, f, bias=b, scale=s, accum_out=a),
            inc="se", waits=waits)

    def vec(fn, waits=()):
        return em.op("vector", fn, inc="ve", waits=waits)

    def pool(fn, waits=()):
        return em.op("gpsimd", fn, inc="po", waits=waits)

    def _dma(eng, sem, out, in_, waits):
        em.getsem(sem)
        pre = em.count[sem]
        if pre > 0:
            waits = list(waits) + [(sem, pre)]
        return em.op(eng,
                     lambda E, o=out, i=in_: E.dma_start(out=o, in_=i),
                     inc=sem, inc_n=16, waits=waits)

    def dma_w(out, in_, sem, waits=()):
        return _dma("gpsimd", sem, out, in_, waits)

    def dma_x(out, in_, sem, waits=()):
        return _dma("sync", sem, out, in_, waits)

    def dma_o(out, in_, waits=()):
        return _dma("sync", "dO", out, in_, waits)

    S = ExitStack()
    sA = ExitStack()
    sL = ExitStack()
    sA2 = ExitStack()
    sP = ExitStack()
    sPp = ExitStack()
    sF = ExitStack()

    id_bf = S.enter_context(nc.sbuf_tensor("id_bf", [128, 128], BF16))
    dmask = S.enter_context(nc.sbuf_tensor("dmask_sb", [128, 64], BF16))
    ones_bf = S.enter_context(nc.sbuf_tensor("ones_bf", [128, 1], BF16))
    ones1f = S.enter_context(nc.sbuf_tensor("ones1f", [1, 128], F32))
    ones1b = S.enter_context(nc.sbuf_tensor("ones1b", [1, 128], BF16))
    bv_bf = S.enter_context(nc.sbuf_tensor("bv_bf", [128, 16], BF16))
    battn = S.enter_context(nc.sbuf_tensor("battn_sb", [128, 48], F32))
    wr_f = S.enter_context(nc.sbuf_tensor("wr_f", [128, 128], F32))
    wr_bf = S.enter_context(nc.sbuf_tensor("wr_bf", [128, 128], BF16))
    wsg_bf = S.enter_context(nc.sbuf_tensor("wsg_bf", [128, 16], BF16))
    stats = S.enter_context(nc.sbuf_tensor("stats", [128, 32], F32))
    bnst = S.enter_context(nc.sbuf_tensor("bnst", [128, 64], F32))
    C_tm = S.enter_context(nc.sbuf_tensor("C_tm", [128, 64], F32))
    s_tm = S.enter_context(nc.sbuf_tensor("s_tm", [128, 8], F32))
    yallT = S.enter_context(nc.sbuf_tensor("yallT", [128, NH * TOWN], BF16))
    wv4_b = S.enter_context(nc.sbuf_tensor("wv4_b", [128, CC * 512], BF16))

    pA0 = S.enter_context(nc.psum_tensor("pA0", [128, 512], F32))
    pA1 = S.enter_context(nc.psum_tensor("pA1", [128, 512], F32))
    pS = S.enter_context(nc.psum_tensor("pS", [128, 512], F32))
    pA = (pA0, pA1)

    x1T = sA.enter_context(nc.sbuf_tensor("x1T", [128, CC * TKV], BF16))
    xqT = sA.enter_context(nc.sbuf_tensor("xqT", [128, CC * TOWN], BF16))
    wqk_b = sA.enter_context(nc.sbuf_tensor("wqk_b", [128, CC * 256], BF16))
    pA2 = sA.enter_context(nc.psum_tensor("pA2", [128, 512], F32))
    pYa = [sA.enter_context(nc.psum_tensor(f"pY{i}", [128, 512], F32))
           for i in range(2)]


    pT0 = sL.enter_context(nc.psum_tensor("pT0", [128, 1024], BF16))
    pT1 = sL.enter_context(nc.psum_tensor("pT1", [128, 1024], BF16))
    pT = (pT0, pT1)
    lnx = [sL.enter_context(nc.sbuf_tensor(f"lnx{i}", [128, C], F32)) for i in range(2)]
    lntm = [sL.enter_context(nc.sbuf_tensor(f"lntm{i}", [128, C], BF16)) for i in range(2)]

    x1T_v = x1T.ap().rearrange("p (cc t) -> p cc t", cc=CC)
    xqT_v = xqT.ap().rearrange("p (cc t) -> p cc t", cc=CC)

    dma_x(id_bf[:, :], d_ident[:, :], "dC")
    dma_x(dmask[:, :], d_dmask[:, :], "dC")
    dma_x(battn[:, :], d_battn[:, :], "dC")
    dma_x(wr_f[:, :], d_wr[:, :], "dC")
    t_id = t_dm = t_ba = t_wr = ("dC", 64)
    t_ones = vec(lambda E: E.memset(ones_bf[:, :], 1.0))
    t_ones1 = vec(lambda E: E.memset(ones1f[:, :], 1.0))
    t_ones1b = vec(lambda E: E.memset(ones1b[:, :], 1.0))
    t_wrbf = vec(lambda E: E.tensor_copy(out=wr_bf[:, :], in_=wr_f[:, :]),
                 waits=[t_wr])
    t_bvbf = vec(lambda E: E.tensor_copy(out=bv_bf[:, :], in_=battn[:, 32:48]),
                 waits=[t_ba])
    t_wsg = dma_w(wsg_bf[:, :], d_wsg[:, :], "dC2")
    t_wqk = dma_w(wqk_b[:, :], d_wqk[0], "dQK")
    t_wv4 = dma_w(wv4_b[:, :], d_wv4[0], "dV4")

    ln_cons = [[], []]
    tm_cons = [None, None]
    pT_rd = [None, None]
    x1T_tok = [None] * CC
    xqT_tok = [None] * 8

    def ln_stats_norm(b, src_f32_ap, dst_bf_ap, extra_norm_waits=()):
        st = stats.ap()
        col = lambda k: st[:, b * 16 + k: b * 16 + k + 1]
        var_c, t1_c, r_c, nmr_c = [col(k) for k in range(4, 8)]
        nm_c = col(3)
        bn = bnst.ap()[:, b * 32: b * 32 + 24].rearrange("p (c k) -> p c k", c=4)
        mv = bnst.ap()[:, b * 32 + 24: b * 32 + 26]
        for ch in range(4):
            t_bn = vec(lambda E, o=bn[:, ch, :],
                       i=src_f32_ap.rearrange("p (c n) -> p c n", c=4)[:, ch, :]:
                       E.bn_stats(o, i))
        t_ag = vec(lambda E, o=mv, i=bnst.ap()[:, b * 32: b * 32 + 24]:
                   E.bn_aggr(o, i), waits=[t_bn])
        vec(lambda E, o=var_c, i=mv: E.tensor_scalar_add(o, i[:, 1:2], LN_EPS),
            waits=[t_ag])
        tok_nm_ = vec(lambda E, o=nm_c, i=mv: E.tensor_scalar_mul(o, i[:, 0:1],
                                                                  -1.0))
        t_l1 = act(t1_c, var_c, AF.Ln, waits=[tok_nm_])
        act(r_c, t1_c, AF.Exp, scale=-0.5, waits=[t_l1])
        em.op("scalar", lambda E, o=nmr_c, i=nm_c, r=r_c: E.mul(o, i, r),
              inc="se")
        tok_nm = act(dst_bf_ap, src_f32_ap, AF.Identity, bias=nmr_c, scale=r_c,
                     waits=list(extra_norm_waits))
        return tok_nm, None

    pend = [None]

    def ln_transpose_flush(idx):
        b, tok_nm, destT_v, dest_tok_list, dest_i = pend[0]
        last_ev = None
        for g in range(4):
            pb = (idx * 4 + g) % 2
            for j in range(4):
                cc = 4 * g + j
                tk = pe_tr(pT[pb][:, j * 128:(j + 1) * 128],
                           lntm[b][:, cc * 128:(cc + 1) * 128], id_bf[:, :],
                           inc="pe" if j == 3 else None,
                           waits=([tok_nm, t_id, pT_rd[pb]] if j == 0 else []))
            dv = destT_v[:, 4 * g:4 * g + 4, dest_i * 128:(dest_i + 1) * 128]
            sv = pT[pb].ap()[:, 0:512].rearrange("p (c n) -> p c n", c=4)
            last_ev = vec(lambda E, o=dv, i=sv: E.tensor_copy(out=o, in_=i),
                          waits=[tk])
            pT_rd[pb] = last_ev
        tm_cons[b] = last_ev
        dest_tok_list[dest_i] = last_ev
        pend[0] = None

    def ln_tile(idx, src_ap, destT_v, dest_tok_list, dest_i):
        b = idx % 2
        tok_d = dma_x(lnx[b][:, :], src_ap, f"dL{b}", waits=ln_cons[b])
        em.wait("vector", tok_d)
        tok_nm, _ = ln_stats_norm(b, lnx[b][:, :], lntm[b][:, :],
                                  extra_norm_waits=[tm_cons[b]])
        ln_cons[b] = [tok_nm]
        if pend[0] is not None:
            ln_transpose_flush(idx)
        pend[0] = (b, tok_nm, destT_v, dest_tok_list, dest_i)

    for i in range(CC):
        ln_tile(i, d_xkv[i * 128:(i + 1) * 128, :], x1T_v, x1T_tok, i)
    for i in range(8):
        ln_tile(CC + i, d_xown[i * 128:(i + 1) * 128, :], xqT_v, xqT_tok, i)
    ln_transpose_flush(CC + 8)

    ln1_end = em.snapshot()
    if "x1T" in DEBUG_TAPS:
        d_dbg_x1 = nc.dram_tensor("dbg_x1T", [128, CC * TKV], BF16, kind="ExternalOutput").ap()
        d_dbg_xq = nc.dram_tensor("dbg_xqT", [128, CC * TOWN], BF16, kind="ExternalOutput").ap()
        dma_o(d_dbg_x1, x1T[:, :], waits=[x1T_tok[CC - 1]] + ln1_end)
        dma_o(d_dbg_xq, xqT[:, :], waits=[xqT_tok[7]])
    sL.close()
    K_sb = sA2.enter_context(nc.sbuf_tensor("K_sb", [128, TKV], BF16))
    Q_sb = sA2.enter_context(nc.sbuf_tensor("Q_sb", [128, TOWN], BF16))
    V4_sb = sA2.enter_context(nc.sbuf_tensor("V4_sb", [128, 4 * TKV], BF16))
    PT_sb = sA2.enter_context(nc.sbuf_tensor("PT_sb", [128, 16 * 512], BF16))
    dg_scr = sA2.enter_context(nc.sbuf_tensor("dg_scr", [128, 64], BF16))
    ytmp2 = [sA2.enter_context(nc.sbuf_tensor(f"ytmp{i}", [128, 512], F32))
             for i in range(2)]
    linv2 = [sA2.enter_context(nc.sbuf_tensor(f"linv{i}", [128, 512], F32))
             for i in range(2)]
    rrow = sA2.enter_context(nc.sbuf_tensor("rrow", [1, 1024], F32))
    pYb = [sA2.enter_context(nc.psum_tensor(f"pYb{i}", [128, 512], F32))
           for i in range(2)]

    for _eng in ("tensor", "vector", "scalar", "gpsimd"):
        em.wait(_eng, ln1_end)

    pAh = (pA0, pA1, pA2)
    wp_tok = [None] * 4
    pYq = [[pYa[0], pYb[0]], [pYa[1], pYb[1]]]
    pA_rd = [None, None, None]
    pS_rd = []
    pY_rd = [[None, None], [None, None]]
    PT_rd_tok = [None] * 16
    wqk_tok = t_wqk
    wv4_tok = t_wv4
    ytmp_rd = [None, None]
    linv_rd = [None, None]
    dg_rd = None
    rrow_rd = [None]
    deferred = None
    pool_deferred = None

    for h in range(NH):
        hr = h % 4
        bq = battn[:, h:h + 1]
        bk = battn[:, 16 + h:16 + h + 1]
        bv = battn[:, 32 + h:32 + h + 1]

        k_ev = [None] * 4
        nu_p = 0
        for u in range(4):
            pb = nu_p % 3
            nu_p += 1
            for cc in range(CC):
                tk = mm(pAh[pb][:, :],
                        wqk_b[:, cc * 256 + 128: cc * 256 + 256],
                        x1T_v[:, cc, u * 512:(u + 1) * 512],
                        start=(cc == 0), stop=(cc == CC - 1),
                        inc="pe" if cc == CC - 1 else None,
                        waits=([wqk_tok, pA_rd[pb], x1T_tok[4 * u + 3]]
                               if cc == 0 else []))
            k_ev[u] = vec(lambda E, o=K_sb[:, u * 512:(u + 1) * 512],
                          i=pAh[pb], b=bk: E.tensor_scalar_add(o, i[:, :], b),
                          waits=[tk, t_ba])
            pA_rd[pb] = k_ev[u]
        q_ev = [None] * 2
        for u in range(2):
            pb = nu_p % 3
            nu_p += 1
            for cc in range(CC):
                tk = mm(pAh[pb][:, :],
                        wqk_b[:, cc * 256: cc * 256 + 128],
                        xqT_v[:, cc, u * 512:(u + 1) * 512],
                        start=(cc == 0), stop=(cc == CC - 1),
                        inc="pe" if cc == CC - 1 else None,
                        waits=([wqk_tok, pA_rd[pb], xqT_tok[4 * u + 3]]
                               if cc == 0 else []))
            q_ev[u] = vec(lambda E, o=Q_sb[:, u * 512:(u + 1) * 512],
                          i=pAh[pb], b=bq: E.tensor_scalar_add(o, i[:, :], b),
                          waits=[tk, t_ba])
            pA_rd[pb] = q_ev[u]
        wqk_mm_done = tk
        prod_last = ("pe", em.count["pe"])
        if h + 1 < NH:
            wqk_tok = dma_w(wqk_b[:, :], d_wqk[h + 1], "dQK",
                            waits=[wqk_mm_done])

        if hr == 0:
            v4_v = V4_sb.ap().rearrange("p (hr t) -> p hr t", hr=4)
            for u in range(4):
                for j in range(4):
                    kt = 4 * u + j
                    pb = nu_p % 3
                    nu_p += 1
                    for cc in range(CC):
                        tk = mm(pAh[pb][:, :],
                                x1T_v[:, cc, kt * 128:(kt + 1) * 128],
                                wv4_b[:, cc * 512:(cc + 1) * 512],
                                start=(cc == 0), stop=(cc == CC - 1),
                                inc="pe" if cc == CC - 1 else None,
                                waits=([wv4_tok, pA_rd[pb], x1T_tok[kt]]
                                       if cc == 0 else []))
                    dv = v4_v[:, :, kt * 128:(kt + 1) * 128]
                    sv = pAh[pb].ap().rearrange("p (c n) -> p c n", c=4)
                    ev = vec(lambda E, o=dv, i=sv: E.tensor_copy(out=o, in_=i),
                             waits=[tk])
                    pA_rd[pb] = ev
            if h + 4 < NH:
                wv4_tok = dma_w(wv4_b[:, :], d_wv4[h // 4 + 1], "dV4",
                                waits=[tk])
            else:
                wp_tok[0] = [dma_w(wv4_b[:, 2048 * j: 2048 * (j + 1)],
                                   d_wp[0][:, 2048 * j: 2048 * (j + 1)],
                                   "dPf0", waits=([tk] if j == 0 else []))
                             for j in range(4)]

        if deferred is not None:
            deferred()
            deferred = None

        ldone_q = [None, None]
        for qh in range(2):
            exp_tok = [None] * 16
            kcs = list(range(8 * qh + 8))
            lrow = pS[32 * qh: 32 * qh + 1, :]

            def emit_st(kc):
                pb = (kc + 2) % 3
                voff = max(0, kc - 8 * qh) * 64
                N = 512 - voff
                mm(pAh[pb][:, 0:N],
                   K_sb[:, kc * 128:(kc + 1) * 128],
                   Q_sb[:, qh * 512 + voff: (qh + 1) * 512],
                   start=True, stop=True, inc="pe",
                   waits=[k_ev[kc // 4], q_ev[qh], pA_rd[pb]])
                st_tok[kc] = ("pe", em.count["pe"])

            st_tok = [None] * 16
            LOOK = 3
            for kc in kcs[:LOOK]:
                emit_st(kc)
            for kc in kcs:
                pb = (kc + 2) % 3
                voff = max(0, kc - 8 * qh) * 64
                N = 512 - voff
                diag = kc >= 8 * qh
                msk = None
                if diag:
                    tdg = act(dg_scr[:, :], pAh[pb][:, 0:64], AF.Exp, scale=ISQ,
                              waits=[st_tok[kc], dg_rd])
                    msk = pool(lambda E,
                               o=PT_sb[:, kc * 512: kc * 512 + 64],
                               a=dg_scr, m=dmask:
                               E.tensor_mul(o, a[:, :], m[:, :]),
                               waits=[tdg, t_dm, PT_rd_tok[kc]])
                    dg_rd = msk
                    if N > 64:
                        exp_tok[kc] = act(PT_sb[:, kc * 512 + 64: kc * 512 + N],
                                          pAh[pb][:, 64:N], AF.Exp, scale=ISQ,
                                          waits=[PT_rd_tok[kc]])
                    else:
                        exp_tok[kc] = tdg
                else:
                    exp_tok[kc] = act(PT_sb[:, kc * 512: kc * 512 + N],
                                      pAh[pb][:, 0:N], AF.Exp, scale=ISQ,
                                      waits=[st_tok[kc], PT_rd_tok[kc]])
                pA_rd[pb] = exp_tok[kc]
                wl = [exp_tok[kc], t_ones]
                if msk is not None:
                    wl.append(msk)
                if kc == 0:
                    wl += pS_rd + [pY_rd[qh][h % 2]]
                mm(lrow[:, voff:512], ones_bf[:, :],
                   PT_sb[:, kc * 512: kc * 512 + N],
                   start=(kc == 0), stop=(kc == kcs[-1]), waits=wl)
                tv = mm(pYq[qh][h % 2][:, voff:512],
                        V4_sb[:, hr * TKV + kc * 128: hr * TKV + (kc + 1) * 128],
                        PT_sb[:, kc * 512: kc * 512 + N],
                        start=(kc == 0), stop=(kc == kcs[-1]), inc="pe")
                PT_rd_tok[kc] = tv
                if kc + LOOK in kcs:
                    emit_st(kc + LOOK)
            ldone_q[qh] = ("pe", em.count["pe"])

        def make_deferred(h=h, hr=hr, ldone=tuple(ldone_q)):
            def run():
                nonlocal pS_rd
                toks = []
                for qh in range(2):
                    lrow = pS[32 * qh: 32 * qh + 1, :]
                    t_ln = act(rrow[0:1, 0:512], lrow, AF.Ln,
                               waits=[ldone[qh], rrow_rd[0]])
                    t_li = act(rrow[0:1, 512:1024],
                               rrow[0:1, 0:512], AF.Exp, scale=-1.0,
                               waits=[t_ln])
                    toks.append(t_li)
                    t_bc = mm(pAh[qh][:, :], ones1f[0:1, :],
                              rrow[0:1, 512:1024],
                              start=True, stop=True, inc="pe",
                              waits=[t_li, t_ones1, pA_rd[qh]])
                    rrow_rd[0] = t_bc
                    t_lc = vec(lambda E, o=linv2[qh], i=pAh[qh]:
                               E.tensor_copy(out=o[:, :], in_=i[:, :]),
                               waits=[t_bc, linv_rd[qh]])
                    pA_rd[qh] = t_lc
                    dst = yallT[:, h * TOWN + qh * 512: h * TOWN + (qh + 1) * 512]
                    t_ym = vec(lambda E, o=dst, i=pYq[qh][h % 2],
                               l=linv2[qh]:
                               E.tensor_mul(o, i[:, :], l[:, :]),
                               waits=[t_lc, ldone[qh]])
                    pY_rd[qh][h % 2] = t_ym
                    linv_rd[qh] = t_ym
                pS_rd = toks
            return run

        deferred = make_deferred()

    deferred()
    deferred = None
    prod15 = prod_last

    sA2.close()
    sA.close()
    heads_end = em.snapshot()
    if "yallT" in DEBUG_TAPS:
        d_dbg_ya = nc.dram_tensor("dbg_yallT", [128, NH * TOWN], BF16, kind="ExternalOutput").ap()
        dma_o(d_dbg_ya, yallT[:, :], waits=heads_end)

    acc = S.enter_context(nc.sbuf_tensor("acc", [128, 8 * C], F32))
    x2T = S.enter_context(nc.sbuf_tensor("x2T", [128, CC * TOWN], BF16))
    x2T_v = x2T.ap().rearrange("p (cc t) -> p cc t", cc=CC)
    wg_b = [sF.enter_context(nc.sbuf_tensor("wg0", [128, CC * 128], BF16)),
            None]
    wu_b = [sF.enter_context(nc.sbuf_tensor("wu0", [128, CC * 128], BF16)),
            None]
    wp_b = [sP.enter_context(nc.sbuf_tensor(f"wp{i}", [128, CC * 512], BF16))
            for i in range(2)]
    acc_init = [None] * 8
    for tt in range(8):
        acc_init[tt] = dma_x(acc[:, tt * C:(tt + 1) * C],
                             d_xown[tt * 128:(tt + 1) * 128, :], f"dA{tt}",
                             waits=[prod15] if tt == 0 else [])
    wg_tok, wu_tok = {}, {}
    wg_rd = [None, None]
    wu_rd = [None, None]
    wg_tok[(0, 0)] = dma_w(wg_b[0][:, :], d_wg[0], "dG0", waits=[prod15])
    wu_tok[(0, 0)] = dma_w(wu_b[0][:, :], d_wu[0], "dU0")
    wpbufs = [wv4_b, wp_b[0], wp_b[1], wp_b[0]]
    for fg in (1, 2):
        wp_tok[fg] = [dma_w(wpbufs[fg][:, 2048 * j: 2048 * (j + 1)],
                            d_wp[fg][:, 2048 * j: 2048 * (j + 1)], f"dP{fg}_{j}",
                            waits=(heads_end if j == 0 else []))
                      for j in range(4)]
    bp_row = sP.enter_context(nc.sbuf_tensor("bp_sb", [1, C], BF16))
    bvp_sb = sP.enter_context(nc.sbuf_tensor("bvp_sb", [1, 2048], BF16))
    x2tm2 = [sP.enter_context(nc.sbuf_tensor(f"x2tm{i}", [128, C], BF16))
             for i in range(2)]
    lg_sb = sP.enter_context(nc.sbuf_tensor("lg_sb", [128, 8], F32))
    ex_sb = sP.enter_context(nc.sbuf_tensor("ex_sb", [128, 8], F32))
    ge_sb = sP.enter_context(nc.sbuf_tensor("ge_sb", [128, 8], F32))
    pT0b = sPp.enter_context(nc.psum_tensor("pT0b", [128, 1024], BF16))
    pT1b = sPp.enter_context(nc.psum_tensor("pT1b", [128, 1024], BF16))
    pTb = (pT0b, pT1b)

    t_bp = dma_w(bp_row[0:1, :], d_bp[0:1, :], "dBp", waits=heads_end)

    pT_rd2 = [None, None]
    x2_done = [None] * 8
    C_done = [None] * 8
    S_done = [None] * 8
    x2tm_rd = [None, None]
    pS_rd2 = [t for t in pS_rd]
    lg_rd = [None]
    ex_rd = [None]
    first_ln2 = [True]

    ln2_bf = {}

    def ln2_part1(tt, ready_tok):
        b = tt % 2
        src = acc[:, tt * C:(tt + 1) * C]
        if first_ln2[0]:
            em.wait("vector", heads_end)
            first_ln2[0] = False
        em.wait("vector", ready_tok)
        x2tm = x2tm2[tt % 2]
        tok_bf, _ = ln_stats_norm(b, src, x2tm[:, :],
                                  extra_norm_waits=[x2tm_rd[b]])
        ln2_bf[tt] = tok_bf

    def ln2_router(tt):
        b = tt % 2
        st = stats.ap()
        col = lambda k: st[:, b * 16 + k: b * 16 + k + 1]
        mx1, nmx, den, mx2, rde, s1, s2 = [col(k) for k in range(8, 15)]
        x2tm = x2tm2[tt % 2]
        tok_bf = ln2_bf[tt]
        last_ev = None
        for g in range(4):
            pb = (tt * 4 + g) % 2
            for j in range(4):
                cc = 4 * g + j
                tk = pe_tr(pTb[pb][:, j * 128:(j + 1) * 128],
                           x2tm[:, cc * 128:(cc + 1) * 128], id_bf[:, :],
                           inc="pe" if j == 3 else None,
                           waits=([tok_bf, pT_rd2[pb]] if j == 0 else []))
                _ = 0
            dv = x2T_v[:, 4 * g:4 * g + 4, tt * 128:(tt + 1) * 128]
            sv = pTb[pb].ap()[:, 0:512].rearrange("p (c n) -> p c n", c=4)
            last_ev = vec(lambda E, o=dv, i=sv: E.tensor_copy(out=o, in_=i),
                          waits=[tk])
            pT_rd2[pb] = last_ev
        x2tm_rd[b] = last_ev
        x2_done[tt] = last_ev
        for cc in range(CC):
            mm(pS[:, 0:8], x2T_v[:, cc, tt * 128:(tt + 1) * 128],
               wr_bf[:, cc * 8:(cc + 1) * 8],
               start=(cc == 0), stop=(cc == CC - 1),
               waits=([last_ev, t_wrbf] + pS_rd2) if cc == 0 else [])
        for cc in range(CC):
            tk = mm(pS[:, 8:9], x2T_v[:, cc, tt * 128:(tt + 1) * 128],
                    wsg_bf[:, cc:cc + 1],
                    start=(cc == 0), stop=(cc == CC - 1),
                    inc="pe" if cc == CC - 1 else None,
                    waits=[t_wsg] if cc == 0 else [])
        t_s1 = act(s1, pS[:, 8:9], AF.Exp, scale=-1.0, waits=[tk])
        t_s1b = vec(lambda E, o=s2, i=s1: E.tensor_scalar_add(o, i, 1.0),
                    waits=[t_s1])
        t_sln = act(s2, s2, AF.Ln, waits=[t_s1b])
        t_sg = act(s_tm.ap()[:, tt:tt + 1], s2, AF.Exp, scale=-1.0, waits=[t_sln])
        t_lg = vec(lambda E, o=lg_sb, i=pS: E.tensor_copy(out=o[:, :], in_=i[:, 0:8]),
                   waits=[tk, lg_rd[0]])
        pS_rd2[:] = [t_lg, t_s1]
        t_mx1 = vec(lambda E, o=mx1, i=lg_sb: E.reduce_max(o, i[:, :], axis=AX.X),
                    waits=[t_lg])
        t_nmx = vec(lambda E, o=nmx, i=mx1: E.tensor_scalar_mul(o, i, -1.0),
                    waits=[t_mx1])
        t_ex = act(ex_sb[:, :], lg_sb[:, :], AF.Exp, bias=nmx, accum_out=den,
                   waits=[t_nmx, ex_rd[0]])
        t_ge = vec(lambda E, o=ge_sb, i=lg_sb, m=mx1: E.tensor_scalar(
            out=o[:, :], in0=i[:, :], scalar1=m, scalar2=None, op0=ALU.is_ge),
            waits=[t_ex, t_mx1])
        t_lg2 = vec(lambda E, o=ge_sb, i=ge_sb, l=lg_sb: E.scalar_tensor_tensor(
            out=o[:, :], in0=i[:, :], scalar=-1e30, in1=l[:, :],
            op0=ALU.mult, op1=ALU.add), waits=[t_ge])
        t_mx2 = vec(lambda E, o=mx2, i=ge_sb: E.reduce_max(o, i[:, :], axis=AX.X),
                    waits=[t_lg2])
        t_sel = vec(lambda E, o=ge_sb, i=lg_sb, m=mx2: E.tensor_scalar(
            out=o[:, :], in0=i[:, :], scalar1=m, scalar2=None, op0=ALU.is_ge),
            waits=[t_mx2])
        lg_rd[0] = t_ex
        t_num = vec(lambda E, o=ex_sb, a=ex_sb, s=ge_sb:
                    E.tensor_mul(o[:, :], a[:, :], s[:, :]), waits=[t_sel, t_ex])
        t_rd = vec(lambda E, o=rde, i=den: E.reciprocal(o, i), waits=[t_ex])
        t_C = vec(lambda E, o=C_tm.ap()[:, tt * 8:(tt + 1) * 8], i=ex_sb, r=rde:
                  E.tensor_scalar_mul(o, i[:, :], r), waits=[t_num, t_rd])
        ex_rd[0] = t_C
        return t_C, t_sg


    stt_done = [[None] * 4 for _ in range(8)]
    first_proj = True
    pA_rd = pA_rd[:2]
    pS_bvp = pS_rd
    t_bvp = {}
    wp_last = [None] * 4
    for fg in range(4):
        wb = wpbufs[fg]
        for cc in range(CC):
            w0 = [wp_tok[fg][cc // 4]]
            if cc == 0:
                w0 += [t_bvbf] + pS_bvp
            if first_proj:
                w0 += heads_end
                first_proj = False
            tkr = mm(pS[0:1, 0:512], bv_bf[:, cc:cc + 1],
                     wb[:, cc * 512:(cc + 1) * 512],
                     start=(cc == 0), stop=(cc == CC - 1),
                     inc="pe" if cc == CC - 1 else None,
                     waits=w0)
        t_bvp[fg] = vec(lambda E, o=bvp_sb[0:1, fg * 512:(fg + 1) * 512],
                        i=pS[0:1, 0:512],
                        b=bp_row[0:1, fg * 512:(fg + 1) * 512]:
                        E.tensor_add(o, i, b),
                        waits=[tkr, t_bp])
        pS_bvp = [t_bvp[fg]]
        for tt in range(8):
            pb = tt % 2
            for cc in range(CC):
                w0 = [wp_tok[fg][cc // 4]]
                if cc == 0:
                    w0.append(pA_rd[pb])
                mm(pA[pb][:, :],
                   yallT[:, cc * TOWN + tt * 128:
                         cc * TOWN + (tt + 1) * 128],
                   wb[:, cc * 512:(cc + 1) * 512],
                   start=(cc == 0), stop=False, waits=w0)
            tk = mm(pA[pb][:, :], ones1b[0:1, :],
                    bvp_sb[0:1, fg * 512:(fg + 1) * 512],
                    start=False, stop=True, inc="pe",
                    waits=[t_bvp[fg], t_ones1b])
            wp_last[fg] = tk
            ev = vec(lambda E,
                     o=acc[:, tt * C + fg * 512: tt * C + (fg + 1) * 512],
                     i=pA[pb]: E.scalar_tensor_tensor(
                         out=o, in0=i[:, :], scalar=1.0, in1=o,
                         op0=ALU.mult, op1=ALU.add),
                     waits=[tk, acc_init[tt]])
            pA_rd[pb] = ev
            stt_done[tt][fg] = ev
            if fg == 3:
                ln2_part1(tt, ev)
                if tt > 1:
                    C_done[tt - 2], S_done[tt - 2] = ln2_router(tt - 2)
        if fg == 1:
            wp_tok[3] = [
                dma_w(wp_b[0][:, 2048 * j: 2048 * (j + 1)],
                      d_wp[3][:, 2048 * j: 2048 * (j + 1)], f"dP3_{j}",
                      waits=([wp_last[1]] if j == 0 else []))
                for j in range(4)]
    for _t in (6, 7):
        C_done[_t], S_done[_t] = ln2_router(_t)

    acc1_tok = [None] * 8
    if "acc1" in DEBUG_TAPS:
        d_dbg_a1 = nc.dram_tensor("dbg_acc1", [TOWN, C], F32, kind="ExternalOutput").ap()
        for tt in range(8):
            acc1_tok[tt] = dma_o(d_dbg_a1[tt * 128:(tt + 1) * 128, :],
                                 acc[:, tt * C:(tt + 1) * C],
                                 waits=[stt_done[tt][3]])
    if "x2T" in DEBUG_TAPS:
        d_dbg_x2 = nc.dram_tensor("dbg_x2T", [128, CC * TOWN], BF16, kind="ExternalOutput").ap()
        dma_o(d_dbg_x2, x2T[:, :], waits=[x2_done[tt] for tt in range(8)])
    if "Ctm" in DEBUG_TAPS:
        d_dbg_ct = nc.dram_tensor("dbg_Ctm", [128, 64], F32, kind="ExternalOutput").ap()
        d_dbg_st = nc.dram_tensor("dbg_stm", [128, 8], F32, kind="ExternalOutput").ap()
        dma_o(d_dbg_ct, C_tm[:, :], waits=[C_done[tt] for tt in range(8)])
        dma_o(d_dbg_st, s_tm[:, :], waits=[S_done[tt] for tt in range(8)])

    ffn_barrier = em.snapshot()

    sPp.close()
    sP.close()
    wg_b[1] = sF.enter_context(nc.sbuf_tensor("wg1", [128, CC * 128], BF16))
    wu_b[1] = sF.enter_context(nc.sbuf_tensor("wu1", [128, CC * 128], BF16))
    wg_tok[(0, 1)] = dma_w(wg_b[1][:, :], d_wg[1], "dG1", waits=ffn_barrier)
    wu_tok[(0, 1)] = dma_w(wu_b[1][:, :], d_wu[1], "dU1")
    wd_b = [wv4_b,
            sF.enter_context(nc.sbuf_tensor("wd1", [128, FC * 512], BF16))]
    H_sb = sF.enter_context(nc.sbuf_tensor("H_sb", [128, FC * TOWN], BF16))
    sgl = sF.enter_context(nc.sbuf_tensor("sgl", [128, TOWN], F32))
    pG = sF.enter_context(nc.psum_tensor("pG", [128, 1024], F32))
    pU = sF.enter_context(nc.psum_tensor("pU", [128, 1024], F32))

    wd_tok = {}
    wd_rd = [None, None]
    x2_all = list(x2_done)
    pG_rd = None
    pU_rd = None
    sgl_rd = None
    H_wr = [None] * FC
    acc_fin = [None] * 8
    first_ffn = [True]

    nu = 0
    for e in range(NE):
        for fg in (0, 1):
            j = (e * 4 + fg) % 2
            wd_tok[(e, fg)] = dma_w(
                wd_b[j][:, 0:FC * 512], d_wd[e * 4 + fg], f"dD{j}",
                waits=([wd_rd[j]] + (ffn_barrier if (e, fg) == (0, 0) else [])))
        for fc in range(FC):
            j = nu % 2
            w0 = [wg_tok[(e, fc)], pG_rd, x2_done[3]]
            if first_ffn[0]:
                w0 += ffn_barrier
                first_ffn[0] = False
            for half in range(2):
                for cc in range(CC):
                    tkG = mm(pG[:, half * 512:(half + 1) * 512],
                             wg_b[j][:, cc * 128:(cc + 1) * 128],
                             x2T_v[:, cc, half * 512:(half + 1) * 512],
                             start=(cc == 0), stop=(cc == CC - 1),
                             inc="pe" if (half == 1 and cc == CC - 1) else None,
                             waits=(w0 if half == 0 else [x2_done[7]])
                             if cc == 0 else [])
            for half in range(2):
                for cc in range(CC):
                    tkU = mm(pU[:, half * 512:(half + 1) * 512],
                             wu_b[j][:, cc * 128:(cc + 1) * 128],
                             x2T_v[:, cc, half * 512:(half + 1) * 512],
                             start=(cc == 0), stop=(cc == CC - 1),
                             inc="pe" if (half == 1 and cc == CC - 1) else None,
                             waits=[wu_tok[(e, fc)], pU_rd]
                             if (half == 0 and cc == 0) else [])
            ne_, nf_ = (e, fc + 1) if fc + 1 < FC else (e + 1, 0)
            if ne_ < NE and (ne_, nf_) not in wg_tok:
                wg_tok[(ne_, nf_)] = dma_w(wg_b[1 - j][:, :],
                                           d_wg[ne_ * FC + nf_], f"dG{1 - j}",
                                           waits=[wg_rd[1 - j]])
                wu_tok[(ne_, nf_)] = dma_w(wu_b[1 - j][:, :],
                                           d_wu[ne_ * FC + nf_], f"dU{1 - j}",
                                           waits=[wu_rd[1 - j]])
            wg_rd[j] = tkG
            wu_rd[j] = tkU
            t_si = act(sgl[:, :], pG[:, :], AF.Silu, waits=[tkG, sgl_rd])
            pG_rd = t_si
            t_H = vec(lambda E, o=H_sb[:, fc * TOWN:(fc + 1) * TOWN], s=sgl, u=pU:
                      E.tensor_mul(o, s[:, :], u[:, :]),
                      waits=[t_si, tkU])
            pU_rd = t_H
            sgl_rd = t_H
            H_wr[fc] = t_H
            nu += 1
        h_all = list(H_wr)
        for fg in range(4):
            j = (e * 4 + fg) % 2
            for tt in range(8):
                pb = tt % 2
                for kc in range(FC):
                    tk = mm(pA[pb][:, :],
                            H_sb[:, kc * TOWN + tt * 128:
                                 kc * TOWN + (tt + 1) * 128],
                            wd_b[j][:, kc * 512:(kc + 1) * 512],
                            start=(kc == 0), stop=(kc == FC - 1),
                            inc="pe" if kc == FC - 1 else None,
                            waits=([wd_tok[(e, fg)], pA_rd[pb]] + h_all)
                            if kc == 0 else [])
                gate = (C_tm.ap()[:, tt * 8 + e: tt * 8 + e + 1] if e < 8
                        else s_tm.ap()[:, tt:tt + 1])
                gw = [tk, C_done[tt] if e < 8 else S_done[tt]]
                if e == 0:
                    gw.append(stt_done[tt][fg])
                    gw.append(acc1_tok[tt])
                ev = vec(lambda E,
                         o=acc[:, tt * C + fg * 512: tt * C + (fg + 1) * 512],
                         i=pA[pb], g=gate: E.scalar_tensor_tensor(
                             out=o, in0=i[:, :], scalar=g, in1=o,
                             op0=ALU.mult, op1=ALU.add),
                         waits=gw)
                pA_rd[pb] = ev
                if e == NE - 1:
                    acc_fin[tt] = ev
            wd_rd[j] = tk
            if fg < 2:
                j2 = (e * 4 + fg + 2) % 2
                wd_tok[(e, fg + 2)] = dma_w(wd_b[j2][:, 0:FC * 512],
                                            d_wd[e * 4 + fg + 2],
                                            f"dD{j2}", waits=[wd_rd[j2] or tk])

    for tt in range(8):
        em.getsem(f"dO{tt}")
        em.op("sync",
              lambda E, o=d_y[tt * 128:(tt + 1) * 128, :],
              i=acc[:, tt * C:(tt + 1) * C]: E.dma_start(out=o, in_=i),
              inc=f"dO{tt}", inc_n=16, waits=[acc_fin[tt]])
    for tt in range(8):
        em.prog["sync"].append(("w", f"dO{tt}", 16))

    with ExitStack() as semstack:
        semh = {}
        for s in em.count:
            semh[s] = semstack.enter_context(nc.semaphore(f"{s}_s"))
        with nc.Block() as block:

            def replay(E, name):
                for it in em.prog[name]:
                    if it[0] == "w":
                        E.wait_ge(semh[it[1]], it[2])
                    else:
                        _, fn, inc, inc_n = it
                        ins = fn(E)
                        if inc is not None:
                            ins.then_inc(semh[inc], inc_n)

            @block.tensor
            def _(t):
                replay(t, "tensor")

            @block.vector
            def _(v):
                replay(v, "vector")

            @block.scalar
            def _(sc):
                replay(sc, "scalar")

            @block.gpsimd
            def _(g):
                replay(g, "gpsimd")

            @block.sync
            def _(s):
                replay(s, "sync")

    sF.close()
    S.close()
    return nc


def prep_inputs(inputs):
    bf = ml_dtypes.bfloat16
    Wa = np.asarray(inputs["L0_attn_W_attn"], dtype=np.float32)
    ba = np.asarray(inputs["L0_attn_b_attn"], dtype=np.float32)
    Wp = np.asarray(inputs["L0_attn_W_proj"], dtype=np.float32)
    bp = np.asarray(inputs["L0_attn_b_proj"], dtype=np.float32)
    Wr = np.asarray(inputs["L0_ffn_Wr"], dtype=np.float32)
    Wg = np.asarray(inputs["L0_ffn_Wg"], dtype=np.float32)
    Wu = np.asarray(inputs["L0_ffn_Wu"], dtype=np.float32)
    Wd = np.asarray(inputs["L0_ffn_Wd"], dtype=np.float32)
    Sg = np.asarray(inputs["L0_ffn_Sg"], dtype=np.float32)
    Su = np.asarray(inputs["L0_ffn_Su"], dtype=np.float32)
    Sd = np.asarray(inputs["L0_ffn_Sd"], dtype=np.float32)
    Wsg = np.asarray(inputs["L0_ffn_Wsg"], dtype=np.float32)

    w5 = Wa.reshape(CC, 128, 3, NH, 128)
    wqk = np.ascontiguousarray(
        w5[:, :, 0:2].transpose(3, 1, 0, 2, 4).reshape(NH, 128, CC * 256))
    wv4 = np.ascontiguousarray(
        w5[:, :, 2].reshape(CC, 128, 4, 4, 128).transpose(2, 1, 0, 3, 4)
        .reshape(4, 128, CC * 512))
    battn = np.ascontiguousarray(ba.reshape(48, 128).T)
    wp = np.ascontiguousarray(
        Wp.reshape(CC, 128, 4, 512).transpose(2, 1, 0, 3).reshape(4, 128, CC * 512))
    bp_row = np.ascontiguousarray(bp.reshape(1, C))
    wr = np.ascontiguousarray(
        Wr.reshape(CC, 128, 8).transpose(1, 0, 2).reshape(128, 128))
    wsg = np.ascontiguousarray(Wsg.reshape(CC, 128).T)
    wg_r = Wg.reshape(8, CC, 128, FC, 128).transpose(0, 3, 2, 1, 4)
    sg_r = Sg.reshape(CC, 128, 4, FC, 128).transpose(2, 3, 1, 0, 4)
    wg_all = np.ascontiguousarray(
        np.concatenate([wg_r, sg_r], axis=0).reshape(NE * FC, 128, CC * 128))
    wu_r = Wu.reshape(8, CC, 128, FC, 128).transpose(0, 3, 2, 1, 4)
    su_r = Su.reshape(CC, 128, 4, FC, 128).transpose(2, 3, 1, 0, 4)
    wu_all = np.ascontiguousarray(
        np.concatenate([wu_r, su_r], axis=0).reshape(NE * FC, 128, CC * 128))
    wd_r = Wd.reshape(8, FC, 128, 4, 512).transpose(0, 3, 2, 1, 4)
    sd_r = Sd.reshape(4, FC, 128, 4, 512).transpose(0, 3, 2, 1, 4)
    wd_all = np.ascontiguousarray(
        np.concatenate([wd_r, sd_r], axis=0).reshape(NE * 4, 128, FC * 512))
    ident = np.eye(128, dtype=bf)

    x = np.asarray(inputs["L0_attn_x"], dtype=np.float32)
    in_maps = []
    for core in range(8):
        b, h = core // 2, core % 2
        xb = x[b]
        own = np.ascontiguousarray(
            xb.reshape(CC, 128, C)[:, 64 * h:64 * h + 64, :].reshape(TOWN, C))
        dmask_np = ((np.arange(64)[None, :] + 64 * h)
                    >= np.arange(128)[:, None]).astype(bf)
        in_maps.append(dict(
            x_kv=np.ascontiguousarray(xb),
            x_own=own,
            dmask=np.ascontiguousarray(dmask_np),
            ident=ident,
            wqk=wqk, wv4=wv4, battn=battn, wp=wp, bp_row=bp_row,
            wr=wr, wsg=wsg, wg=wg_all, wu=wu_all, wd=wd_all,
        ))
    return in_maps


def kernel(**inputs):
    in_maps = prep_inputs(inputs)
    nc = build_nc()
    res = run_bass_kernel_spmd(nc, in_maps, core_ids=list(range(8)))
    out = np.empty((4, 2048, C), dtype=np.float32)
    for core in range(8):
        b, h = core // 2, core % 2
        yc = np.asarray(res.results[core]["y"]).reshape(CC, 64, C)
        out[b].reshape(CC, 128, C)[:, 64 * h:64 * h + 64, :] = yc
    return out


# version 15
# speedup vs baseline: 1.1519x; 1.1519x over previous
import numpy as np
import ml_dtypes
from contextlib import ExitStack

import concourse.bass as bass
import concourse.mybir as mybir
from concourse.bass_utils import run_bass_kernel_spmd

F32 = mybir.dt.float32
BF16 = mybir.dt.bfloat16
AF = mybir.ActivationFunctionType
ALU = mybir.AluOpType
AX = mybir.AxisListType

C = 2048
CC = 16
NH = 16
TKV = 2048
TOWN = 1024
NE = 12
FC = 11
ISQ = float(1.0 / np.sqrt(128.0))
LN_EPS = 1e-5

DEBUG_TAPS = ()

ENGN = ("tensor", "vector", "scalar", "gpsimd", "sync")
SEMS = ("pe", "ve", "se", "po")


class Emit:

    def __init__(self):
        self.prog = {e: [] for e in ENGN}
        self.count = {s: 0 for s in SEMS}
        self.lastw = {e: {} for e in ENGN}

    def getsem(self, s):
        if s not in self.count:
            self.count[s] = 0

    def wait(self, eng, tok):
        if tok is None:
            return
        if isinstance(tok, (list, tuple)) and tok and isinstance(tok[0], tuple):
            for t in tok:
                self.wait(eng, t)
            return
        sem, v = tok
        assert v <= self.count[sem], f"wait on future token {sem}:{v}"
        if self.lastw[eng].get(sem, 0) >= v:
            return
        self.lastw[eng][sem] = v
        self.prog[eng].append(("w", sem, v))

    _CHAIN = {"vector": "ve", "scalar": "se", "gpsimd": "po"}

    def op(self, eng, fn, inc=None, inc_n=1, waits=()):
        ch = self._CHAIN.get(eng)
        if ch is not None and inc == ch and self.count[ch] > 0:
            self.wait(eng, (ch, self.count[ch]))
        for t in waits:
            self.wait(eng, t)
        tok = None
        if inc is not None:
            self.count[inc] += inc_n
            tok = (inc, self.count[inc])
        self.prog[eng].append(("i", fn, inc, inc_n))
        return tok

    def snapshot(self):
        return [(s, self.count[s]) for s in ("pe", "ve", "se", "po")
                if self.count[s] > 0]


def build_nc():
    nc = bass.Bass("TRN2", target_bir_lowering=False)

    d_xkv = nc.dram_tensor("x_kv", [TKV, C], F32, kind="ExternalInput").ap()
    d_xown = nc.dram_tensor("x_own", [TOWN, C], F32, kind="ExternalInput").ap()
    d_dmask = nc.dram_tensor("dmask", [128, 64], BF16, kind="ExternalInput").ap()
    d_ident = nc.dram_tensor("ident", [128, 128], BF16, kind="ExternalInput").ap()
    d_wqk = nc.dram_tensor("wqk", [NH, 128, CC * 256], F32, kind="ExternalInput").ap()
    d_wv4 = nc.dram_tensor("wv4", [4, 128, CC * 512], F32, kind="ExternalInput").ap()
    d_battn = nc.dram_tensor("battn", [128, 48], F32, kind="ExternalInput").ap()
    d_wp = nc.dram_tensor("wp", [4, 128, CC * 512], F32, kind="ExternalInput").ap()
    d_bp = nc.dram_tensor("bp_row", [1, C], F32, kind="ExternalInput").ap()
    d_wr = nc.dram_tensor("wr", [128, 128], F32, kind="ExternalInput").ap()
    d_wsg = nc.dram_tensor("wsg", [128, 16], F32, kind="ExternalInput").ap()
    d_wg = nc.dram_tensor("wg", [NE * FC, 128, CC * 128], F32, kind="ExternalInput").ap()
    d_wu = nc.dram_tensor("wu", [NE * FC, 128, CC * 128], F32, kind="ExternalInput").ap()
    d_wd = nc.dram_tensor("wd", [NE * 4, 128, FC * 512], F32, kind="ExternalInput").ap()
    d_y = nc.dram_tensor("y", [TOWN, C], F32, kind="ExternalOutput").ap()

    def row3(ap3, i):
        return ap3[i:i + 1, :, :].rearrange("o p f -> p (o f)")

    em = Emit()

    def mm(out, lhsT, rhs, start, stop, inc=None, waits=()):
        return em.op(
            "tensor",
            lambda E, o=out, l=lhsT, r=rhs, st=start, sp=stop: E.matmul(
                o, l, r, start=st, stop=sp),
            inc=inc, waits=waits)

    def pe_tr(out, in_, ident, inc=None, waits=()):
        return em.op(
            "tensor",
            lambda E, o=out, i=in_, d=ident: E.transpose(o, i, d),
            inc=inc, waits=waits)

    def act(out, in_, func, bias=0.0, scale=1.0, accum_out=None, waits=()):
        return em.op(
            "scalar",
            lambda E, o=out, i=in_, f=func, b=bias, s=scale, a=accum_out:
            E.activation(o, i, f, bias=b, scale=s, accum_out=a),
            inc="se", waits=waits)

    def vec(fn, waits=()):
        return em.op("vector", fn, inc="ve", waits=waits)

    def pool(fn, waits=()):
        return em.op("gpsimd", fn, inc="po", waits=waits)

    def _dma(eng, sem, out, in_, waits):
        em.getsem(sem)
        pre = em.count[sem]
        if pre > 0:
            waits = list(waits) + [(sem, pre)]
        return em.op(eng,
                     lambda E, o=out, i=in_: E.dma_start(out=o, in_=i),
                     inc=sem, inc_n=16, waits=waits)

    def dma_w(out, in_, sem, waits=()):
        return _dma("gpsimd", sem, out, in_, waits)

    def dma_x(out, in_, sem, waits=()):
        return _dma("sync", sem, out, in_, waits)

    def dma_o(out, in_, waits=()):
        return _dma("sync", "dO", out, in_, waits)

    S = ExitStack()
    sA = ExitStack()
    sL = ExitStack()
    sA2 = ExitStack()
    sP = ExitStack()
    sPp = ExitStack()
    sF = ExitStack()

    id_bf = S.enter_context(nc.sbuf_tensor("id_bf", [128, 128], BF16))
    dmask = S.enter_context(nc.sbuf_tensor("dmask_sb", [128, 64], BF16))
    ones_bf = S.enter_context(nc.sbuf_tensor("ones_bf", [128, 1], BF16))
    ones1f = S.enter_context(nc.sbuf_tensor("ones1f", [1, 128], F32))
    ones1b = S.enter_context(nc.sbuf_tensor("ones1b", [1, 128], BF16))
    bv_bf = S.enter_context(nc.sbuf_tensor("bv_bf", [128, 16], BF16))
    battn = S.enter_context(nc.sbuf_tensor("battn_sb", [128, 48], F32))
    wr_f = S.enter_context(nc.sbuf_tensor("wr_f", [128, 128], F32))
    wr_bf = S.enter_context(nc.sbuf_tensor("wr_bf", [128, 128], BF16))
    wsg_bf = S.enter_context(nc.sbuf_tensor("wsg_bf", [128, 16], BF16))
    stats = S.enter_context(nc.sbuf_tensor("stats", [128, 32], F32))
    bnst = S.enter_context(nc.sbuf_tensor("bnst", [128, 64], F32))
    C_tm = S.enter_context(nc.sbuf_tensor("C_tm", [128, 64], F32))
    s_tm = S.enter_context(nc.sbuf_tensor("s_tm", [128, 8], F32))
    yallT = S.enter_context(nc.sbuf_tensor("yallT", [128, NH * TOWN], BF16))
    wv4_b = S.enter_context(nc.sbuf_tensor("wv4_b", [128, CC * 512], BF16))

    pA0 = S.enter_context(nc.psum_tensor("pA0", [128, 512], F32))
    pA1 = S.enter_context(nc.psum_tensor("pA1", [128, 512], F32))
    pS = S.enter_context(nc.psum_tensor("pS", [128, 512], F32))
    pA = (pA0, pA1)

    x1T = sA.enter_context(nc.sbuf_tensor("x1T", [128, CC * TKV], BF16))
    xqT = sA.enter_context(nc.sbuf_tensor("xqT", [128, CC * TOWN], BF16))
    wqk_b = sA.enter_context(nc.sbuf_tensor("wqk_b", [128, CC * 256], BF16))
    pA2 = sA.enter_context(nc.psum_tensor("pA2", [128, 512], F32))
    pYa = [sA.enter_context(nc.psum_tensor(f"pY{i}", [128, 512], F32))
           for i in range(2)]


    pT0 = sL.enter_context(nc.psum_tensor("pT0", [128, 1024], BF16))
    pT1 = sL.enter_context(nc.psum_tensor("pT1", [128, 1024], BF16))
    pT = (pT0, pT1)
    lnx = [sL.enter_context(nc.sbuf_tensor(f"lnx{i}", [128, C], F32)) for i in range(2)]
    lntm = [sL.enter_context(nc.sbuf_tensor(f"lntm{i}", [128, C], BF16)) for i in range(2)]

    x1T_v = x1T.ap().rearrange("p (cc t) -> p cc t", cc=CC)
    xqT_v = xqT.ap().rearrange("p (cc t) -> p cc t", cc=CC)

    dma_x(id_bf[:, :], d_ident[:, :], "dC")
    dma_x(dmask[:, :], d_dmask[:, :], "dC")
    dma_x(battn[:, :], d_battn[:, :], "dC")
    dma_x(wr_f[:, :], d_wr[:, :], "dC")
    t_id = t_dm = t_ba = t_wr = ("dC", 64)
    t_ones = vec(lambda E: E.memset(ones_bf[:, :], 1.0))
    t_ones1 = vec(lambda E: E.memset(ones1f[:, :], 1.0))
    t_ones1b = vec(lambda E: E.memset(ones1b[:, :], 1.0))
    t_wrbf = vec(lambda E: E.tensor_copy(out=wr_bf[:, :], in_=wr_f[:, :]),
                 waits=[t_wr])
    t_bvbf = vec(lambda E: E.tensor_copy(out=bv_bf[:, :], in_=battn[:, 32:48]),
                 waits=[t_ba])
    t_wsg = dma_w(wsg_bf[:, :], d_wsg[:, :], "dC2")
    t_wqk = dma_w(wqk_b[:, :], d_wqk[0], "dQK")
    t_wv4 = dma_w(wv4_b[:, :], d_wv4[0], "dV4")

    ln_cons = [[], []]
    tm_cons = [None, None]
    pT_rd = [None, None]
    x1T_tok = [None] * CC
    xqT_tok = [None] * 8

    def ln_stats_norm(b, src_f32_ap, dst_bf_ap, extra_norm_waits=()):
        st = stats.ap()
        col = lambda k: st[:, b * 16 + k: b * 16 + k + 1]
        var_c, t1_c, r_c, nmr_c = [col(k) for k in range(4, 8)]
        nm_c = col(3)
        bn = bnst.ap()[:, b * 32: b * 32 + 24].rearrange("p (c k) -> p c k", c=4)
        mv = bnst.ap()[:, b * 32 + 24: b * 32 + 26]
        for ch in range(4):
            t_bn = vec(lambda E, o=bn[:, ch, :],
                       i=src_f32_ap.rearrange("p (c n) -> p c n", c=4)[:, ch, :]:
                       E.bn_stats(o, i))
        t_ag = vec(lambda E, o=mv, i=bnst.ap()[:, b * 32: b * 32 + 24]:
                   E.bn_aggr(o, i), waits=[t_bn])
        vec(lambda E, o=var_c, i=mv: E.tensor_scalar_add(o, i[:, 1:2], LN_EPS),
            waits=[t_ag])
        tok_nm_ = vec(lambda E, o=nm_c, i=mv: E.tensor_scalar_mul(o, i[:, 0:1],
                                                                  -1.0))
        t_l1 = act(t1_c, var_c, AF.Ln, waits=[tok_nm_])
        act(r_c, t1_c, AF.Exp, scale=-0.5, waits=[t_l1])
        em.op("scalar", lambda E, o=nmr_c, i=nm_c, r=r_c: E.mul(o, i, r),
              inc="se")
        tok_nm = act(dst_bf_ap, src_f32_ap, AF.Identity, bias=nmr_c, scale=r_c,
                     waits=list(extra_norm_waits))
        return tok_nm, None

    pend = [None]

    def ln_transpose_flush(idx):
        b, tok_nm, destT_v, dest_tok_list, dest_i = pend[0]
        last_ev = None
        for g in range(4):
            pb = (idx * 4 + g) % 2
            for j in range(4):
                cc = 4 * g + j
                tk = pe_tr(pT[pb][:, j * 128:(j + 1) * 128],
                           lntm[b][:, cc * 128:(cc + 1) * 128], id_bf[:, :],
                           inc="pe" if j == 3 else None,
                           waits=([tok_nm, t_id, pT_rd[pb]] if j == 0 else []))
            dv = destT_v[:, 4 * g:4 * g + 4, dest_i * 128:(dest_i + 1) * 128]
            sv = pT[pb].ap()[:, 0:512].rearrange("p (c n) -> p c n", c=4)
            last_ev = vec(lambda E, o=dv, i=sv: E.tensor_copy(out=o, in_=i),
                          waits=[tk])
            pT_rd[pb] = last_ev
        tm_cons[b] = last_ev
        dest_tok_list[dest_i] = last_ev
        pend[0] = None

    def ln_tile(idx, src_ap, destT_v, dest_tok_list, dest_i):
        b = idx % 2
        tok_d = dma_x(lnx[b][:, :], src_ap, f"dL{b}", waits=ln_cons[b])
        em.wait("vector", tok_d)
        tok_nm, _ = ln_stats_norm(b, lnx[b][:, :], lntm[b][:, :],
                                  extra_norm_waits=[tm_cons[b]])
        ln_cons[b] = [tok_nm]
        if pend[0] is not None:
            ln_transpose_flush(idx)
        pend[0] = (b, tok_nm, destT_v, dest_tok_list, dest_i)

    for i in range(CC):
        ln_tile(i, d_xkv[i * 128:(i + 1) * 128, :], x1T_v, x1T_tok, i)
    for i in range(8):
        ln_tile(CC + i, d_xown[i * 128:(i + 1) * 128, :], xqT_v, xqT_tok, i)
    ln_transpose_flush(CC + 8)

    ln1_end = em.snapshot()
    if "x1T" in DEBUG_TAPS:
        d_dbg_x1 = nc.dram_tensor("dbg_x1T", [128, CC * TKV], BF16, kind="ExternalOutput").ap()
        d_dbg_xq = nc.dram_tensor("dbg_xqT", [128, CC * TOWN], BF16, kind="ExternalOutput").ap()
        dma_o(d_dbg_x1, x1T[:, :], waits=[x1T_tok[CC - 1]] + ln1_end)
        dma_o(d_dbg_xq, xqT[:, :], waits=[xqT_tok[7]])
    sL.close()
    K_sb = sA2.enter_context(nc.sbuf_tensor("K_sb", [128, TKV], BF16))
    Q_sb = sA2.enter_context(nc.sbuf_tensor("Q_sb", [128, TOWN], BF16))
    V4_sb = sA2.enter_context(nc.sbuf_tensor("V4_sb", [128, 4 * TKV], BF16))
    PT_sb = sA2.enter_context(nc.sbuf_tensor("PT_sb", [128, 16 * 512], BF16))
    dg_scr = sA2.enter_context(nc.sbuf_tensor("dg_scr", [128, 64], BF16))
    ytmp2 = [sA2.enter_context(nc.sbuf_tensor(f"ytmp{i}", [128, 512], F32))
             for i in range(2)]
    linv2 = [sA2.enter_context(nc.sbuf_tensor(f"linv{i}", [128, 512], F32))
             for i in range(2)]
    rrow = sA2.enter_context(nc.sbuf_tensor("rrow", [1, 1024], F32))
    pYb = [sA2.enter_context(nc.psum_tensor(f"pYb{i}", [128, 512], F32))
           for i in range(2)]

    for _eng in ("tensor", "vector", "scalar", "gpsimd"):
        em.wait(_eng, ln1_end)

    pAh = (pA0, pA1, pA2)
    wp_tok = [None] * 4
    pYq = [[pYa[0], pYb[0]], [pYa[1], pYb[1]]]
    pA_rd = [None, None, None]
    pS_rd = []
    pY_rd = [[None, None], [None, None]]
    PT_rd_tok = [None] * 16
    wqk_tok = t_wqk
    wv4_tok = t_wv4
    ytmp_rd = [None, None]
    linv_rd = [None, None]
    dg_rd = None
    rrow_rd = [None]
    deferred = None
    pool_deferred = None

    for h in range(NH):
        hr = h % 4
        bq = battn[:, h:h + 1]
        bk = battn[:, 16 + h:16 + h + 1]
        bv = battn[:, 32 + h:32 + h + 1]

        k_ev = [None] * 4
        nu_p = 0
        for u in range(4):
            pb = nu_p % 3
            nu_p += 1
            for cc in range(CC):
                tk = mm(pAh[pb][:, :],
                        wqk_b[:, cc * 256 + 128: cc * 256 + 256],
                        x1T_v[:, cc, u * 512:(u + 1) * 512],
                        start=(cc == 0), stop=(cc == CC - 1),
                        inc="pe" if cc == CC - 1 else None,
                        waits=([wqk_tok, pA_rd[pb], x1T_tok[4 * u + 3]]
                               if cc == 0 else []))
            k_ev[u] = vec(lambda E, o=K_sb[:, u * 512:(u + 1) * 512],
                          i=pAh[pb], b=bk: E.tensor_scalar_add(o, i[:, :], b),
                          waits=[tk, t_ba])
            pA_rd[pb] = k_ev[u]
        q_ev = [None] * 2
        for u in range(2):
            pb = nu_p % 3
            nu_p += 1
            for cc in range(CC):
                tk = mm(pAh[pb][:, :],
                        wqk_b[:, cc * 256: cc * 256 + 128],
                        xqT_v[:, cc, u * 512:(u + 1) * 512],
                        start=(cc == 0), stop=(cc == CC - 1),
                        inc="pe" if cc == CC - 1 else None,
                        waits=([wqk_tok, pA_rd[pb], xqT_tok[4 * u + 3]]
                               if cc == 0 else []))
            q_ev[u] = vec(lambda E, o=Q_sb[:, u * 512:(u + 1) * 512],
                          i=pAh[pb], b=bq: E.tensor_scalar_add(o, i[:, :], b),
                          waits=[tk, t_ba])
            pA_rd[pb] = q_ev[u]
        wqk_mm_done = tk
        prod_last = ("pe", em.count["pe"])
        if h + 1 < NH:
            wqk_tok = dma_w(wqk_b[:, :], d_wqk[h + 1], "dQK",
                            waits=[wqk_mm_done])

        if hr == 0:
            v4_v = V4_sb.ap().rearrange("p (hr t) -> p hr t", hr=4)
            for u in range(4):
                for j in range(4):
                    kt = 4 * u + j
                    pb = nu_p % 3
                    nu_p += 1
                    for cc in range(CC):
                        tk = mm(pAh[pb][:, :],
                                x1T_v[:, cc, kt * 128:(kt + 1) * 128],
                                wv4_b[:, cc * 512:(cc + 1) * 512],
                                start=(cc == 0), stop=(cc == CC - 1),
                                inc="pe" if cc == CC - 1 else None,
                                waits=([wv4_tok, pA_rd[pb], x1T_tok[kt]]
                                       if cc == 0 else []))
                    dv = v4_v[:, :, kt * 128:(kt + 1) * 128]
                    sv = pAh[pb].ap().rearrange("p (c n) -> p c n", c=4)
                    ev = vec(lambda E, o=dv, i=sv: E.tensor_copy(out=o, in_=i),
                             waits=[tk])
                    pA_rd[pb] = ev
            if h + 4 < NH:
                wv4_tok = dma_w(wv4_b[:, :], d_wv4[h // 4 + 1], "dV4",
                                waits=[tk])
            else:
                wp_tok[0] = [dma_w(wv4_b[:, 2048 * j: 2048 * (j + 1)],
                                   d_wp[0][:, 2048 * j: 2048 * (j + 1)],
                                   f"dPf0_{j}", waits=([tk] if j == 0 else []))
                             for j in range(4)]

        if deferred is not None:
            deferred()
            deferred = None

        ldone_q = [None, None]
        for qh in range(2):
            exp_tok = [None] * 16
            kcs = list(range(8 * qh + 8))
            lrow = pS[32 * qh: 32 * qh + 1, :]

            def emit_st(kc):
                pb = (kc + 2) % 3
                voff = max(0, kc - 8 * qh) * 64
                N = 512 - voff
                mm(pAh[pb][:, 0:N],
                   K_sb[:, kc * 128:(kc + 1) * 128],
                   Q_sb[:, qh * 512 + voff: (qh + 1) * 512],
                   start=True, stop=True, inc="pe",
                   waits=[k_ev[kc // 4], q_ev[qh], pA_rd[pb]])
                st_tok[kc] = ("pe", em.count["pe"])

            st_tok = [None] * 16
            LOOK = 3
            for kc in kcs[:LOOK]:
                emit_st(kc)
            for kc in kcs:
                pb = (kc + 2) % 3
                voff = max(0, kc - 8 * qh) * 64
                N = 512 - voff
                diag = kc >= 8 * qh
                msk = None
                if diag:
                    tdg = act(dg_scr[:, :], pAh[pb][:, 0:64], AF.Exp, scale=ISQ,
                              waits=[st_tok[kc], dg_rd])
                    msk = pool(lambda E,
                               o=PT_sb[:, kc * 512: kc * 512 + 64],
                               a=dg_scr, m=dmask:
                               E.tensor_mul(o, a[:, :], m[:, :]),
                               waits=[tdg, t_dm, PT_rd_tok[kc]])
                    dg_rd = msk
                    if N > 64:
                        exp_tok[kc] = act(PT_sb[:, kc * 512 + 64: kc * 512 + N],
                                          pAh[pb][:, 64:N], AF.Exp, scale=ISQ,
                                          waits=[PT_rd_tok[kc]])
                    else:
                        exp_tok[kc] = tdg
                else:
                    exp_tok[kc] = act(PT_sb[:, kc * 512: kc * 512 + N],
                                      pAh[pb][:, 0:N], AF.Exp, scale=ISQ,
                                      waits=[st_tok[kc], PT_rd_tok[kc]])
                pA_rd[pb] = exp_tok[kc]
                wl = [exp_tok[kc], t_ones]
                if msk is not None:
                    wl.append(msk)
                if kc == 0:
                    wl += pS_rd + [pY_rd[qh][h % 2]]
                mm(lrow[:, voff:512], ones_bf[:, :],
                   PT_sb[:, kc * 512: kc * 512 + N],
                   start=(kc == 0), stop=(kc == kcs[-1]), waits=wl)
                tv = mm(pYq[qh][h % 2][:, voff:512],
                        V4_sb[:, hr * TKV + kc * 128: hr * TKV + (kc + 1) * 128],
                        PT_sb[:, kc * 512: kc * 512 + N],
                        start=(kc == 0), stop=(kc == kcs[-1]), inc="pe")
                PT_rd_tok[kc] = tv
                if kc + LOOK in kcs:
                    emit_st(kc + LOOK)
            ldone_q[qh] = ("pe", em.count["pe"])

        def make_deferred(h=h, hr=hr, ldone=tuple(ldone_q)):
            def run():
                nonlocal pS_rd
                toks = []
                for qh in range(2):
                    lrow = pS[32 * qh: 32 * qh + 1, :]
                    t_ln = act(rrow[0:1, 0:512], lrow, AF.Ln,
                               waits=[ldone[qh], rrow_rd[0]])
                    t_li = act(rrow[0:1, 512:1024],
                               rrow[0:1, 0:512], AF.Exp, scale=-1.0,
                               waits=[t_ln])
                    toks.append(t_li)
                    t_bc = mm(pAh[qh][:, :], ones1f[0:1, :],
                              rrow[0:1, 512:1024],
                              start=True, stop=True, inc="pe",
                              waits=[t_li, t_ones1, pA_rd[qh]])
                    rrow_rd[0] = t_bc
                    t_lc = vec(lambda E, o=linv2[qh], i=pAh[qh]:
                               E.tensor_copy(out=o[:, :], in_=i[:, :]),
                               waits=[t_bc, linv_rd[qh]])
                    pA_rd[qh] = t_lc
                    dst = yallT[:, h * TOWN + qh * 512: h * TOWN + (qh + 1) * 512]
                    t_ym = vec(lambda E, o=dst, i=pYq[qh][h % 2],
                               l=linv2[qh]:
                               E.tensor_mul(o, i[:, :], l[:, :]),
                               waits=[t_lc, ldone[qh]])
                    pY_rd[qh][h % 2] = t_ym
                    linv_rd[qh] = t_ym
                pS_rd = toks
            return run

        deferred = make_deferred()

        if h == 13:
            wp1_j2 = dma_w(V4_sb[:, 1024:3072], d_wp[1][:, 4096:6144],
                           "dP1_2", waits=[("pe", em.count["pe"])])
        if h == 14:
            wp1_j3 = dma_w(V4_sb[:, 3072:5120], d_wp[1][:, 6144:8192],
                           "dP1_3", waits=[("pe", em.count["pe"])])

    deferred()
    deferred = None
    prod15 = prod_last

    sA2.close()
    sA.close()
    heads_end = em.snapshot()
    if "yallT" in DEBUG_TAPS:
        d_dbg_ya = nc.dram_tensor("dbg_yallT", [128, NH * TOWN], BF16, kind="ExternalOutput").ap()
        dma_o(d_dbg_ya, yallT[:, :], waits=heads_end)

    acc = S.enter_context(nc.sbuf_tensor("acc", [128, 8 * C], F32))
    x2T = S.enter_context(nc.sbuf_tensor("x2T", [128, CC * TOWN], BF16))
    x2T_v = x2T.ap().rearrange("p (cc t) -> p cc t", cc=CC)
    wg_b = [sF.enter_context(nc.sbuf_tensor("wg0", [128, CC * 128], BF16)),
            None]
    wu_b = [sF.enter_context(nc.sbuf_tensor("wu0", [128, CC * 128], BF16)),
            None]
    wp_b = [sP.enter_context(nc.sbuf_tensor(f"wp{i}", [128, CC * 512], BF16))
            for i in range(2)]
    acc_init = [None] * 8
    for tt in range(8):
        acc_init[tt] = dma_x(acc[:, tt * C:(tt + 1) * C],
                             d_xown[tt * 128:(tt + 1) * 128, :], f"dA{tt}",
                             waits=[prod15] if tt == 0 else [])
    wg_tok, wu_tok = {}, {}
    wg_rd = [None, None]
    wu_rd = [None, None]
    wg_tok[(0, 0)] = dma_w(wg_b[0][:, :], d_wg[0], "dG0", waits=[prod15])
    wu_tok[(0, 0)] = dma_w(wu_b[0][:, :], d_wu[0], "dU0")
    wpbufs = [wv4_b, wp_b[0], wp_b[1], wv4_b]
    wp_tok[1] = [dma_w(wp_b[0][:, 2048 * j: 2048 * (j + 1)],
                       d_wp[1][:, 2048 * j: 2048 * (j + 1)], f"dP1_{j}",
                       waits=(heads_end if j == 0 else []))
                 for j in range(2)] + [wp1_j2, wp1_j3]
    wp_tok[2] = [dma_w(wp_b[1][:, 2048 * j: 2048 * (j + 1)],
                       d_wp[2][:, 2048 * j: 2048 * (j + 1)], f"dP2_{j}",
                       waits=(heads_end if j == 0 else []))
                 for j in range(4)]
    bp_row = sP.enter_context(nc.sbuf_tensor("bp_sb", [1, C], BF16))
    bvp_sb = sP.enter_context(nc.sbuf_tensor("bvp_sb", [1, 2048], BF16))
    x2tm2 = [sP.enter_context(nc.sbuf_tensor(f"x2tm{i}", [128, C], BF16))
             for i in range(2)]
    lg_sb = sP.enter_context(nc.sbuf_tensor("lg_sb", [128, 8], F32))
    ex_sb = sP.enter_context(nc.sbuf_tensor("ex_sb", [128, 8], F32))
    ge_sb = sP.enter_context(nc.sbuf_tensor("ge_sb", [128, 8], F32))
    pT0b = sPp.enter_context(nc.psum_tensor("pT0b", [128, 1024], BF16))
    pT1b = sPp.enter_context(nc.psum_tensor("pT1b", [128, 1024], BF16))
    pTb = (pT0b, pT1b)

    t_bp = dma_w(bp_row[0:1, :], d_bp[0:1, :], "dBp", waits=heads_end)

    pT_rd2 = [None, None]
    x2_done = [None] * 8
    C_done = [None] * 8
    S_done = [None] * 8
    x2tm_rd = [None, None]
    pS_rd2 = [t for t in pS_rd]
    lg_rd = [None]
    ex_rd = [None]
    first_ln2 = [True]

    ln2_bf = {}

    def ln2_part1(tt, ready_tok):
        b = tt % 2
        src = acc[:, tt * C:(tt + 1) * C]
        if first_ln2[0]:
            em.wait("vector", heads_end)
            first_ln2[0] = False
        em.wait("vector", ready_tok)
        x2tm = x2tm2[tt % 2]
        tok_bf, _ = ln_stats_norm(b, src, x2tm[:, :],
                                  extra_norm_waits=[x2tm_rd[b]])
        ln2_bf[tt] = tok_bf

    def ln2_router(tt):
        b = tt % 2
        st = stats.ap()
        col = lambda k: st[:, b * 16 + k: b * 16 + k + 1]
        mx1, nmx, den, mx2, rde, s1, s2 = [col(k) for k in range(8, 15)]
        x2tm = x2tm2[tt % 2]
        tok_bf = ln2_bf[tt]
        last_ev = None
        for g in range(4):
            pb = (tt * 4 + g) % 2
            for j in range(4):
                cc = 4 * g + j
                tk = pe_tr(pTb[pb][:, j * 128:(j + 1) * 128],
                           x2tm[:, cc * 128:(cc + 1) * 128], id_bf[:, :],
                           inc="pe" if j == 3 else None,
                           waits=([tok_bf, pT_rd2[pb]] if j == 0 else []))
                _ = 0
            dv = x2T_v[:, 4 * g:4 * g + 4, tt * 128:(tt + 1) * 128]
            sv = pTb[pb].ap()[:, 0:512].rearrange("p (c n) -> p c n", c=4)
            last_ev = vec(lambda E, o=dv, i=sv: E.tensor_copy(out=o, in_=i),
                          waits=[tk])
            pT_rd2[pb] = last_ev
        x2tm_rd[b] = last_ev
        x2_done[tt] = last_ev
        for cc in range(CC):
            mm(pS[:, 0:8], x2T_v[:, cc, tt * 128:(tt + 1) * 128],
               wr_bf[:, cc * 8:(cc + 1) * 8],
               start=(cc == 0), stop=(cc == CC - 1),
               waits=([last_ev, t_wrbf] + pS_rd2) if cc == 0 else [])
        for cc in range(CC):
            tk = mm(pS[:, 8:9], x2T_v[:, cc, tt * 128:(tt + 1) * 128],
                    wsg_bf[:, cc:cc + 1],
                    start=(cc == 0), stop=(cc == CC - 1),
                    inc="pe" if cc == CC - 1 else None,
                    waits=[t_wsg] if cc == 0 else [])
        t_s1 = act(s1, pS[:, 8:9], AF.Exp, scale=-1.0, waits=[tk])
        t_s1b = vec(lambda E, o=s2, i=s1: E.tensor_scalar_add(o, i, 1.0),
                    waits=[t_s1])
        t_sln = act(s2, s2, AF.Ln, waits=[t_s1b])
        t_sg = act(s_tm.ap()[:, tt:tt + 1], s2, AF.Exp, scale=-1.0, waits=[t_sln])
        t_lg = vec(lambda E, o=lg_sb, i=pS: E.tensor_copy(out=o[:, :], in_=i[:, 0:8]),
                   waits=[tk, lg_rd[0]])
        pS_rd2[:] = [t_lg, t_s1]
        t_mx1 = vec(lambda E, o=mx1, i=lg_sb: E.reduce_max(o, i[:, :], axis=AX.X),
                    waits=[t_lg])
        t_nmx = vec(lambda E, o=nmx, i=mx1: E.tensor_scalar_mul(o, i, -1.0),
                    waits=[t_mx1])
        t_ex = act(ex_sb[:, :], lg_sb[:, :], AF.Exp, bias=nmx, accum_out=den,
                   waits=[t_nmx, ex_rd[0]])
        t_ge = vec(lambda E, o=ge_sb, i=lg_sb, m=mx1: E.tensor_scalar(
            out=o[:, :], in0=i[:, :], scalar1=m, scalar2=None, op0=ALU.is_ge),
            waits=[t_ex, t_mx1])
        t_lg2 = vec(lambda E, o=ge_sb, i=ge_sb, l=lg_sb: E.scalar_tensor_tensor(
            out=o[:, :], in0=i[:, :], scalar=-1e30, in1=l[:, :],
            op0=ALU.mult, op1=ALU.add), waits=[t_ge])
        t_mx2 = vec(lambda E, o=mx2, i=ge_sb: E.reduce_max(o, i[:, :], axis=AX.X),
                    waits=[t_lg2])
        t_sel = vec(lambda E, o=ge_sb, i=lg_sb, m=mx2: E.tensor_scalar(
            out=o[:, :], in0=i[:, :], scalar1=m, scalar2=None, op0=ALU.is_ge),
            waits=[t_mx2])
        lg_rd[0] = t_ex
        t_num = vec(lambda E, o=ex_sb, a=ex_sb, s=ge_sb:
                    E.tensor_mul(o[:, :], a[:, :], s[:, :]), waits=[t_sel, t_ex])
        t_rd = vec(lambda E, o=rde, i=den: E.reciprocal(o, i), waits=[t_ex])
        t_C = vec(lambda E, o=C_tm.ap()[:, tt * 8:(tt + 1) * 8], i=ex_sb, r=rde:
                  E.tensor_scalar_mul(o, i[:, :], r), waits=[t_num, t_rd])
        ex_rd[0] = t_C
        return t_C, t_sg


    stt_done = [[None] * 4 for _ in range(8)]
    first_proj = [True]
    pA_rd = pA_rd[:2]
    pS_bvp = [list(pS_rd)]
    t_bvp = {}
    wp_last = [None] * 4

    def emit_bvp(fg):
        wb = wpbufs[fg]
        for cc in range(CC):
            w0 = [wp_tok[fg][cc // 4]]
            if cc == 0:
                w0 += [t_bvbf] + pS_bvp[0]
            if first_proj[0]:
                w0 += heads_end
                first_proj[0] = False
            tkr = mm(pS[0:1, 0:512], bv_bf[:, cc:cc + 1],
                     wb[:, cc * 512:(cc + 1) * 512],
                     start=(cc == 0), stop=(cc == CC - 1),
                     inc="pe" if cc == CC - 1 else None,
                     waits=w0)
        t_bvp[fg] = vec(lambda E, o=bvp_sb[0:1, fg * 512:(fg + 1) * 512],
                        i=pS[0:1, 0:512],
                        b=bp_row[0:1, fg * 512:(fg + 1) * 512]:
                        E.tensor_add(o, i, b),
                        waits=[tkr, t_bp])
        pS_bvp[0] = [t_bvp[fg]]

    def emit_proj_tile(fg, tt, pb):
        wb = wpbufs[fg]
        for cc in range(CC):
            w0 = [wp_tok[fg][cc // 4]]
            if cc == 0:
                w0.append(pA_rd[pb])
            mm(pA[pb][:, :],
               yallT[:, cc * TOWN + tt * 128:
                     cc * TOWN + (tt + 1) * 128],
               wb[:, cc * 512:(cc + 1) * 512],
               start=(cc == 0), stop=False, waits=w0)
        tk = mm(pA[pb][:, :], ones1b[0:1, :],
                bvp_sb[0:1, fg * 512:(fg + 1) * 512],
                start=False, stop=True, inc="pe",
                waits=[t_bvp[fg], t_ones1b])
        wp_last[fg] = tk
        ev = vec(lambda E,
                 o=acc[:, tt * C + fg * 512: tt * C + (fg + 1) * 512],
                 i=pA[pb]: E.scalar_tensor_tensor(
                     out=o, in0=i[:, :], scalar=1.0, in1=o,
                     op0=ALU.mult, op1=ALU.add),
                 waits=[tk, acc_init[tt]])
        pA_rd[pb] = ev
        stt_done[tt][fg] = ev

    for fg in (0, 1):
        emit_bvp(fg)
        for tt in range(8):
            emit_proj_tile(fg, tt, tt % 2)
        if fg == 0:
            wp_tok[3] = [
                dma_w(wv4_b[:, 2048 * j: 2048 * (j + 1)],
                      d_wp[3][:, 2048 * j: 2048 * (j + 1)], f"dP3_{j}",
                      waits=([wp_last[0]] if j == 0 else []))
                for j in range(4)]
    emit_bvp(2)
    emit_proj_tile(2, 0, 0)
    emit_bvp(3)
    emit_proj_tile(3, 0, 1)
    ln2_part1(0, stt_done[0][3])
    for tt in range(1, 8):
        emit_proj_tile(2, tt, 0)
        emit_proj_tile(3, tt, 1)
        if tt > 1:
            C_done[tt - 2], S_done[tt - 2] = ln2_router(tt - 2)
        ln2_part1(tt, stt_done[tt][3])
    for _t in (6, 7):
        C_done[_t], S_done[_t] = ln2_router(_t)

    acc1_tok = [None] * 8
    if "acc1" in DEBUG_TAPS:
        d_dbg_a1 = nc.dram_tensor("dbg_acc1", [TOWN, C], F32, kind="ExternalOutput").ap()
        for tt in range(8):
            acc1_tok[tt] = dma_o(d_dbg_a1[tt * 128:(tt + 1) * 128, :],
                                 acc[:, tt * C:(tt + 1) * C],
                                 waits=[stt_done[tt][3]])
    if "x2T" in DEBUG_TAPS:
        d_dbg_x2 = nc.dram_tensor("dbg_x2T", [128, CC * TOWN], BF16, kind="ExternalOutput").ap()
        dma_o(d_dbg_x2, x2T[:, :], waits=[x2_done[tt] for tt in range(8)])
    if "Ctm" in DEBUG_TAPS:
        d_dbg_ct = nc.dram_tensor("dbg_Ctm", [128, 64], F32, kind="ExternalOutput").ap()
        d_dbg_st = nc.dram_tensor("dbg_stm", [128, 8], F32, kind="ExternalOutput").ap()
        dma_o(d_dbg_ct, C_tm[:, :], waits=[C_done[tt] for tt in range(8)])
        dma_o(d_dbg_st, s_tm[:, :], waits=[S_done[tt] for tt in range(8)])

    ffn_barrier = em.snapshot()

    sPp.close()
    sP.close()
    wg_b[1] = sF.enter_context(nc.sbuf_tensor("wg1", [128, CC * 128], BF16))
    wu_b[1] = sF.enter_context(nc.sbuf_tensor("wu1", [128, CC * 128], BF16))
    wg_tok[(0, 1)] = dma_w(wg_b[1][:, :], d_wg[1], "dG1", waits=ffn_barrier)
    wu_tok[(0, 1)] = dma_w(wu_b[1][:, :], d_wu[1], "dU1")
    wd_b = [wv4_b,
            sF.enter_context(nc.sbuf_tensor("wd1", [128, FC * 512], BF16))]
    H_sb = sF.enter_context(nc.sbuf_tensor("H_sb", [128, FC * TOWN], BF16))
    sgl = sF.enter_context(nc.sbuf_tensor("sgl", [128, TOWN], F32))
    pG = sF.enter_context(nc.psum_tensor("pG", [128, 1024], F32))
    pU = sF.enter_context(nc.psum_tensor("pU", [128, 1024], F32))

    wd_tok = {}
    wd_rd = [None, None]
    x2_all = list(x2_done)
    pG_rd = None
    pU_rd = None
    sgl_rd = None
    H_wr = [None] * FC
    acc_fin = [None] * 8
    first_ffn = [True]

    nu = 0
    for e in range(NE):
        for fg in (0, 1):
            j = (e * 4 + fg) % 2
            wd_tok[(e, fg)] = dma_w(
                wd_b[j][:, 0:FC * 512], d_wd[e * 4 + fg], f"dD{j}",
                waits=([wd_rd[j]] + (ffn_barrier if (e, fg) == (0, 0) else [])))
        for fc in range(FC):
            j = nu % 2
            w0 = [wg_tok[(e, fc)], pG_rd, x2_done[3]]
            if first_ffn[0]:
                w0 += ffn_barrier
                first_ffn[0] = False
            for half in range(2):
                for cc in range(CC):
                    tkG = mm(pG[:, half * 512:(half + 1) * 512],
                             wg_b[j][:, cc * 128:(cc + 1) * 128],
                             x2T_v[:, cc, half * 512:(half + 1) * 512],
                             start=(cc == 0), stop=(cc == CC - 1),
                             inc="pe" if (half == 1 and cc == CC - 1) else None,
                             waits=(w0 if half == 0 else [x2_done[7]])
                             if cc == 0 else [])
            for half in range(2):
                for cc in range(CC):
                    tkU = mm(pU[:, half * 512:(half + 1) * 512],
                             wu_b[j][:, cc * 128:(cc + 1) * 128],
                             x2T_v[:, cc, half * 512:(half + 1) * 512],
                             start=(cc == 0), stop=(cc == CC - 1),
                             inc="pe" if (half == 1 and cc == CC - 1) else None,
                             waits=[wu_tok[(e, fc)], pU_rd]
                             if (half == 0 and cc == 0) else [])
            ne_, nf_ = (e, fc + 1) if fc + 1 < FC else (e + 1, 0)
            if ne_ < NE and (ne_, nf_) not in wg_tok:
                wg_tok[(ne_, nf_)] = dma_w(wg_b[1 - j][:, :],
                                           d_wg[ne_ * FC + nf_], f"dG{1 - j}",
                                           waits=[wg_rd[1 - j]])
                wu_tok[(ne_, nf_)] = dma_w(wu_b[1 - j][:, :],
                                           d_wu[ne_ * FC + nf_], f"dU{1 - j}",
                                           waits=[wu_rd[1 - j]])
            wg_rd[j] = tkG
            wu_rd[j] = tkU
            t_si = act(sgl[:, :], pG[:, :], AF.Silu, waits=[tkG, sgl_rd])
            pG_rd = t_si
            t_H = vec(lambda E, o=H_sb[:, fc * TOWN:(fc + 1) * TOWN], s=sgl, u=pU:
                      E.tensor_mul(o, s[:, :], u[:, :]),
                      waits=[t_si, tkU])
            pU_rd = t_H
            sgl_rd = t_H
            H_wr[fc] = t_H
            nu += 1
        h_all = list(H_wr)
        for fg in range(4):
            j = (e * 4 + fg) % 2
            for tt in range(8):
                pb = tt % 2
                for kc in range(FC):
                    tk = mm(pA[pb][:, :],
                            H_sb[:, kc * TOWN + tt * 128:
                                 kc * TOWN + (tt + 1) * 128],
                            wd_b[j][:, kc * 512:(kc + 1) * 512],
                            start=(kc == 0), stop=(kc == FC - 1),
                            inc="pe" if kc == FC - 1 else None,
                            waits=([wd_tok[(e, fg)], pA_rd[pb]] + h_all)
                            if kc == 0 else [])
                gate = (C_tm.ap()[:, tt * 8 + e: tt * 8 + e + 1] if e < 8
                        else s_tm.ap()[:, tt:tt + 1])
                gw = [tk, C_done[tt] if e < 8 else S_done[tt]]
                if e == 0:
                    gw.append(stt_done[tt][fg])
                    gw.append(acc1_tok[tt])
                ev = vec(lambda E,
                         o=acc[:, tt * C + fg * 512: tt * C + (fg + 1) * 512],
                         i=pA[pb], g=gate: E.scalar_tensor_tensor(
                             out=o, in0=i[:, :], scalar=g, in1=o,
                             op0=ALU.mult, op1=ALU.add),
                         waits=gw)
                pA_rd[pb] = ev
                if e == NE - 1:
                    acc_fin[tt] = ev
            wd_rd[j] = tk
            if fg < 2:
                j2 = (e * 4 + fg + 2) % 2
                wd_tok[(e, fg + 2)] = dma_w(wd_b[j2][:, 0:FC * 512],
                                            d_wd[e * 4 + fg + 2],
                                            f"dD{j2}", waits=[wd_rd[j2] or tk])

    for tt in range(8):
        em.getsem(f"dO{tt}")
        em.op("sync",
              lambda E, o=d_y[tt * 128:(tt + 1) * 128, :],
              i=acc[:, tt * C:(tt + 1) * C]: E.dma_start(out=o, in_=i),
              inc=f"dO{tt}", inc_n=16, waits=[acc_fin[tt]])
    for tt in range(8):
        em.prog["sync"].append(("w", f"dO{tt}", 16))

    with ExitStack() as semstack:
        semh = {}
        for s in em.count:
            semh[s] = semstack.enter_context(nc.semaphore(f"{s}_s"))
        with nc.Block() as block:

            def replay(E, name):
                for it in em.prog[name]:
                    if it[0] == "w":
                        E.wait_ge(semh[it[1]], it[2])
                    else:
                        _, fn, inc, inc_n = it
                        ins = fn(E)
                        if inc is not None:
                            ins.then_inc(semh[inc], inc_n)

            @block.tensor
            def _(t):
                replay(t, "tensor")

            @block.vector
            def _(v):
                replay(v, "vector")

            @block.scalar
            def _(sc):
                replay(sc, "scalar")

            @block.gpsimd
            def _(g):
                replay(g, "gpsimd")

            @block.sync
            def _(s):
                replay(s, "sync")

    sF.close()
    S.close()
    return nc


def prep_inputs(inputs):
    bf = ml_dtypes.bfloat16
    Wa = np.asarray(inputs["L0_attn_W_attn"], dtype=np.float32)
    ba = np.asarray(inputs["L0_attn_b_attn"], dtype=np.float32)
    Wp = np.asarray(inputs["L0_attn_W_proj"], dtype=np.float32)
    bp = np.asarray(inputs["L0_attn_b_proj"], dtype=np.float32)
    Wr = np.asarray(inputs["L0_ffn_Wr"], dtype=np.float32)
    Wg = np.asarray(inputs["L0_ffn_Wg"], dtype=np.float32)
    Wu = np.asarray(inputs["L0_ffn_Wu"], dtype=np.float32)
    Wd = np.asarray(inputs["L0_ffn_Wd"], dtype=np.float32)
    Sg = np.asarray(inputs["L0_ffn_Sg"], dtype=np.float32)
    Su = np.asarray(inputs["L0_ffn_Su"], dtype=np.float32)
    Sd = np.asarray(inputs["L0_ffn_Sd"], dtype=np.float32)
    Wsg = np.asarray(inputs["L0_ffn_Wsg"], dtype=np.float32)

    w5 = Wa.reshape(CC, 128, 3, NH, 128)
    wqk = np.ascontiguousarray(
        w5[:, :, 0:2].transpose(3, 1, 0, 2, 4).reshape(NH, 128, CC * 256))
    wv4 = np.ascontiguousarray(
        w5[:, :, 2].reshape(CC, 128, 4, 4, 128).transpose(2, 1, 0, 3, 4)
        .reshape(4, 128, CC * 512))
    battn = np.ascontiguousarray(ba.reshape(48, 128).T)
    wp = np.ascontiguousarray(
        Wp.reshape(CC, 128, 4, 512).transpose(2, 1, 0, 3).reshape(4, 128, CC * 512))
    bp_row = np.ascontiguousarray(bp.reshape(1, C))
    wr = np.ascontiguousarray(
        Wr.reshape(CC, 128, 8).transpose(1, 0, 2).reshape(128, 128))
    wsg = np.ascontiguousarray(Wsg.reshape(CC, 128).T)
    wg_r = Wg.reshape(8, CC, 128, FC, 128).transpose(0, 3, 2, 1, 4)
    sg_r = Sg.reshape(CC, 128, 4, FC, 128).transpose(2, 3, 1, 0, 4)
    wg_all = np.ascontiguousarray(
        np.concatenate([wg_r, sg_r], axis=0).reshape(NE * FC, 128, CC * 128))
    wu_r = Wu.reshape(8, CC, 128, FC, 128).transpose(0, 3, 2, 1, 4)
    su_r = Su.reshape(CC, 128, 4, FC, 128).transpose(2, 3, 1, 0, 4)
    wu_all = np.ascontiguousarray(
        np.concatenate([wu_r, su_r], axis=0).reshape(NE * FC, 128, CC * 128))
    wd_r = Wd.reshape(8, FC, 128, 4, 512).transpose(0, 3, 2, 1, 4)
    sd_r = Sd.reshape(4, FC, 128, 4, 512).transpose(0, 3, 2, 1, 4)
    wd_all = np.ascontiguousarray(
        np.concatenate([wd_r, sd_r], axis=0).reshape(NE * 4, 128, FC * 512))
    ident = np.eye(128, dtype=bf)

    x = np.asarray(inputs["L0_attn_x"], dtype=np.float32)
    in_maps = []
    for core in range(8):
        b, h = core // 2, core % 2
        xb = x[b]
        own = np.ascontiguousarray(
            xb.reshape(CC, 128, C)[:, 64 * h:64 * h + 64, :].reshape(TOWN, C))
        dmask_np = ((np.arange(64)[None, :] + 64 * h)
                    >= np.arange(128)[:, None]).astype(bf)
        in_maps.append(dict(
            x_kv=np.ascontiguousarray(xb),
            x_own=own,
            dmask=np.ascontiguousarray(dmask_np),
            ident=ident,
            wqk=wqk, wv4=wv4, battn=battn, wp=wp, bp_row=bp_row,
            wr=wr, wsg=wsg, wg=wg_all, wu=wu_all, wd=wd_all,
        ))
    return in_maps


def kernel(**inputs):
    in_maps = prep_inputs(inputs)
    nc = build_nc()
    res = run_bass_kernel_spmd(nc, in_maps, core_ids=list(range(8)))
    out = np.empty((4, 2048, C), dtype=np.float32)
    for core in range(8):
        b, h = core // 2, core % 2
        yc = np.asarray(res.results[core]["y"]).reshape(CC, 64, C)
        out[b].reshape(CC, 128, C)[:, 64 * h:64 * h + 64, :] = yc
    return out


# version 17
# speedup vs baseline: 1.1956x; 1.0380x over previous
import numpy as np
import ml_dtypes
from contextlib import ExitStack

import concourse.bass as bass
import concourse.mybir as mybir
from concourse.bass_utils import run_bass_kernel_spmd

F32 = mybir.dt.float32
BF16 = mybir.dt.bfloat16
AF = mybir.ActivationFunctionType
ALU = mybir.AluOpType
AX = mybir.AxisListType

C = 2048
CC = 16
NH = 16
TKV = 2048
TOWN = 1024
NE = 12
FC = 11
ISQ = float(1.0 / np.sqrt(128.0))
LN_EPS = 1e-5

DEBUG_TAPS = ()

ENGN = ("tensor", "vector", "scalar", "gpsimd", "sync")
SEMS = ("pe", "ve", "se", "po")


class Emit:

    def __init__(self):
        self.prog = {e: [] for e in ENGN}
        self.count = {s: 0 for s in SEMS}
        self.lastw = {e: {} for e in ENGN}

    def getsem(self, s):
        if s not in self.count:
            self.count[s] = 0

    def wait(self, eng, tok):
        if tok is None:
            return
        if isinstance(tok, (list, tuple)) and tok and isinstance(tok[0], tuple):
            for t in tok:
                self.wait(eng, t)
            return
        sem, v = tok
        assert v <= self.count[sem], f"wait on future token {sem}:{v}"
        if self.lastw[eng].get(sem, 0) >= v:
            return
        self.lastw[eng][sem] = v
        self.prog[eng].append(("w", sem, v))

    _CHAIN = {"vector": "ve", "scalar": "se", "gpsimd": "po"}

    def op(self, eng, fn, inc=None, inc_n=1, waits=(), chain=True):
        ch = self._CHAIN.get(eng)
        if chain and ch is not None and inc == ch and self.count[ch] > 0:
            self.wait(eng, (ch, self.count[ch]))
        for t in waits:
            self.wait(eng, t)
        tok = None
        if inc is not None:
            self.count[inc] += inc_n
            tok = (inc, self.count[inc])
        self.prog[eng].append(("i", fn, inc, inc_n))
        return tok

    def snapshot(self):
        return [(s, self.count[s]) for s in ("pe", "ve", "se", "po")
                if self.count[s] > 0]


def build_nc():
    nc = bass.Bass("TRN2", target_bir_lowering=False)

    d_xkv = nc.dram_tensor("x_kv", [TKV, C], F32, kind="ExternalInput").ap()
    d_xown = nc.dram_tensor("x_own", [TOWN, C], F32, kind="ExternalInput").ap()
    d_dmask = nc.dram_tensor("dmask", [128, 64], BF16, kind="ExternalInput").ap()
    d_ident = nc.dram_tensor("ident", [128, 128], BF16, kind="ExternalInput").ap()
    d_wqk = nc.dram_tensor("wqk", [NH, 128, CC * 256], F32, kind="ExternalInput").ap()
    d_wv4 = nc.dram_tensor("wv4", [4, 128, CC * 512], F32, kind="ExternalInput").ap()
    d_battn = nc.dram_tensor("battn", [128, 48], F32, kind="ExternalInput").ap()
    d_wp = nc.dram_tensor("wp", [4, 128, CC * 512], F32, kind="ExternalInput").ap()
    d_bp = nc.dram_tensor("bp_row", [1, C], F32, kind="ExternalInput").ap()
    d_wr = nc.dram_tensor("wr", [128, 128], F32, kind="ExternalInput").ap()
    d_wsg = nc.dram_tensor("wsg", [128, 16], F32, kind="ExternalInput").ap()
    d_wg = nc.dram_tensor("wg", [NE * FC, 128, CC * 128], F32, kind="ExternalInput").ap()
    d_wu = nc.dram_tensor("wu", [NE * FC, 128, CC * 128], F32, kind="ExternalInput").ap()
    d_wd = nc.dram_tensor("wd", [NE * 4, 128, FC * 512], F32, kind="ExternalInput").ap()
    d_y = nc.dram_tensor("y", [TOWN, C], F32, kind="ExternalOutput").ap()

    def row3(ap3, i):
        return ap3[i:i + 1, :, :].rearrange("o p f -> p (o f)")

    em = Emit()

    def mm(out, lhsT, rhs, start, stop, inc=None, waits=()):
        return em.op(
            "tensor",
            lambda E, o=out, l=lhsT, r=rhs, st=start, sp=stop: E.matmul(
                o, l, r, start=st, stop=sp),
            inc=inc, waits=waits)

    def pe_tr(out, in_, ident, inc=None, waits=()):
        return em.op(
            "tensor",
            lambda E, o=out, i=in_, d=ident: E.transpose(o, i, d),
            inc=inc, waits=waits)

    def act(out, in_, func, bias=0.0, scale=1.0, accum_out=None, waits=(),
            chain=True):
        return em.op(
            "scalar",
            lambda E, o=out, i=in_, f=func, b=bias, s=scale, a=accum_out:
            E.activation(o, i, f, bias=b, scale=s, accum_out=a),
            inc="se", waits=waits, chain=chain)

    def vec(fn, waits=(), chain=True):
        return em.op("vector", fn, inc="ve", waits=waits, chain=chain)

    def pool(fn, waits=()):
        return em.op("gpsimd", fn, inc="po", waits=waits)

    def _dma(eng, sem, out, in_, waits):
        em.getsem(sem)
        pre = em.count[sem]
        if pre > 0:
            waits = list(waits) + [(sem, pre)]
        return em.op(eng,
                     lambda E, o=out, i=in_: E.dma_start(out=o, in_=i),
                     inc=sem, inc_n=16, waits=waits)

    def dma_w(out, in_, sem, waits=()):
        return _dma("gpsimd", sem, out, in_, waits)

    def dma_x(out, in_, sem, waits=()):
        return _dma("sync", sem, out, in_, waits)

    def dma_o(out, in_, waits=()):
        return _dma("sync", "dO", out, in_, waits)

    S = ExitStack()
    sA = ExitStack()
    sL = ExitStack()
    sA2 = ExitStack()
    sP = ExitStack()
    sPp = ExitStack()
    sF = ExitStack()

    id_bf = S.enter_context(nc.sbuf_tensor("id_bf", [128, 128], BF16))
    dmask = S.enter_context(nc.sbuf_tensor("dmask_sb", [128, 64], BF16))
    ones_bf = S.enter_context(nc.sbuf_tensor("ones_bf", [128, 1], BF16))
    ones1f = S.enter_context(nc.sbuf_tensor("ones1f", [1, 128], F32))
    ones1b = S.enter_context(nc.sbuf_tensor("ones1b", [1, 128], BF16))
    bv_bf = S.enter_context(nc.sbuf_tensor("bv_bf", [128, 16], BF16))
    battn = S.enter_context(nc.sbuf_tensor("battn_sb", [128, 48], F32))
    wr_f = S.enter_context(nc.sbuf_tensor("wr_f", [128, 128], F32))
    wr_bf = S.enter_context(nc.sbuf_tensor("wr_bf", [128, 128], BF16))
    wsg_bf = S.enter_context(nc.sbuf_tensor("wsg_bf", [128, 16], BF16))
    stats = S.enter_context(nc.sbuf_tensor("stats", [128, 32], F32))
    bnst = S.enter_context(nc.sbuf_tensor("bnst", [128, 64], F32))
    C_tm = S.enter_context(nc.sbuf_tensor("C_tm", [128, 64], F32))
    s_tm = S.enter_context(nc.sbuf_tensor("s_tm", [128, 8], F32))
    yallT = S.enter_context(nc.sbuf_tensor("yallT", [128, NH * TOWN], BF16))
    wv4_b = S.enter_context(nc.sbuf_tensor("wv4_b", [128, CC * 512], BF16))

    pA0 = S.enter_context(nc.psum_tensor("pA0", [128, 512], F32))
    pA1 = S.enter_context(nc.psum_tensor("pA1", [128, 512], F32))
    pS = S.enter_context(nc.psum_tensor("pS", [128, 512], F32))
    pA = (pA0, pA1)

    x1T = sA.enter_context(nc.sbuf_tensor("x1T", [128, CC * TKV], BF16))
    xqT = sA.enter_context(nc.sbuf_tensor("xqT", [128, CC * TOWN], BF16))
    wqk_b = sA.enter_context(nc.sbuf_tensor("wqk_b", [128, CC * 256], BF16))
    K_sb = sA.enter_context(nc.sbuf_tensor("K_sb", [128, TKV], BF16))
    Q_sb = sA.enter_context(nc.sbuf_tensor("Q_sb", [128, TOWN], BF16))
    V4_sb = sA.enter_context(nc.sbuf_tensor("V4_sb", [128, 4 * TKV], BF16))
    pA2 = sA.enter_context(nc.psum_tensor("pA2", [128, 512], F32))
    pYa = [sA.enter_context(nc.psum_tensor(f"pY{i}", [128, 512], F32))
           for i in range(2)]


    pT0 = sL.enter_context(nc.psum_tensor("pT0", [128, 1024], BF16))
    pT1 = sL.enter_context(nc.psum_tensor("pT1", [128, 1024], BF16))
    pT = (pT0, pT1)
    lnx = [sL.enter_context(nc.sbuf_tensor(f"lnx{i}", [128, C], F32)) for i in range(2)]
    lntm = [sL.enter_context(nc.sbuf_tensor(f"lntm{i}", [128, C], BF16)) for i in range(2)]

    x1T_v = x1T.ap().rearrange("p (cc t) -> p cc t", cc=CC)
    xqT_v = xqT.ap().rearrange("p (cc t) -> p cc t", cc=CC)

    dma_x(id_bf[:, :], d_ident[:, :], "dC")
    dma_x(dmask[:, :], d_dmask[:, :], "dC")
    dma_x(battn[:, :], d_battn[:, :], "dC")
    dma_x(wr_f[:, :], d_wr[:, :], "dC")
    t_id = t_dm = t_ba = t_wr = ("dC", 64)
    t_ones = vec(lambda E: E.memset(ones_bf[:, :], 1.0))
    t_ones1 = vec(lambda E: E.memset(ones1f[:, :], 1.0))
    t_ones1b = vec(lambda E: E.memset(ones1b[:, :], 1.0))
    t_wrbf = vec(lambda E: E.tensor_copy(out=wr_bf[:, :], in_=wr_f[:, :]),
                 waits=[t_wr])
    t_bvbf = vec(lambda E: E.tensor_copy(out=bv_bf[:, :], in_=battn[:, 32:48]),
                 waits=[t_ba])
    t_wsg = dma_w(wsg_bf[:, :], d_wsg[:, :], "dC2")
    t_wqk = dma_w(wqk_b[:, :], d_wqk[0], "dQK")
    t_wv4 = dma_w(wv4_b[:, :], d_wv4[0], "dV4")

    ln_cons = [[], []]
    tm_cons = [None, None]
    pT_rd = [None, None]
    x1T_tok = [None] * CC
    xqT_tok = [None] * 8

    def ln_stats_norm(b, src_f32_ap, dst_bf_ap, extra_norm_waits=()):
        st = stats.ap()
        col = lambda k: st[:, b * 16 + k: b * 16 + k + 1]
        var_c, t1_c, r_c, nmr_c = [col(k) for k in range(4, 8)]
        nm_c = col(3)
        bn = bnst.ap()[:, b * 32: b * 32 + 24].rearrange("p (c k) -> p c k", c=4)
        mv = bnst.ap()[:, b * 32 + 24: b * 32 + 26]
        for ch in range(4):
            t_bn = vec(lambda E, o=bn[:, ch, :],
                       i=src_f32_ap.rearrange("p (c n) -> p c n", c=4)[:, ch, :]:
                       E.bn_stats(o, i), chain=(ch == 0))
        t_ag = vec(lambda E, o=mv, i=bnst.ap()[:, b * 32: b * 32 + 24]:
                   E.bn_aggr(o, i), waits=[t_bn])
        vec(lambda E, o=var_c, i=mv: E.tensor_scalar_add(o, i[:, 1:2], LN_EPS),
            waits=[t_ag])
        tok_nm_ = vec(lambda E, o=nm_c, i=mv: E.tensor_scalar_mul(o, i[:, 0:1],
                                                                  -1.0))
        t_l1 = act(t1_c, var_c, AF.Ln, waits=[tok_nm_])
        act(r_c, t1_c, AF.Exp, scale=-0.5, waits=[t_l1])
        em.op("scalar", lambda E, o=nmr_c, i=nm_c, r=r_c: E.mul(o, i, r),
              inc="se")
        toks = []
        for ch in range(4):
            toks.append(act(dst_bf_ap[:, ch * 512:(ch + 1) * 512],
                            src_f32_ap[:, ch * 512:(ch + 1) * 512],
                            AF.Identity, bias=nmr_c, scale=r_c,
                            waits=list(extra_norm_waits), chain=(ch == 0)))
        return toks, None

    pend = [None]

    def ln_transpose_flush(idx):
        b, tok_nm, destT_v, dest_tok_list, dest_i = pend[0]
        last_ev = None
        for g in range(4):
            pb = (idx * 4 + g) % 2
            for j in range(4):
                cc = 4 * g + j
                tk = pe_tr(pT[pb][:, j * 128:(j + 1) * 128],
                           lntm[b][:, cc * 128:(cc + 1) * 128], id_bf[:, :],
                           inc="pe" if j == 3 else None,
                           waits=([tok_nm[g], t_id, pT_rd[pb]]
                                  if j == 0 else []))
            dv = destT_v[:, 4 * g:4 * g + 4, dest_i * 128:(dest_i + 1) * 128]
            sv = pT[pb].ap()[:, 0:512].rearrange("p (c n) -> p c n", c=4)
            last_ev = vec(lambda E, o=dv, i=sv: E.tensor_copy(out=o, in_=i),
                          waits=[tk], chain=(g == 0))
            pT_rd[pb] = last_ev
        tm_cons[b] = last_ev
        dest_tok_list[dest_i] = last_ev
        pend[0] = None

    def ln_tile(idx, src_ap, destT_v, dest_tok_list, dest_i):
        b = idx % 2
        tok_d = dma_x(lnx[b][:, :], src_ap, f"dL{b}", waits=ln_cons[b])
        em.wait("vector", tok_d)
        tok_nm, _ = ln_stats_norm(b, lnx[b][:, :], lntm[b][:, :],
                                  extra_norm_waits=[tm_cons[b]])
        ln_cons[b] = list(tok_nm)
        if pend[0] is not None:
            ln_transpose_flush(idx)
        pend[0] = (b, tok_nm, destT_v, dest_tok_list, dest_i)

    pAh = (pA0, pA1, pA2)
    pA_rd = [None, None, None]
    k_ev0 = [None] * 4
    q_ev0 = [None] * 2
    nu0 = [0]
    v4_v0 = V4_sb.ap().rearrange("p (hr t) -> p hr t", hr=4)
    hoist = {}

    def hoist_after_tile(t):
        if t < 0:
            return
        if t == 23:
            v_kts = [13, 14, 15]
        elif t >= 10:
            v_kts = [t - 10]
        else:
            v_kts = []
        for kt in v_kts:
            pb = nu0[0] % 3
            nu0[0] += 1
            for cc in range(CC):
                tkv = mm(pAh[pb][:, :], x1T_v[:, cc, kt * 128:(kt + 1) * 128],
                         wv4_b[:, cc * 512:(cc + 1) * 512],
                         start=(cc == 0), stop=(cc == CC - 1),
                         inc="pe" if cc == CC - 1 else None,
                         waits=([t_wv4, pA_rd[pb], x1T_tok[kt]]
                                if cc == 0 else []))
            dv = v4_v0[:, :, kt * 128:(kt + 1) * 128]
            sv = pAh[pb].ap().rearrange("p (c n) -> p c n", c=4)
            ev = vec(lambda E, o=dv, i=sv: E.tensor_copy(out=o, in_=i),
                     waits=[tkv], chain=False)
            pA_rd[pb] = ev
            if kt == CC - 1:
                hoist["wv41"] = dma_w(wv4_b[:, :], d_wv4[1], "dV4",
                                      waits=[tkv])
        if t in (3, 7, 11, 15):
            u = (t - 3) // 4
            pb = nu0[0] % 3
            nu0[0] += 1
            for cc in range(CC):
                tkk = mm(pAh[pb][:, :],
                         wqk_b[:, cc * 256 + 128: cc * 256 + 256],
                         x1T_v[:, cc, u * 512:(u + 1) * 512],
                         start=(cc == 0), stop=(cc == CC - 1),
                         inc="pe" if cc == CC - 1 else None,
                         waits=([t_wqk, pA_rd[pb], x1T_tok[4 * u + 3]]
                                if cc == 0 else []))
            k_ev0[u] = vec(lambda E, o=K_sb[:, u * 512:(u + 1) * 512],
                           i=pAh[pb], b=battn[:, 16:17]:
                           E.tensor_scalar_add(o, i[:, :], b),
                           waits=[tkk, t_ba], chain=False)
            pA_rd[pb] = k_ev0[u]
        if t in (19, 23):
            u = (t - 19) // 4
            pb = nu0[0] % 3
            nu0[0] += 1
            for cc in range(CC):
                tkq = mm(pAh[pb][:, :],
                         wqk_b[:, cc * 256: cc * 256 + 128],
                         xqT_v[:, cc, u * 512:(u + 1) * 512],
                         start=(cc == 0), stop=(cc == CC - 1),
                         inc="pe" if cc == CC - 1 else None,
                         waits=([t_wqk, pA_rd[pb], xqT_tok[4 * u + 3]]
                                if cc == 0 else []))
            q_ev0[u] = vec(lambda E, o=Q_sb[:, u * 512:(u + 1) * 512],
                           i=pAh[pb], b=battn[:, 0:1]:
                           E.tensor_scalar_add(o, i[:, :], b),
                           waits=[tkq, t_ba], chain=False)
            pA_rd[pb] = q_ev0[u]
            if t == 23:
                hoist["prod"] = ("pe", em.count["pe"])
                hoist["wqk1"] = dma_w(wqk_b[:, :], d_wqk[1], "dQK",
                                      waits=[tkq])

    for i in range(CC):
        ln_tile(i, d_xkv[i * 128:(i + 1) * 128, :], x1T_v, x1T_tok, i)
        hoist_after_tile(i - 1)
    for i in range(8):
        ln_tile(CC + i, d_xown[i * 128:(i + 1) * 128, :], xqT_v, xqT_tok, i)
        hoist_after_tile(CC + i - 1)
    ln_transpose_flush(CC + 8)
    hoist_after_tile(23)

    ln1_end = em.snapshot()
    if "x1T" in DEBUG_TAPS:
        d_dbg_x1 = nc.dram_tensor("dbg_x1T", [128, CC * TKV], BF16, kind="ExternalOutput").ap()
        d_dbg_xq = nc.dram_tensor("dbg_xqT", [128, CC * TOWN], BF16, kind="ExternalOutput").ap()
        dma_o(d_dbg_x1, x1T[:, :], waits=[x1T_tok[CC - 1]] + ln1_end)
        dma_o(d_dbg_xq, xqT[:, :], waits=[xqT_tok[7]])
    sL.close()
    PT_sb = sA2.enter_context(nc.sbuf_tensor("PT_sb", [128, 16 * 512], BF16))
    dg_scr = sA2.enter_context(nc.sbuf_tensor("dg_scr", [128, 64], BF16))
    ytmp2 = [sA2.enter_context(nc.sbuf_tensor(f"ytmp{i}", [128, 512], F32))
             for i in range(2)]
    linv2 = [sA2.enter_context(nc.sbuf_tensor(f"linv{i}", [128, 512], F32))
             for i in range(2)]
    rrow = sA2.enter_context(nc.sbuf_tensor("rrow", [1, 1024], F32))
    pYb = [sA2.enter_context(nc.psum_tensor(f"pYb{i}", [128, 512], F32))
           for i in range(2)]

    for _eng in ("tensor", "vector", "scalar", "gpsimd"):
        em.wait(_eng, ln1_end)

    wp_tok = [None] * 4
    pYq = [[pYa[0], pYb[0]], [pYa[1], pYb[1]]]
    pS_rd = []
    pY_rd = [[None, None], [None, None]]
    PT_rd_tok = [None] * 16
    wqk_tok = t_wqk
    wv4_tok = t_wv4
    ytmp_rd = [None, None]
    linv_rd = [None, None]
    dg_rd = None
    rrow_rd = [None]
    deferred = None
    pool_deferred = None

    for h in range(NH):
        hr = h % 4
        bq = battn[:, h:h + 1]
        bk = battn[:, 16 + h:16 + h + 1]
        bv = battn[:, 32 + h:32 + h + 1]

        if h == 0:
            k_ev = k_ev0
            q_ev = q_ev0
            prod_last = hoist["prod"]
            wqk_tok = hoist["wqk1"]
            wv4_tok = hoist["wv41"]
        else:
            k_ev = [None] * 4
            nu_p = 0
            for u in range(4):
                pb = nu_p % 3
                nu_p += 1
                for cc in range(CC):
                    tk = mm(pAh[pb][:, :],
                            wqk_b[:, cc * 256 + 128: cc * 256 + 256],
                            x1T_v[:, cc, u * 512:(u + 1) * 512],
                            start=(cc == 0), stop=(cc == CC - 1),
                            inc="pe" if cc == CC - 1 else None,
                            waits=([wqk_tok, pA_rd[pb], x1T_tok[4 * u + 3]]
                                   if cc == 0 else []))
                k_ev[u] = vec(lambda E, o=K_sb[:, u * 512:(u + 1) * 512],
                              i=pAh[pb], b=bk:
                              E.tensor_scalar_add(o, i[:, :], b),
                              waits=[tk, t_ba], chain=False)
                pA_rd[pb] = k_ev[u]
            q_ev = [None] * 2
            for u in range(2):
                pb = nu_p % 3
                nu_p += 1
                for cc in range(CC):
                    tk = mm(pAh[pb][:, :],
                            wqk_b[:, cc * 256: cc * 256 + 128],
                            xqT_v[:, cc, u * 512:(u + 1) * 512],
                            start=(cc == 0), stop=(cc == CC - 1),
                            inc="pe" if cc == CC - 1 else None,
                            waits=([wqk_tok, pA_rd[pb], xqT_tok[4 * u + 3]]
                                   if cc == 0 else []))
                q_ev[u] = vec(lambda E, o=Q_sb[:, u * 512:(u + 1) * 512],
                              i=pAh[pb], b=bq:
                              E.tensor_scalar_add(o, i[:, :], b),
                              waits=[tk, t_ba], chain=False)
                pA_rd[pb] = q_ev[u]
            wqk_mm_done = tk
            prod_last = ("pe", em.count["pe"])
            if h + 1 < NH:
                wqk_tok = dma_w(wqk_b[:, :], d_wqk[h + 1], "dQK",
                                waits=[wqk_mm_done])

            if hr == 0:
                v4_v = V4_sb.ap().rearrange("p (hr t) -> p hr t", hr=4)
                for u in range(4):
                    for j in range(4):
                        kt = 4 * u + j
                        pb = nu_p % 3
                        nu_p += 1
                        for cc in range(CC):
                            tk = mm(pAh[pb][:, :],
                                    x1T_v[:, cc, kt * 128:(kt + 1) * 128],
                                    wv4_b[:, cc * 512:(cc + 1) * 512],
                                    start=(cc == 0), stop=(cc == CC - 1),
                                    inc="pe" if cc == CC - 1 else None,
                                    waits=([wv4_tok, pA_rd[pb], x1T_tok[kt]]
                                           if cc == 0 else []))
                        dv = v4_v[:, :, kt * 128:(kt + 1) * 128]
                        sv = pAh[pb].ap().rearrange("p (c n) -> p c n", c=4)
                        ev = vec(lambda E, o=dv, i=sv:
                                 E.tensor_copy(out=o, in_=i),
                                 waits=[tk], chain=False)
                        pA_rd[pb] = ev
                if h + 4 < NH:
                    wv4_tok = dma_w(wv4_b[:, :], d_wv4[h // 4 + 1], "dV4",
                                    waits=[tk])
                else:
                    wp_tok[0] = [dma_w(wv4_b[:, 2048 * j: 2048 * (j + 1)],
                                       d_wp[0][:, 2048 * j: 2048 * (j + 1)],
                                       f"dPf0_{j}",
                                       waits=([tk] if j == 0 else []))
                                 for j in range(4)]

        if deferred is not None:
            deferred()
            deferred = None

        ldone_q = [None, None]
        for qh in range(2):
            exp_tok = [None] * 16
            kcs = list(range(8 * qh + 8))
            lrow = pS[32 * qh: 32 * qh + 1, :]

            def emit_st(kc):
                pb = (kc + 2) % 3
                voff = max(0, kc - 8 * qh) * 64
                N = 512 - voff
                mm(pAh[pb][:, 0:N],
                   K_sb[:, kc * 128:(kc + 1) * 128],
                   Q_sb[:, qh * 512 + voff: (qh + 1) * 512],
                   start=True, stop=True, inc="pe",
                   waits=[k_ev[kc // 4], q_ev[qh], pA_rd[pb]])
                st_tok[kc] = ("pe", em.count["pe"])

            st_tok = [None] * 16
            LOOK = 3
            for kc in kcs[:LOOK]:
                emit_st(kc)
            for kc in kcs:
                pb = (kc + 2) % 3
                voff = max(0, kc - 8 * qh) * 64
                N = 512 - voff
                diag = kc >= 8 * qh
                msk = None
                if diag:
                    tdg = act(dg_scr[:, :], pAh[pb][:, 0:64], AF.Exp, scale=ISQ,
                              waits=[st_tok[kc], dg_rd])
                    msk = pool(lambda E,
                               o=PT_sb[:, kc * 512: kc * 512 + 64],
                               a=dg_scr, m=dmask:
                               E.tensor_mul(o, a[:, :], m[:, :]),
                               waits=[tdg, t_dm, PT_rd_tok[kc]])
                    dg_rd = msk
                    if N > 64:
                        exp_tok[kc] = act(PT_sb[:, kc * 512 + 64: kc * 512 + N],
                                          pAh[pb][:, 64:N], AF.Exp, scale=ISQ,
                                          waits=[PT_rd_tok[kc]])
                    else:
                        exp_tok[kc] = tdg
                else:
                    exp_tok[kc] = act(PT_sb[:, kc * 512: kc * 512 + N],
                                      pAh[pb][:, 0:N], AF.Exp, scale=ISQ,
                                      waits=[st_tok[kc], PT_rd_tok[kc]])
                pA_rd[pb] = exp_tok[kc]
                wl = [exp_tok[kc], t_ones]
                if msk is not None:
                    wl.append(msk)
                if kc == 0:
                    wl += pS_rd + [pY_rd[qh][h % 2]]
                mm(lrow[:, voff:512], ones_bf[:, :],
                   PT_sb[:, kc * 512: kc * 512 + N],
                   start=(kc == 0), stop=(kc == kcs[-1]), waits=wl)
                tv = mm(pYq[qh][h % 2][:, voff:512],
                        V4_sb[:, hr * TKV + kc * 128: hr * TKV + (kc + 1) * 128],
                        PT_sb[:, kc * 512: kc * 512 + N],
                        start=(kc == 0), stop=(kc == kcs[-1]), inc="pe")
                PT_rd_tok[kc] = tv
                if kc + LOOK in kcs:
                    emit_st(kc + LOOK)
            ldone_q[qh] = ("pe", em.count["pe"])

        def make_deferred(h=h, hr=hr, ldone=tuple(ldone_q)):
            def run():
                nonlocal pS_rd
                toks = []
                for qh in range(2):
                    lrow = pS[32 * qh: 32 * qh + 1, :]
                    t_ln = act(rrow[0:1, 0:512], lrow, AF.Ln,
                               waits=[ldone[qh], rrow_rd[0]])
                    t_li = act(rrow[0:1, 512:1024],
                               rrow[0:1, 0:512], AF.Exp, scale=-1.0,
                               waits=[t_ln])
                    toks.append(t_li)
                    t_bc = mm(pAh[qh][:, :], ones1f[0:1, :],
                              rrow[0:1, 512:1024],
                              start=True, stop=True, inc="pe",
                              waits=[t_li, t_ones1, pA_rd[qh]])
                    rrow_rd[0] = t_bc
                    t_lc = vec(lambda E, o=linv2[qh], i=pAh[qh]:
                               E.tensor_copy(out=o[:, :], in_=i[:, :]),
                               waits=[t_bc, linv_rd[qh]])
                    pA_rd[qh] = t_lc
                    dst = yallT[:, h * TOWN + qh * 512: h * TOWN + (qh + 1) * 512]
                    t_ym = vec(lambda E, o=dst, i=pYq[qh][h % 2],
                               l=linv2[qh]:
                               E.tensor_mul(o, i[:, :], l[:, :]),
                               waits=[t_lc, ldone[qh]])
                    pY_rd[qh][h % 2] = t_ym
                    linv_rd[qh] = t_ym
                pS_rd = toks
            return run

        deferred = make_deferred()

        if h == 13:
            wp1_j2 = dma_w(V4_sb[:, 1024:3072], d_wp[1][:, 4096:6144],
                           "dP1_2", waits=[("pe", em.count["pe"])])
        if h == 14:
            wp1_j3 = dma_w(V4_sb[:, 3072:5120], d_wp[1][:, 6144:8192],
                           "dP1_3", waits=[("pe", em.count["pe"])])

    deferred()
    deferred = None
    prod15 = prod_last

    sA2.close()
    sA.close()
    heads_end = em.snapshot()
    if "yallT" in DEBUG_TAPS:
        d_dbg_ya = nc.dram_tensor("dbg_yallT", [128, NH * TOWN], BF16, kind="ExternalOutput").ap()
        dma_o(d_dbg_ya, yallT[:, :], waits=heads_end)

    acc = S.enter_context(nc.sbuf_tensor("acc", [128, 8 * C], F32))
    x2T = S.enter_context(nc.sbuf_tensor("x2T", [128, CC * TOWN], BF16))
    x2T_v = x2T.ap().rearrange("p (cc t) -> p cc t", cc=CC)
    wg_b = [sF.enter_context(nc.sbuf_tensor("wg0", [128, CC * 128], BF16)),
            None]
    wu_b = [sF.enter_context(nc.sbuf_tensor("wu0", [128, CC * 128], BF16)),
            None]
    wp_b = [sP.enter_context(nc.sbuf_tensor(f"wp{i}", [128, CC * 512], BF16))
            for i in range(2)]
    acc_init = [None] * 8
    for tt in range(8):
        acc_init[tt] = dma_x(acc[:, tt * C:(tt + 1) * C],
                             d_xown[tt * 128:(tt + 1) * 128, :], f"dA{tt}",
                             waits=[prod15] if tt == 0 else [])
    wg_tok, wu_tok = {}, {}
    wg_rd = [None, None]
    wu_rd = [None, None]
    wg_tok[(0, 0)] = dma_w(wg_b[0][:, :], d_wg[0], "dG0", waits=[prod15])
    wu_tok[(0, 0)] = dma_w(wu_b[0][:, :], d_wu[0], "dU0")
    wpbufs = [wv4_b, wp_b[0], wp_b[1], wv4_b]
    wp_tok[1] = [dma_w(wp_b[0][:, 2048 * j: 2048 * (j + 1)],
                       d_wp[1][:, 2048 * j: 2048 * (j + 1)], f"dP1_{j}",
                       waits=(heads_end if j == 0 else []))
                 for j in range(2)] + [wp1_j2, wp1_j3]
    wp_tok[2] = [dma_w(wp_b[1][:, 2048 * j: 2048 * (j + 1)],
                       d_wp[2][:, 2048 * j: 2048 * (j + 1)], f"dP2_{j}",
                       waits=(heads_end if j == 0 else []))
                 for j in range(4)]
    bp_row = sP.enter_context(nc.sbuf_tensor("bp_sb", [1, C], BF16))
    bvp_sb = sP.enter_context(nc.sbuf_tensor("bvp_sb", [1, 2048], BF16))
    x2tm2 = [sP.enter_context(nc.sbuf_tensor(f"x2tm{i}", [128, C], BF16))
             for i in range(2)]
    lg_sb = sP.enter_context(nc.sbuf_tensor("lg_sb", [128, 8], F32))
    ex_sb = sP.enter_context(nc.sbuf_tensor("ex_sb", [128, 8], F32))
    ge_sb = sP.enter_context(nc.sbuf_tensor("ge_sb", [128, 8], F32))
    pT0b = sPp.enter_context(nc.psum_tensor("pT0b", [128, 1024], BF16))
    pT1b = sPp.enter_context(nc.psum_tensor("pT1b", [128, 1024], BF16))
    pTb = (pT0b, pT1b)

    t_bp = dma_w(bp_row[0:1, :], d_bp[0:1, :], "dBp", waits=heads_end)

    pT_rd2 = [None, None]
    x2_done = [None] * 8
    C_done = [None] * 8
    S_done = [None] * 8
    x2tm_rd = [None, None]
    pS_rd2 = [t for t in pS_rd]
    lg_rd = [None]
    ex_rd = [None]
    first_ln2 = [True]

    ln2_bf = {}

    def ln2_part1(tt, ready_tok):
        b = tt % 2
        src = acc[:, tt * C:(tt + 1) * C]
        if first_ln2[0]:
            em.wait("vector", heads_end)
            first_ln2[0] = False
        em.wait("vector", ready_tok)
        x2tm = x2tm2[tt % 2]
        tok_bf, _ = ln_stats_norm(b, src, x2tm[:, :],
                                  extra_norm_waits=[x2tm_rd[b]])
        ln2_bf[tt] = tok_bf

    def ln2_router(tt):
        b = tt % 2
        st = stats.ap()
        col = lambda k: st[:, b * 16 + k: b * 16 + k + 1]
        mx1, nmx, den, mx2, rde, s1, s2 = [col(k) for k in range(8, 15)]
        x2tm = x2tm2[tt % 2]
        tok_bf = ln2_bf[tt]
        last_ev = None
        for g in range(4):
            pb = (tt * 4 + g) % 2
            for j in range(4):
                cc = 4 * g + j
                tk = pe_tr(pTb[pb][:, j * 128:(j + 1) * 128],
                           x2tm[:, cc * 128:(cc + 1) * 128], id_bf[:, :],
                           inc="pe" if j == 3 else None,
                           waits=([tok_bf[g], pT_rd2[pb]] if j == 0 else []))
                _ = 0
            dv = x2T_v[:, 4 * g:4 * g + 4, tt * 128:(tt + 1) * 128]
            sv = pTb[pb].ap()[:, 0:512].rearrange("p (c n) -> p c n", c=4)
            last_ev = vec(lambda E, o=dv, i=sv: E.tensor_copy(out=o, in_=i),
                          waits=[tk])
            pT_rd2[pb] = last_ev
        x2tm_rd[b] = last_ev
        x2_done[tt] = last_ev
        for cc in range(CC):
            mm(pS[:, 0:8], x2T_v[:, cc, tt * 128:(tt + 1) * 128],
               wr_bf[:, cc * 8:(cc + 1) * 8],
               start=(cc == 0), stop=(cc == CC - 1),
               waits=([last_ev, t_wrbf] + pS_rd2) if cc == 0 else [])
        for cc in range(CC):
            tk = mm(pS[:, 8:9], x2T_v[:, cc, tt * 128:(tt + 1) * 128],
                    wsg_bf[:, cc:cc + 1],
                    start=(cc == 0), stop=(cc == CC - 1),
                    inc="pe" if cc == CC - 1 else None,
                    waits=[t_wsg] if cc == 0 else [])
        t_s1 = act(s1, pS[:, 8:9], AF.Exp, scale=-1.0, waits=[tk])
        t_s1b = vec(lambda E, o=s2, i=s1: E.tensor_scalar_add(o, i, 1.0),
                    waits=[t_s1])
        t_sln = act(s2, s2, AF.Ln, waits=[t_s1b])
        t_sg = act(s_tm.ap()[:, tt:tt + 1], s2, AF.Exp, scale=-1.0, waits=[t_sln])
        t_lg = vec(lambda E, o=lg_sb, i=pS: E.tensor_copy(out=o[:, :], in_=i[:, 0:8]),
                   waits=[tk, lg_rd[0]])
        pS_rd2[:] = [t_lg, t_s1]
        t_mx1 = vec(lambda E, o=mx1, i=lg_sb: E.reduce_max(o, i[:, :], axis=AX.X),
                    waits=[t_lg])
        t_nmx = vec(lambda E, o=nmx, i=mx1: E.tensor_scalar_mul(o, i, -1.0),
                    waits=[t_mx1])
        t_ex = act(ex_sb[:, :], lg_sb[:, :], AF.Exp, bias=nmx, accum_out=den,
                   waits=[t_nmx, ex_rd[0]])
        t_ge = vec(lambda E, o=ge_sb, i=lg_sb, m=mx1: E.tensor_scalar(
            out=o[:, :], in0=i[:, :], scalar1=m, scalar2=None, op0=ALU.is_ge),
            waits=[t_ex, t_mx1])
        t_lg2 = vec(lambda E, o=ge_sb, i=ge_sb, l=lg_sb: E.scalar_tensor_tensor(
            out=o[:, :], in0=i[:, :], scalar=-1e30, in1=l[:, :],
            op0=ALU.mult, op1=ALU.add), waits=[t_ge])
        t_mx2 = vec(lambda E, o=mx2, i=ge_sb: E.reduce_max(o, i[:, :], axis=AX.X),
                    waits=[t_lg2])
        t_sel = vec(lambda E, o=ge_sb, i=lg_sb, m=mx2: E.tensor_scalar(
            out=o[:, :], in0=i[:, :], scalar1=m, scalar2=None, op0=ALU.is_ge),
            waits=[t_mx2])
        lg_rd[0] = t_ex
        t_num = vec(lambda E, o=ex_sb, a=ex_sb, s=ge_sb:
                    E.tensor_mul(o[:, :], a[:, :], s[:, :]), waits=[t_sel, t_ex])
        t_rd = vec(lambda E, o=rde, i=den: E.reciprocal(o, i), waits=[t_ex])
        t_C = vec(lambda E, o=C_tm.ap()[:, tt * 8:(tt + 1) * 8], i=ex_sb, r=rde:
                  E.tensor_scalar_mul(o, i[:, :], r), waits=[t_num, t_rd])
        ex_rd[0] = t_C
        return t_C, t_sg


    stt_done = [[None] * 4 for _ in range(8)]
    first_proj = [True]
    pA_rd = pA_rd[:2]
    pS_bvp = [list(pS_rd)]
    t_bvp = {}
    wp_last = [None] * 4

    def emit_bvp(fg):
        wb = wpbufs[fg]
        for cc in range(CC):
            w0 = [wp_tok[fg][cc // 4]]
            if cc == 0:
                w0 += [t_bvbf] + pS_bvp[0]
            if first_proj[0]:
                w0 += heads_end
                first_proj[0] = False
            tkr = mm(pS[0:1, 0:512], bv_bf[:, cc:cc + 1],
                     wb[:, cc * 512:(cc + 1) * 512],
                     start=(cc == 0), stop=(cc == CC - 1),
                     inc="pe" if cc == CC - 1 else None,
                     waits=w0)
        t_bvp[fg] = vec(lambda E, o=bvp_sb[0:1, fg * 512:(fg + 1) * 512],
                        i=pS[0:1, 0:512],
                        b=bp_row[0:1, fg * 512:(fg + 1) * 512]:
                        E.tensor_add(o, i, b),
                        waits=[tkr, t_bp])
        pS_bvp[0] = [t_bvp[fg]]

    def emit_proj_tile(fg, tt, pb):
        wb = wpbufs[fg]
        for cc in range(CC):
            w0 = [wp_tok[fg][cc // 4]]
            if cc == 0:
                w0.append(pA_rd[pb])
            mm(pA[pb][:, :],
               yallT[:, cc * TOWN + tt * 128:
                     cc * TOWN + (tt + 1) * 128],
               wb[:, cc * 512:(cc + 1) * 512],
               start=(cc == 0), stop=False, waits=w0)
        tk = mm(pA[pb][:, :], ones1b[0:1, :],
                bvp_sb[0:1, fg * 512:(fg + 1) * 512],
                start=False, stop=True, inc="pe",
                waits=[t_bvp[fg], t_ones1b])
        wp_last[fg] = tk
        ev = vec(lambda E,
                 o=acc[:, tt * C + fg * 512: tt * C + (fg + 1) * 512],
                 i=pA[pb]: E.scalar_tensor_tensor(
                     out=o, in0=i[:, :], scalar=1.0, in1=o,
                     op0=ALU.mult, op1=ALU.add),
                 waits=[tk, acc_init[tt]])
        pA_rd[pb] = ev
        stt_done[tt][fg] = ev

    for fg in (0, 1):
        emit_bvp(fg)
        for tt in range(8):
            emit_proj_tile(fg, tt, tt % 2)
        if fg == 0:
            wp_tok[3] = [
                dma_w(wv4_b[:, 2048 * j: 2048 * (j + 1)],
                      d_wp[3][:, 2048 * j: 2048 * (j + 1)], f"dP3_{j}",
                      waits=([wp_last[0]] if j == 0 else []))
                for j in range(4)]
    emit_bvp(2)
    emit_proj_tile(2, 0, 0)
    emit_bvp(3)
    emit_proj_tile(3, 0, 1)
    ln2_part1(0, stt_done[0][3])
    for tt in range(1, 8):
        emit_proj_tile(2, tt, 0)
        emit_proj_tile(3, tt, 1)
        if tt > 1:
            C_done[tt - 2], S_done[tt - 2] = ln2_router(tt - 2)
        ln2_part1(tt, stt_done[tt][3])
    for _t in (6, 7):
        C_done[_t], S_done[_t] = ln2_router(_t)

    acc1_tok = [None] * 8
    if "acc1" in DEBUG_TAPS:
        d_dbg_a1 = nc.dram_tensor("dbg_acc1", [TOWN, C], F32, kind="ExternalOutput").ap()
        for tt in range(8):
            acc1_tok[tt] = dma_o(d_dbg_a1[tt * 128:(tt + 1) * 128, :],
                                 acc[:, tt * C:(tt + 1) * C],
                                 waits=[stt_done[tt][3]])
    if "x2T" in DEBUG_TAPS:
        d_dbg_x2 = nc.dram_tensor("dbg_x2T", [128, CC * TOWN], BF16, kind="ExternalOutput").ap()
        dma_o(d_dbg_x2, x2T[:, :], waits=[x2_done[tt] for tt in range(8)])
    if "Ctm" in DEBUG_TAPS:
        d_dbg_ct = nc.dram_tensor("dbg_Ctm", [128, 64], F32, kind="ExternalOutput").ap()
        d_dbg_st = nc.dram_tensor("dbg_stm", [128, 8], F32, kind="ExternalOutput").ap()
        dma_o(d_dbg_ct, C_tm[:, :], waits=[C_done[tt] for tt in range(8)])
        dma_o(d_dbg_st, s_tm[:, :], waits=[S_done[tt] for tt in range(8)])

    ffn_barrier = em.snapshot()

    sPp.close()
    sP.close()
    wg_b[1] = sF.enter_context(nc.sbuf_tensor("wg1", [128, CC * 128], BF16))
    wu_b[1] = sF.enter_context(nc.sbuf_tensor("wu1", [128, CC * 128], BF16))
    wg_tok[(0, 1)] = dma_w(wg_b[1][:, :], d_wg[1], "dG1", waits=ffn_barrier)
    wu_tok[(0, 1)] = dma_w(wu_b[1][:, :], d_wu[1], "dU1")
    wd_b = [wv4_b,
            sF.enter_context(nc.sbuf_tensor("wd1", [128, FC * 512], BF16))]
    H_sb = sF.enter_context(nc.sbuf_tensor("H_sb", [128, FC * TOWN], BF16))
    sgl = sF.enter_context(nc.sbuf_tensor("sgl", [128, TOWN], F32))
    pG = sF.enter_context(nc.psum_tensor("pG", [128, 1024], F32))
    pU = sF.enter_context(nc.psum_tensor("pU", [128, 1024], F32))

    wd_tok = {}
    wd_rd = [None, None]
    x2_all = list(x2_done)
    pG_rd = None
    pU_rd = None
    sgl_rd = None
    H_wr = [None] * FC
    acc_fin = [None] * 8
    first_ffn = [True]

    nu = 0
    for e in range(NE):
        for fg in (0, 1):
            j = (e * 4 + fg) % 2
            wd_tok[(e, fg)] = dma_w(
                wd_b[j][:, 0:FC * 512], d_wd[e * 4 + fg], f"dD{j}",
                waits=([wd_rd[j]] + (ffn_barrier if (e, fg) == (0, 0) else [])))
        for fc in range(FC):
            j = nu % 2
            w0 = [wg_tok[(e, fc)], pG_rd, x2_done[3]]
            if first_ffn[0]:
                w0 += ffn_barrier
                first_ffn[0] = False
            for half in range(2):
                for cc in range(CC):
                    tkG = mm(pG[:, half * 512:(half + 1) * 512],
                             wg_b[j][:, cc * 128:(cc + 1) * 128],
                             x2T_v[:, cc, half * 512:(half + 1) * 512],
                             start=(cc == 0), stop=(cc == CC - 1),
                             inc="pe" if (half == 1 and cc == CC - 1) else None,
                             waits=(w0 if half == 0 else [x2_done[7]])
                             if cc == 0 else [])
            for half in range(2):
                for cc in range(CC):
                    tkU = mm(pU[:, half * 512:(half + 1) * 512],
                             wu_b[j][:, cc * 128:(cc + 1) * 128],
                             x2T_v[:, cc, half * 512:(half + 1) * 512],
                             start=(cc == 0), stop=(cc == CC - 1),
                             inc="pe" if (half == 1 and cc == CC - 1) else None,
                             waits=[wu_tok[(e, fc)], pU_rd]
                             if (half == 0 and cc == 0) else [])
            ne_, nf_ = (e, fc + 1) if fc + 1 < FC else (e + 1, 0)
            if ne_ < NE and (ne_, nf_) not in wg_tok:
                wg_tok[(ne_, nf_)] = dma_w(wg_b[1 - j][:, :],
                                           d_wg[ne_ * FC + nf_], f"dG{1 - j}",
                                           waits=[wg_rd[1 - j]])
                wu_tok[(ne_, nf_)] = dma_w(wu_b[1 - j][:, :],
                                           d_wu[ne_ * FC + nf_], f"dU{1 - j}",
                                           waits=[wu_rd[1 - j]])
            wg_rd[j] = tkG
            wu_rd[j] = tkU
            t_si = act(sgl[:, :], pG[:, :], AF.Silu, waits=[tkG, sgl_rd])
            pG_rd = t_si
            t_H = vec(lambda E, o=H_sb[:, fc * TOWN:(fc + 1) * TOWN], s=sgl, u=pU:
                      E.tensor_mul(o, s[:, :], u[:, :]),
                      waits=[t_si, tkU])
            pU_rd = t_H
            sgl_rd = t_H
            H_wr[fc] = t_H
            nu += 1
        h_all = list(H_wr)
        for fg in range(4):
            j = (e * 4 + fg) % 2
            for tt in range(8):
                pb = tt % 2
                for kc in range(FC):
                    tk = mm(pA[pb][:, :],
                            H_sb[:, kc * TOWN + tt * 128:
                                 kc * TOWN + (tt + 1) * 128],
                            wd_b[j][:, kc * 512:(kc + 1) * 512],
                            start=(kc == 0), stop=(kc == FC - 1),
                            inc="pe" if kc == FC - 1 else None,
                            waits=([wd_tok[(e, fg)], pA_rd[pb]] + h_all)
                            if kc == 0 else [])
                gate = (C_tm.ap()[:, tt * 8 + e: tt * 8 + e + 1] if e < 8
                        else s_tm.ap()[:, tt:tt + 1])
                gw = [tk, C_done[tt] if e < 8 else S_done[tt]]
                if e == 0:
                    gw.append(stt_done[tt][fg])
                    gw.append(acc1_tok[tt])
                ev = vec(lambda E,
                         o=acc[:, tt * C + fg * 512: tt * C + (fg + 1) * 512],
                         i=pA[pb], g=gate: E.scalar_tensor_tensor(
                             out=o, in0=i[:, :], scalar=g, in1=o,
                             op0=ALU.mult, op1=ALU.add),
                         waits=gw)
                pA_rd[pb] = ev
                if e == NE - 1:
                    acc_fin[tt] = ev
            wd_rd[j] = tk
            if fg < 2:
                j2 = (e * 4 + fg + 2) % 2
                wd_tok[(e, fg + 2)] = dma_w(wd_b[j2][:, 0:FC * 512],
                                            d_wd[e * 4 + fg + 2],
                                            f"dD{j2}", waits=[wd_rd[j2] or tk])

    for tt in range(8):
        em.getsem(f"dO{tt}")
        em.op("sync",
              lambda E, o=d_y[tt * 128:(tt + 1) * 128, :],
              i=acc[:, tt * C:(tt + 1) * C]: E.dma_start(out=o, in_=i),
              inc=f"dO{tt}", inc_n=16, waits=[acc_fin[tt]])
    for tt in range(8):
        em.prog["sync"].append(("w", f"dO{tt}", 16))

    with ExitStack() as semstack:
        semh = {}
        for s in em.count:
            semh[s] = semstack.enter_context(nc.semaphore(f"{s}_s"))
        with nc.Block() as block:

            def replay(E, name):
                for it in em.prog[name]:
                    if it[0] == "w":
                        E.wait_ge(semh[it[1]], it[2])
                    else:
                        _, fn, inc, inc_n = it
                        ins = fn(E)
                        if inc is not None:
                            ins.then_inc(semh[inc], inc_n)

            @block.tensor
            def _(t):
                replay(t, "tensor")

            @block.vector
            def _(v):
                replay(v, "vector")

            @block.scalar
            def _(sc):
                replay(sc, "scalar")

            @block.gpsimd
            def _(g):
                replay(g, "gpsimd")

            @block.sync
            def _(s):
                replay(s, "sync")

    sF.close()
    S.close()
    return nc


def prep_inputs(inputs):
    bf = ml_dtypes.bfloat16
    Wa = np.asarray(inputs["L0_attn_W_attn"], dtype=np.float32)
    ba = np.asarray(inputs["L0_attn_b_attn"], dtype=np.float32)
    Wp = np.asarray(inputs["L0_attn_W_proj"], dtype=np.float32)
    bp = np.asarray(inputs["L0_attn_b_proj"], dtype=np.float32)
    Wr = np.asarray(inputs["L0_ffn_Wr"], dtype=np.float32)
    Wg = np.asarray(inputs["L0_ffn_Wg"], dtype=np.float32)
    Wu = np.asarray(inputs["L0_ffn_Wu"], dtype=np.float32)
    Wd = np.asarray(inputs["L0_ffn_Wd"], dtype=np.float32)
    Sg = np.asarray(inputs["L0_ffn_Sg"], dtype=np.float32)
    Su = np.asarray(inputs["L0_ffn_Su"], dtype=np.float32)
    Sd = np.asarray(inputs["L0_ffn_Sd"], dtype=np.float32)
    Wsg = np.asarray(inputs["L0_ffn_Wsg"], dtype=np.float32)

    w5 = Wa.reshape(CC, 128, 3, NH, 128)
    wqk = np.ascontiguousarray(
        w5[:, :, 0:2].transpose(3, 1, 0, 2, 4).reshape(NH, 128, CC * 256))
    wv4 = np.ascontiguousarray(
        w5[:, :, 2].reshape(CC, 128, 4, 4, 128).transpose(2, 1, 0, 3, 4)
        .reshape(4, 128, CC * 512))
    battn = np.ascontiguousarray(ba.reshape(48, 128).T)
    wp = np.ascontiguousarray(
        Wp.reshape(CC, 128, 4, 512).transpose(2, 1, 0, 3).reshape(4, 128, CC * 512))
    bp_row = np.ascontiguousarray(bp.reshape(1, C))
    wr = np.ascontiguousarray(
        Wr.reshape(CC, 128, 8).transpose(1, 0, 2).reshape(128, 128))
    wsg = np.ascontiguousarray(Wsg.reshape(CC, 128).T)
    wg_r = Wg.reshape(8, CC, 128, FC, 128).transpose(0, 3, 2, 1, 4)
    sg_r = Sg.reshape(CC, 128, 4, FC, 128).transpose(2, 3, 1, 0, 4)
    wg_all = np.ascontiguousarray(
        np.concatenate([wg_r, sg_r], axis=0).reshape(NE * FC, 128, CC * 128))
    wu_r = Wu.reshape(8, CC, 128, FC, 128).transpose(0, 3, 2, 1, 4)
    su_r = Su.reshape(CC, 128, 4, FC, 128).transpose(2, 3, 1, 0, 4)
    wu_all = np.ascontiguousarray(
        np.concatenate([wu_r, su_r], axis=0).reshape(NE * FC, 128, CC * 128))
    wd_r = Wd.reshape(8, FC, 128, 4, 512).transpose(0, 3, 2, 1, 4)
    sd_r = Sd.reshape(4, FC, 128, 4, 512).transpose(0, 3, 2, 1, 4)
    wd_all = np.ascontiguousarray(
        np.concatenate([wd_r, sd_r], axis=0).reshape(NE * 4, 128, FC * 512))
    ident = np.eye(128, dtype=bf)

    x = np.asarray(inputs["L0_attn_x"], dtype=np.float32)
    in_maps = []
    for core in range(8):
        b, h = core // 2, core % 2
        xb = x[b]
        own = np.ascontiguousarray(
            xb.reshape(CC, 128, C)[:, 64 * h:64 * h + 64, :].reshape(TOWN, C))
        dmask_np = ((np.arange(64)[None, :] + 64 * h)
                    >= np.arange(128)[:, None]).astype(bf)
        in_maps.append(dict(
            x_kv=np.ascontiguousarray(xb),
            x_own=own,
            dmask=np.ascontiguousarray(dmask_np),
            ident=ident,
            wqk=wqk, wv4=wv4, battn=battn, wp=wp, bp_row=bp_row,
            wr=wr, wsg=wsg, wg=wg_all, wu=wu_all, wd=wd_all,
        ))
    return in_maps


def kernel(**inputs):
    in_maps = prep_inputs(inputs)
    nc = build_nc()
    res = run_bass_kernel_spmd(nc, in_maps, core_ids=list(range(8)))
    out = np.empty((4, 2048, C), dtype=np.float32)
    for core in range(8):
        b, h = core // 2, core % 2
        yc = np.asarray(res.results[core]["y"]).reshape(CC, 64, C)
        out[b].reshape(CC, 128, C)[:, 64 * h:64 * h + 64, :] = yc
    return out


# version 18
# speedup vs baseline: 1.2017x; 1.0051x over previous
import numpy as np
import ml_dtypes
from contextlib import ExitStack

import concourse.bass as bass
import concourse.mybir as mybir
from concourse.bass_utils import run_bass_kernel_spmd

F32 = mybir.dt.float32
BF16 = mybir.dt.bfloat16
AF = mybir.ActivationFunctionType
ALU = mybir.AluOpType
AX = mybir.AxisListType

C = 2048
CC = 16
NH = 16
TKV = 2048
TOWN = 1024
NE = 12
FC = 11
ISQ = float(1.0 / np.sqrt(128.0))
LN_EPS = 1e-5

DEBUG_TAPS = ()

ENGN = ("tensor", "vector", "scalar", "gpsimd", "sync")
SEMS = ("pe", "ve", "se", "po")


class Emit:

    def __init__(self):
        self.prog = {e: [] for e in ENGN}
        self.count = {s: 0 for s in SEMS}
        self.lastw = {e: {} for e in ENGN}

    def getsem(self, s):
        if s not in self.count:
            self.count[s] = 0

    def wait(self, eng, tok):
        if tok is None:
            return
        if isinstance(tok, (list, tuple)) and tok and isinstance(tok[0], tuple):
            for t in tok:
                self.wait(eng, t)
            return
        sem, v = tok
        assert v <= self.count[sem], f"wait on future token {sem}:{v}"
        if self.lastw[eng].get(sem, 0) >= v:
            return
        self.lastw[eng][sem] = v
        self.prog[eng].append(("w", sem, v))

    _CHAIN = {"vector": "ve", "scalar": "se", "gpsimd": "po"}

    def op(self, eng, fn, inc=None, inc_n=1, waits=(), chain=True):
        ch = self._CHAIN.get(eng)
        if chain and ch is not None and inc == ch and self.count[ch] > 0:
            self.wait(eng, (ch, self.count[ch]))
        for t in waits:
            self.wait(eng, t)
        tok = None
        if inc is not None:
            self.count[inc] += inc_n
            tok = (inc, self.count[inc])
        self.prog[eng].append(("i", fn, inc, inc_n))
        return tok

    def snapshot(self):
        return [(s, self.count[s]) for s in ("pe", "ve", "se", "po")
                if self.count[s] > 0]


def build_nc():
    nc = bass.Bass("TRN2", target_bir_lowering=False)

    d_xkv = nc.dram_tensor("x_kv", [TKV, C], F32, kind="ExternalInput").ap()
    d_xown = nc.dram_tensor("x_own", [TOWN, C], F32, kind="ExternalInput").ap()
    d_dmask = nc.dram_tensor("dmask", [128, 64], BF16, kind="ExternalInput").ap()
    d_ident = nc.dram_tensor("ident", [128, 128], BF16, kind="ExternalInput").ap()
    d_wqk = nc.dram_tensor("wqk", [NH, 128, CC * 256], F32, kind="ExternalInput").ap()
    d_wv4 = nc.dram_tensor("wv4", [4, 128, CC * 512], F32, kind="ExternalInput").ap()
    d_battn = nc.dram_tensor("battn", [128, 48], F32, kind="ExternalInput").ap()
    d_wp = nc.dram_tensor("wp", [4, 128, CC * 512], F32, kind="ExternalInput").ap()
    d_bp = nc.dram_tensor("bp_row", [1, C], F32, kind="ExternalInput").ap()
    d_wr = nc.dram_tensor("wr", [128, 128], F32, kind="ExternalInput").ap()
    d_wsg = nc.dram_tensor("wsg", [128, 16], F32, kind="ExternalInput").ap()
    d_wg = nc.dram_tensor("wg", [NE * FC, 128, CC * 128], F32, kind="ExternalInput").ap()
    d_wu = nc.dram_tensor("wu", [NE * FC, 128, CC * 128], F32, kind="ExternalInput").ap()
    d_wd = nc.dram_tensor("wd", [NE * 4, 128, FC * 512], F32, kind="ExternalInput").ap()
    d_y = nc.dram_tensor("y", [TOWN, C], F32, kind="ExternalOutput").ap()

    def row3(ap3, i):
        return ap3[i:i + 1, :, :].rearrange("o p f -> p (o f)")

    em = Emit()

    def mm(out, lhsT, rhs, start, stop, inc=None, waits=()):
        return em.op(
            "tensor",
            lambda E, o=out, l=lhsT, r=rhs, st=start, sp=stop: E.matmul(
                o, l, r, start=st, stop=sp),
            inc=inc, waits=waits)

    def pe_tr(out, in_, ident, inc=None, waits=()):
        return em.op(
            "tensor",
            lambda E, o=out, i=in_, d=ident: E.transpose(o, i, d),
            inc=inc, waits=waits)

    def act(out, in_, func, bias=0.0, scale=1.0, accum_out=None, waits=(),
            chain=True):
        return em.op(
            "scalar",
            lambda E, o=out, i=in_, f=func, b=bias, s=scale, a=accum_out:
            E.activation(o, i, f, bias=b, scale=s, accum_out=a),
            inc="se", waits=waits, chain=chain)

    def vec(fn, waits=(), chain=True):
        return em.op("vector", fn, inc="ve", waits=waits, chain=chain)

    def pool(fn, waits=()):
        return em.op("gpsimd", fn, inc="po", waits=waits)

    def _dma(eng, sem, out, in_, waits):
        em.getsem(sem)
        pre = em.count[sem]
        if pre > 0:
            waits = list(waits) + [(sem, pre)]
        return em.op(eng,
                     lambda E, o=out, i=in_: E.dma_start(out=o, in_=i),
                     inc=sem, inc_n=16, waits=waits)

    def dma_w(out, in_, sem, waits=()):
        return _dma("gpsimd", sem, out, in_, waits)

    def dma_x(out, in_, sem, waits=()):
        return _dma("sync", sem, out, in_, waits)

    def dma_o(out, in_, waits=()):
        return _dma("sync", "dO", out, in_, waits)

    S = ExitStack()
    sA = ExitStack()
    sL = ExitStack()
    sA2 = ExitStack()
    sP = ExitStack()
    sPp = ExitStack()
    sF = ExitStack()

    id_bf = S.enter_context(nc.sbuf_tensor("id_bf", [128, 128], BF16))
    dmask = S.enter_context(nc.sbuf_tensor("dmask_sb", [128, 64], BF16))
    ones_bf = S.enter_context(nc.sbuf_tensor("ones_bf", [128, 1], BF16))
    ones1f = S.enter_context(nc.sbuf_tensor("ones1f", [1, 128], F32))
    ones1b = S.enter_context(nc.sbuf_tensor("ones1b", [1, 128], BF16))
    bv_bf = S.enter_context(nc.sbuf_tensor("bv_bf", [128, 16], BF16))
    battn = S.enter_context(nc.sbuf_tensor("battn_sb", [128, 48], F32))
    wr_f = S.enter_context(nc.sbuf_tensor("wr_f", [128, 128], F32))
    wr_bf = S.enter_context(nc.sbuf_tensor("wr_bf", [128, 128], BF16))
    wsg_bf = S.enter_context(nc.sbuf_tensor("wsg_bf", [128, 16], BF16))
    stats = S.enter_context(nc.sbuf_tensor("stats", [128, 48], F32))
    bnst = S.enter_context(nc.sbuf_tensor("bnst", [128, 96], F32))
    C_tm = S.enter_context(nc.sbuf_tensor("C_tm", [128, 64], F32))
    s_tm = S.enter_context(nc.sbuf_tensor("s_tm", [128, 8], F32))
    yallT = S.enter_context(nc.sbuf_tensor("yallT", [128, NH * TOWN], BF16))
    wv4_b = S.enter_context(nc.sbuf_tensor("wv4_b", [128, CC * 512], BF16))

    pA0 = S.enter_context(nc.psum_tensor("pA0", [128, 512], F32))
    pA1 = S.enter_context(nc.psum_tensor("pA1", [128, 512], F32))
    pS = S.enter_context(nc.psum_tensor("pS", [128, 512], F32))
    pA = (pA0, pA1)

    x1T = sA.enter_context(nc.sbuf_tensor("x1T", [128, CC * TKV], BF16))
    xqT = sA.enter_context(nc.sbuf_tensor("xqT", [128, CC * TOWN], BF16))
    wqk_b = sA.enter_context(nc.sbuf_tensor("wqk_b", [128, CC * 256], BF16))
    K_sb = sA.enter_context(nc.sbuf_tensor("K_sb", [128, TKV], BF16))
    Q_sb = sA.enter_context(nc.sbuf_tensor("Q_sb", [128, TOWN], BF16))
    V4_sb = sA.enter_context(nc.sbuf_tensor("V4_sb", [128, 4 * TKV], BF16))
    pA2 = sA.enter_context(nc.psum_tensor("pA2", [128, 512], F32))
    pYa = [sA.enter_context(nc.psum_tensor(f"pY{i}", [128, 512], F32))
           for i in range(2)]


    pT0 = sL.enter_context(nc.psum_tensor("pT0", [128, 1024], BF16))
    pT1 = sL.enter_context(nc.psum_tensor("pT1", [128, 1024], BF16))
    pT = (pT0, pT1)
    lnx = [sL.enter_context(nc.sbuf_tensor(f"lnx{i}", [128, C], F32)) for i in range(2)]
    lntm = [sL.enter_context(nc.sbuf_tensor(f"lntm{i}", [128, C], BF16)) for i in range(2)]

    x1T_v = x1T.ap().rearrange("p (cc t) -> p cc t", cc=CC)
    xqT_v = xqT.ap().rearrange("p (cc t) -> p cc t", cc=CC)

    dma_x(id_bf[:, :], d_ident[:, :], "dC")
    dma_x(dmask[:, :], d_dmask[:, :], "dC")
    dma_x(battn[:, :], d_battn[:, :], "dC")
    dma_x(wr_f[:, :], d_wr[:, :], "dC")
    t_id = t_dm = t_ba = t_wr = ("dC", 64)
    t_ones = vec(lambda E: E.memset(ones_bf[:, :], 1.0))
    t_ones1 = vec(lambda E: E.memset(ones1f[:, :], 1.0))
    t_ones1b = vec(lambda E: E.memset(ones1b[:, :], 1.0))
    t_wrbf = vec(lambda E: E.tensor_copy(out=wr_bf[:, :], in_=wr_f[:, :]),
                 waits=[t_wr])
    t_bvbf = vec(lambda E: E.tensor_copy(out=bv_bf[:, :], in_=battn[:, 32:48]),
                 waits=[t_ba])
    t_wsg = dma_w(wsg_bf[:, :], d_wsg[:, :], "dC2")
    t_wqk = dma_w(wqk_b[:, :], d_wqk[0], "dQK")
    t_wv4 = dma_w(wv4_b[:, :], d_wv4[0], "dV4")

    ln_cons = [[[] for _ in range(4)] for _ in range(2)]
    tm_cons = [None] * 2
    pT_rd = [None, None]
    x1T_tok = [None] * CC
    xqT_tok = [None] * 8

    def ln_stats_norm(b, src_f32_ap, dst_bf_ap, extra_norm_waits=(),
                      dma_toks=None):
        st = stats.ap()
        col = lambda k: st[:, b * 16 + k: b * 16 + k + 1]
        var_c, t1_c, r_c, nmr_c = [col(k) for k in range(4, 8)]
        nm_c = col(3)
        bn = bnst.ap()[:, b * 32: b * 32 + 24].rearrange("p (c k) -> p c k", c=4)
        mv = bnst.ap()[:, b * 32 + 24: b * 32 + 26]
        for ch in range(4):
            t_bn = vec(lambda E, o=bn[:, ch, :],
                       i=src_f32_ap.rearrange("p (c n) -> p c n", c=4)[:, ch, :]:
                       E.bn_stats(o, i), chain=(ch == 0),
                       waits=([dma_toks[ch]] if dma_toks else []))
        t_ag = vec(lambda E, o=mv, i=bnst.ap()[:, b * 32: b * 32 + 24]:
                   E.bn_aggr(o, i), waits=[t_bn])
        vec(lambda E, o=var_c, i=mv: E.tensor_scalar_add(o, i[:, 1:2], LN_EPS),
            waits=[t_ag])
        tok_nm_ = vec(lambda E, o=nm_c, i=mv: E.tensor_scalar_mul(o, i[:, 0:1],
                                                                  -1.0))
        t_l1 = act(t1_c, var_c, AF.Ln, waits=[tok_nm_])
        act(r_c, t1_c, AF.Exp, scale=-0.5, waits=[t_l1])
        em.op("scalar", lambda E, o=nmr_c, i=nm_c, r=r_c: E.mul(o, i, r),
              inc="se")
        toks = []
        for ch in range(4):
            toks.append(act(dst_bf_ap[:, ch * 512:(ch + 1) * 512],
                            src_f32_ap[:, ch * 512:(ch + 1) * 512],
                            AF.Identity, bias=nmr_c, scale=r_c,
                            waits=list(extra_norm_waits), chain=(ch == 0)))
        return toks, None

    pend = [None]

    def ln_transpose_flush(idx):
        b, tok_nm, destT_v, dest_tok_list, dest_i = pend[0]
        last_ev = None
        for g in range(4):
            pb = (idx * 4 + g) % 2
            for j in range(4):
                cc = 4 * g + j
                tk = pe_tr(pT[pb][:, j * 128:(j + 1) * 128],
                           lntm[b][:, cc * 128:(cc + 1) * 128], id_bf[:, :],
                           inc="pe" if j == 3 else None,
                           waits=([tok_nm[g], t_id, pT_rd[pb]]
                                  if j == 0 else []))
            dv = destT_v[:, 4 * g:4 * g + 4, dest_i * 128:(dest_i + 1) * 128]
            sv = pT[pb].ap()[:, 0:512].rearrange("p (c n) -> p c n", c=4)
            last_ev = vec(lambda E, o=dv, i=sv: E.tensor_copy(out=o, in_=i),
                          waits=[tk], chain=(g == 0))
            pT_rd[pb] = last_ev
        tm_cons[b] = last_ev
        dest_tok_list[dest_i] = last_ev
        pend[0] = None

    def ln_tile(idx, src_ap, destT_v, dest_tok_list, dest_i):
        b = idx % 2
        tok_d = [dma_x(lnx[b][:, ch * 512:(ch + 1) * 512],
                       src_ap[:, ch * 512:(ch + 1) * 512], f"dL{b}_{ch}",
                       waits=ln_cons[b][ch])
                 for ch in range(4)]
        tok_nm, _ = ln_stats_norm(b, lnx[b][:, :], lntm[b][:, :],
                                  extra_norm_waits=[tm_cons[b]],
                                  dma_toks=tok_d)
        ln_cons[b] = [[tok_nm[ch]] for ch in range(4)]
        if pend[0] is not None:
            ln_transpose_flush(idx)
        pend[0] = (b, tok_nm, destT_v, dest_tok_list, dest_i)

    pAh = (pA0, pA1, pA2)
    pA_rd = [None, None, None]
    k_ev0 = [None] * 4
    q_ev0 = [None] * 2
    nu0 = [0]
    v4_v0 = V4_sb.ap().rearrange("p (hr t) -> p hr t", hr=4)
    hoist = {}

    def hoist_after_tile(t):
        if t < 0:
            return
        if t == 23:
            v_kts = [13, 14, 15]
        elif t >= 10:
            v_kts = [t - 10]
        else:
            v_kts = []
        for kt in v_kts:
            pb = nu0[0] % 3
            nu0[0] += 1
            for cc in range(CC):
                tkv = mm(pAh[pb][:, :], x1T_v[:, cc, kt * 128:(kt + 1) * 128],
                         wv4_b[:, cc * 512:(cc + 1) * 512],
                         start=(cc == 0), stop=(cc == CC - 1),
                         inc="pe" if cc == CC - 1 else None,
                         waits=([t_wv4, pA_rd[pb], x1T_tok[kt]]
                                if cc == 0 else []))
            dv = v4_v0[:, :, kt * 128:(kt + 1) * 128]
            sv = pAh[pb].ap().rearrange("p (c n) -> p c n", c=4)
            ev = vec(lambda E, o=dv, i=sv: E.tensor_copy(out=o, in_=i),
                     waits=[tkv], chain=False)
            pA_rd[pb] = ev
            if kt == CC - 1:
                hoist["wv41"] = dma_w(wv4_b[:, :], d_wv4[1], "dV4",
                                      waits=[tkv])
        if t in (3, 7, 11, 15):
            u = (t - 3) // 4
            pb = nu0[0] % 3
            nu0[0] += 1
            for cc in range(CC):
                tkk = mm(pAh[pb][:, :],
                         wqk_b[:, cc * 256 + 128: cc * 256 + 256],
                         x1T_v[:, cc, u * 512:(u + 1) * 512],
                         start=(cc == 0), stop=(cc == CC - 1),
                         inc="pe" if cc == CC - 1 else None,
                         waits=([t_wqk, pA_rd[pb], x1T_tok[4 * u + 3]]
                                if cc == 0 else []))
            k_ev0[u] = vec(lambda E, o=K_sb[:, u * 512:(u + 1) * 512],
                           i=pAh[pb], b=battn[:, 16:17]:
                           E.tensor_scalar_add(o, i[:, :], b),
                           waits=[tkk, t_ba], chain=False)
            pA_rd[pb] = k_ev0[u]
        if t in (19, 23):
            u = (t - 19) // 4
            pb = nu0[0] % 3
            nu0[0] += 1
            for cc in range(CC):
                tkq = mm(pAh[pb][:, :],
                         wqk_b[:, cc * 256: cc * 256 + 128],
                         xqT_v[:, cc, u * 512:(u + 1) * 512],
                         start=(cc == 0), stop=(cc == CC - 1),
                         inc="pe" if cc == CC - 1 else None,
                         waits=([t_wqk, pA_rd[pb], xqT_tok[4 * u + 3]]
                                if cc == 0 else []))
            q_ev0[u] = vec(lambda E, o=Q_sb[:, u * 512:(u + 1) * 512],
                           i=pAh[pb], b=battn[:, 0:1]:
                           E.tensor_scalar_add(o, i[:, :], b),
                           waits=[tkq, t_ba], chain=False)
            pA_rd[pb] = q_ev0[u]
            if t == 23:
                hoist["prod"] = ("pe", em.count["pe"])
                hoist["wqk1"] = dma_w(wqk_b[:, :], d_wqk[1], "dQK",
                                      waits=[tkq])

    for i in range(CC):
        ln_tile(i, d_xkv[i * 128:(i + 1) * 128, :], x1T_v, x1T_tok, i)
        hoist_after_tile(i - 1)
    for i in range(8):
        ln_tile(CC + i, d_xown[i * 128:(i + 1) * 128, :], xqT_v, xqT_tok, i)
        hoist_after_tile(CC + i - 1)
    ln_transpose_flush(CC + 8)
    hoist_after_tile(23)

    ln1_end = em.snapshot()
    if "x1T" in DEBUG_TAPS:
        d_dbg_x1 = nc.dram_tensor("dbg_x1T", [128, CC * TKV], BF16, kind="ExternalOutput").ap()
        d_dbg_xq = nc.dram_tensor("dbg_xqT", [128, CC * TOWN], BF16, kind="ExternalOutput").ap()
        dma_o(d_dbg_x1, x1T[:, :], waits=[x1T_tok[CC - 1]] + ln1_end)
        dma_o(d_dbg_xq, xqT[:, :], waits=[xqT_tok[7]])
    sL.close()
    PT_sb = sA2.enter_context(nc.sbuf_tensor("PT_sb", [128, 16 * 512], BF16))
    dg_scr = sA2.enter_context(nc.sbuf_tensor("dg_scr", [128, 64], BF16))
    ytmp2 = [sA2.enter_context(nc.sbuf_tensor(f"ytmp{i}", [128, 512], F32))
             for i in range(2)]
    linv2 = [sA2.enter_context(nc.sbuf_tensor(f"linv{i}", [128, 512], F32))
             for i in range(2)]
    rrow = sA2.enter_context(nc.sbuf_tensor("rrow", [1, 1024], F32))
    pYb = [sA2.enter_context(nc.psum_tensor(f"pYb{i}", [128, 512], F32))
           for i in range(2)]

    for _eng in ("tensor", "vector", "scalar", "gpsimd"):
        em.wait(_eng, ln1_end)

    wp_tok = [None] * 4
    pYq = [[pYa[0], pYb[0]], [pYa[1], pYb[1]]]
    pS_rd = []
    pS_lnrd = []
    yall_tok = [[None, None] for _ in range(NH)]
    pY_rd = [[None, None], [None, None]]
    PT_rd_tok = [None] * 16
    wqk_tok = t_wqk
    wv4_tok = t_wv4
    ytmp_rd = [None, None]
    linv_rd = [None, None]
    dg_rd = None
    rrow_rd = [None]
    deferred = None
    pool_deferred = None

    for h in range(NH):
        hr = h % 4
        bq = battn[:, h:h + 1]
        bk = battn[:, 16 + h:16 + h + 1]
        bv = battn[:, 32 + h:32 + h + 1]

        if h == 0:
            k_ev = k_ev0
            q_ev = q_ev0
            prod_last = hoist["prod"]
            wqk_tok = hoist["wqk1"]
            wv4_tok = hoist["wv41"]
        else:
            k_ev = [None] * 4
            nu_p = 0
            for u in range(4):
                pb = nu_p % 3
                nu_p += 1
                for cc in range(CC):
                    tk = mm(pAh[pb][:, :],
                            wqk_b[:, cc * 256 + 128: cc * 256 + 256],
                            x1T_v[:, cc, u * 512:(u + 1) * 512],
                            start=(cc == 0), stop=(cc == CC - 1),
                            inc="pe" if cc == CC - 1 else None,
                            waits=([wqk_tok, pA_rd[pb], x1T_tok[4 * u + 3]]
                                   if cc == 0 else []))
                k_ev[u] = vec(lambda E, o=K_sb[:, u * 512:(u + 1) * 512],
                              i=pAh[pb], b=bk:
                              E.tensor_scalar_add(o, i[:, :], b),
                              waits=[tk, t_ba], chain=False)
                pA_rd[pb] = k_ev[u]
            q_ev = [None] * 2
            for u in range(2):
                pb = nu_p % 3
                nu_p += 1
                for cc in range(CC):
                    tk = mm(pAh[pb][:, :],
                            wqk_b[:, cc * 256: cc * 256 + 128],
                            xqT_v[:, cc, u * 512:(u + 1) * 512],
                            start=(cc == 0), stop=(cc == CC - 1),
                            inc="pe" if cc == CC - 1 else None,
                            waits=([wqk_tok, pA_rd[pb], xqT_tok[4 * u + 3]]
                                   if cc == 0 else []))
                q_ev[u] = vec(lambda E, o=Q_sb[:, u * 512:(u + 1) * 512],
                              i=pAh[pb], b=bq:
                              E.tensor_scalar_add(o, i[:, :], b),
                              waits=[tk, t_ba], chain=False)
                pA_rd[pb] = q_ev[u]
            wqk_mm_done = tk
            prod_last = ("pe", em.count["pe"])
            if h + 1 < NH:
                wqk_tok = dma_w(wqk_b[:, :], d_wqk[h + 1], "dQK",
                                waits=[wqk_mm_done])

            if hr == 0:
                v4_v = V4_sb.ap().rearrange("p (hr t) -> p hr t", hr=4)
                for u in range(4):
                    for j in range(4):
                        kt = 4 * u + j
                        pb = nu_p % 3
                        nu_p += 1
                        for cc in range(CC):
                            tk = mm(pAh[pb][:, :],
                                    x1T_v[:, cc, kt * 128:(kt + 1) * 128],
                                    wv4_b[:, cc * 512:(cc + 1) * 512],
                                    start=(cc == 0), stop=(cc == CC - 1),
                                    inc="pe" if cc == CC - 1 else None,
                                    waits=([wv4_tok, pA_rd[pb], x1T_tok[kt]]
                                           if cc == 0 else []))
                        dv = v4_v[:, :, kt * 128:(kt + 1) * 128]
                        sv = pAh[pb].ap().rearrange("p (c n) -> p c n", c=4)
                        ev = vec(lambda E, o=dv, i=sv:
                                 E.tensor_copy(out=o, in_=i),
                                 waits=[tk], chain=False)
                        pA_rd[pb] = ev
                if h + 4 < NH:
                    wv4_tok = dma_w(wv4_b[:, :], d_wv4[h // 4 + 1], "dV4",
                                    waits=[tk])
                else:
                    wp_tok[0] = [dma_w(wv4_b[:, 2048 * j: 2048 * (j + 1)],
                                       d_wp[0][:, 2048 * j: 2048 * (j + 1)],
                                       f"dPf0_{j}",
                                       waits=([tk] if j == 0 else []))
                                 for j in range(4)]

        if deferred is not None:
            deferred()
            deferred = None

        ldone_q = [None, None]
        for qh in range(2):
            exp_tok = [None] * 16
            kcs = list(range(8 * qh + 8))
            lrow = pS[32 * qh: 32 * qh + 1, :]

            def emit_st(kc):
                pb = (kc + 2) % 3
                voff = max(0, kc - 8 * qh) * 64
                N = 512 - voff
                mm(pAh[pb][:, 0:N],
                   K_sb[:, kc * 128:(kc + 1) * 128],
                   Q_sb[:, qh * 512 + voff: (qh + 1) * 512],
                   start=True, stop=True, inc="pe",
                   waits=[k_ev[kc // 4], q_ev[qh], pA_rd[pb]])
                st_tok[kc] = ("pe", em.count["pe"])

            st_tok = [None] * 16
            LOOK = 3
            for kc in kcs[:LOOK]:
                emit_st(kc)
            for kc in kcs:
                pb = (kc + 2) % 3
                voff = max(0, kc - 8 * qh) * 64
                N = 512 - voff
                diag = kc >= 8 * qh
                msk = None
                if diag:
                    tdg = act(dg_scr[:, :], pAh[pb][:, 0:64], AF.Exp, scale=ISQ,
                              waits=[st_tok[kc], dg_rd])
                    msk = pool(lambda E,
                               o=PT_sb[:, kc * 512: kc * 512 + 64],
                               a=dg_scr, m=dmask:
                               E.tensor_mul(o, a[:, :], m[:, :]),
                               waits=[tdg, t_dm, PT_rd_tok[kc]])
                    dg_rd = msk
                    if N > 64:
                        exp_tok[kc] = act(PT_sb[:, kc * 512 + 64: kc * 512 + N],
                                          pAh[pb][:, 64:N], AF.Exp, scale=ISQ,
                                          waits=[PT_rd_tok[kc]])
                    else:
                        exp_tok[kc] = tdg
                else:
                    exp_tok[kc] = act(PT_sb[:, kc * 512: kc * 512 + N],
                                      pAh[pb][:, 0:N], AF.Exp, scale=ISQ,
                                      waits=[st_tok[kc], PT_rd_tok[kc]])
                pA_rd[pb] = exp_tok[kc]
                wl = [exp_tok[kc], t_ones]
                if msk is not None:
                    wl.append(msk)
                if kc == 0:
                    wl += pS_rd + [pY_rd[qh][h % 2]]
                mm(lrow[:, voff:512], ones_bf[:, :],
                   PT_sb[:, kc * 512: kc * 512 + N],
                   start=(kc == 0), stop=(kc == kcs[-1]), waits=wl)
                tv = mm(pYq[qh][h % 2][:, voff:512],
                        V4_sb[:, hr * TKV + kc * 128: hr * TKV + (kc + 1) * 128],
                        PT_sb[:, kc * 512: kc * 512 + N],
                        start=(kc == 0), stop=(kc == kcs[-1]), inc="pe")
                PT_rd_tok[kc] = tv
                if kc + LOOK in kcs:
                    emit_st(kc + LOOK)
            ldone_q[qh] = ("pe", em.count["pe"])

        def make_deferred(h=h, hr=hr, ldone=tuple(ldone_q)):
            def run():
                nonlocal pS_rd, pS_lnrd
                toks = []
                ln_toks = []
                for qh in range(2):
                    lrow = pS[32 * qh: 32 * qh + 1, :]
                    t_ln = act(rrow[0:1, 0:512], lrow, AF.Ln,
                               waits=[ldone[qh], rrow_rd[0]])
                    ln_toks.append(t_ln)
                    t_li = act(rrow[0:1, 512:1024],
                               rrow[0:1, 0:512], AF.Exp, scale=-1.0,
                               waits=[t_ln])
                    toks.append(t_li)
                    t_bc = mm(pAh[qh][:, :], ones1f[0:1, :],
                              rrow[0:1, 512:1024],
                              start=True, stop=True, inc="pe",
                              waits=[t_li, t_ones1, pA_rd[qh]])
                    rrow_rd[0] = t_bc
                    t_lc = vec(lambda E, o=linv2[qh], i=pAh[qh]:
                               E.tensor_copy(out=o[:, :], in_=i[:, :]),
                               waits=[t_bc, linv_rd[qh]])
                    pA_rd[qh] = t_lc
                    dst = yallT[:, h * TOWN + qh * 512: h * TOWN + (qh + 1) * 512]
                    t_ym = vec(lambda E, o=dst, i=pYq[qh][h % 2],
                               l=linv2[qh]:
                               E.tensor_mul(o, i[:, :], l[:, :]),
                               waits=[t_lc, ldone[qh]])
                    pY_rd[qh][h % 2] = t_ym
                    linv_rd[qh] = t_ym
                    yall_tok[h][qh] = t_ym
                pS_rd = toks
                pS_lnrd = ln_toks
            return run

        deferred = make_deferred()

        if h == 13:
            wp1_j2 = dma_w(V4_sb[:, 1024:3072], d_wp[1][:, 4096:6144],
                           "dP1_2", waits=[("pe", em.count["pe"])])
        if h == 14:
            wp1_j3 = dma_w(V4_sb[:, 3072:5120], d_wp[1][:, 6144:8192],
                           "dP1_3", waits=[("pe", em.count["pe"])])

    deferred()
    deferred = None
    prod15 = prod_last

    sA2.close()
    sA.close()
    heads_end = em.snapshot()
    if "yallT" in DEBUG_TAPS:
        d_dbg_ya = nc.dram_tensor("dbg_yallT", [128, NH * TOWN], BF16, kind="ExternalOutput").ap()
        dma_o(d_dbg_ya, yallT[:, :], waits=heads_end)

    acc = S.enter_context(nc.sbuf_tensor("acc", [128, 8 * C], F32))
    x2T = S.enter_context(nc.sbuf_tensor("x2T", [128, CC * TOWN], BF16))
    x2T_v = x2T.ap().rearrange("p (cc t) -> p cc t", cc=CC)
    wg_b = [sF.enter_context(nc.sbuf_tensor("wg0", [128, CC * 128], BF16)),
            None]
    wu_b = [sF.enter_context(nc.sbuf_tensor("wu0", [128, CC * 128], BF16)),
            None]
    wp_b = [sP.enter_context(nc.sbuf_tensor(f"wp{i}", [128, CC * 512], BF16))
            for i in range(2)]
    acc_init = [None] * 8
    for tt in range(8):
        acc_init[tt] = dma_x(acc[:, tt * C:(tt + 1) * C],
                             d_xown[tt * 128:(tt + 1) * 128, :], f"dA{tt}",
                             waits=[prod15] if tt == 0 else [])
    wg_tok, wu_tok = {}, {}
    wg_rd = [None, None]
    wu_rd = [None, None]
    wg_tok[(0, 0)] = dma_w(wg_b[0][:, :], d_wg[0], "dG0", waits=[prod15])
    wu_tok[(0, 0)] = dma_w(wu_b[0][:, :], d_wu[0], "dU0")
    wpbufs = [wv4_b, wp_b[0], wp_b[1], wv4_b]
    wp_tok[1] = [dma_w(wp_b[0][:, 2048 * j: 2048 * (j + 1)],
                       d_wp[1][:, 2048 * j: 2048 * (j + 1)], f"dP1_{j}",
                       waits=(heads_end if j == 0 else []))
                 for j in range(2)] + [wp1_j2, wp1_j3]
    wp_tok[2] = [dma_w(wp_b[1][:, 2048 * j: 2048 * (j + 1)],
                       d_wp[2][:, 2048 * j: 2048 * (j + 1)], f"dP2_{j}",
                       waits=(heads_end if j == 0 else []))
                 for j in range(4)]
    bp_row = sP.enter_context(nc.sbuf_tensor("bp_sb", [1, C], BF16))
    bvp_sb = sP.enter_context(nc.sbuf_tensor("bvp_sb", [1, 2048], BF16))
    x2tm2 = [sP.enter_context(nc.sbuf_tensor(f"x2tm{i}", [128, C], BF16))
             for i in range(2)]
    lg_sb = sP.enter_context(nc.sbuf_tensor("lg_sb", [128, 8], F32))
    ex_sb = sP.enter_context(nc.sbuf_tensor("ex_sb", [128, 8], F32))
    ge_sb = sP.enter_context(nc.sbuf_tensor("ge_sb", [128, 8], F32))
    pT0b = sPp.enter_context(nc.psum_tensor("pT0b", [128, 1024], BF16))
    pT1b = sPp.enter_context(nc.psum_tensor("pT1b", [128, 1024], BF16))
    pTb = (pT0b, pT1b)

    t_bp = dma_w(bp_row[0:1, :], d_bp[0:1, :], "dBp", waits=heads_end)

    pT_rd2 = [None, None]
    x2_done = [None] * 8
    C_done = [None] * 8
    S_done = [None] * 8
    x2tm_rd = [None, None]
    pS_rd2 = [t for t in pS_rd]
    lg_rd = [None]
    ex_rd = [None]
    first_ln2 = [True]

    ln2_bf = {}

    def ln2_part1(tt, ready_tok):
        b = tt % 2
        src = acc[:, tt * C:(tt + 1) * C]
        if first_ln2[0]:
            em.wait("vector", heads_end)
            first_ln2[0] = False
        em.wait("vector", ready_tok)
        x2tm = x2tm2[tt % 2]
        tok_bf, _ = ln_stats_norm(b, src, x2tm[:, :],
                                  extra_norm_waits=[x2tm_rd[b]])
        ln2_bf[tt] = tok_bf

    def ln2_router(tt):
        b = tt % 2
        st = stats.ap()
        col = lambda k: st[:, b * 16 + k: b * 16 + k + 1]
        mx1, nmx, den, mx2, rde, s1, s2 = [col(k) for k in range(8, 15)]
        x2tm = x2tm2[tt % 2]
        tok_bf = ln2_bf[tt]
        last_ev = None
        for g in range(4):
            pb = (tt * 4 + g) % 2
            for j in range(4):
                cc = 4 * g + j
                tk = pe_tr(pTb[pb][:, j * 128:(j + 1) * 128],
                           x2tm[:, cc * 128:(cc + 1) * 128], id_bf[:, :],
                           inc="pe" if j == 3 else None,
                           waits=([tok_bf[g], pT_rd2[pb]] if j == 0 else []))
                _ = 0
            dv = x2T_v[:, 4 * g:4 * g + 4, tt * 128:(tt + 1) * 128]
            sv = pTb[pb].ap()[:, 0:512].rearrange("p (c n) -> p c n", c=4)
            last_ev = vec(lambda E, o=dv, i=sv: E.tensor_copy(out=o, in_=i),
                          waits=[tk])
            pT_rd2[pb] = last_ev
        x2tm_rd[b] = last_ev
        x2_done[tt] = last_ev
        for cc in range(CC):
            mm(pS[:, 0:8], x2T_v[:, cc, tt * 128:(tt + 1) * 128],
               wr_bf[:, cc * 8:(cc + 1) * 8],
               start=(cc == 0), stop=(cc == CC - 1),
               waits=([last_ev, t_wrbf] + pS_rd2) if cc == 0 else [])
        for cc in range(CC):
            tk = mm(pS[:, 8:9], x2T_v[:, cc, tt * 128:(tt + 1) * 128],
                    wsg_bf[:, cc:cc + 1],
                    start=(cc == 0), stop=(cc == CC - 1),
                    inc="pe" if cc == CC - 1 else None,
                    waits=[t_wsg] if cc == 0 else [])
        t_s1 = act(s1, pS[:, 8:9], AF.Exp, scale=-1.0, waits=[tk])
        t_s1b = vec(lambda E, o=s2, i=s1: E.tensor_scalar_add(o, i, 1.0),
                    waits=[t_s1])
        t_sln = act(s2, s2, AF.Ln, waits=[t_s1b])
        t_sg = act(s_tm.ap()[:, tt:tt + 1], s2, AF.Exp, scale=-1.0, waits=[t_sln])
        t_lg = vec(lambda E, o=lg_sb, i=pS: E.tensor_copy(out=o[:, :], in_=i[:, 0:8]),
                   waits=[tk, lg_rd[0]])
        pS_rd2[:] = [t_lg, t_s1]
        t_mx1 = vec(lambda E, o=mx1, i=lg_sb: E.reduce_max(o, i[:, :], axis=AX.X),
                    waits=[t_lg])
        t_nmx = vec(lambda E, o=nmx, i=mx1: E.tensor_scalar_mul(o, i, -1.0),
                    waits=[t_mx1])
        t_ex = act(ex_sb[:, :], lg_sb[:, :], AF.Exp, bias=nmx, accum_out=den,
                   waits=[t_nmx, ex_rd[0]])
        t_ge = vec(lambda E, o=ge_sb, i=lg_sb, m=mx1: E.tensor_scalar(
            out=o[:, :], in0=i[:, :], scalar1=m, scalar2=None, op0=ALU.is_ge),
            waits=[t_ex, t_mx1])
        t_lg2 = vec(lambda E, o=ge_sb, i=ge_sb, l=lg_sb: E.scalar_tensor_tensor(
            out=o[:, :], in0=i[:, :], scalar=-1e30, in1=l[:, :],
            op0=ALU.mult, op1=ALU.add), waits=[t_ge])
        t_mx2 = vec(lambda E, o=mx2, i=ge_sb: E.reduce_max(o, i[:, :], axis=AX.X),
                    waits=[t_lg2])
        t_sel = vec(lambda E, o=ge_sb, i=lg_sb, m=mx2: E.tensor_scalar(
            out=o[:, :], in0=i[:, :], scalar1=m, scalar2=None, op0=ALU.is_ge),
            waits=[t_mx2])
        lg_rd[0] = t_ex
        t_num = vec(lambda E, o=ex_sb, a=ex_sb, s=ge_sb:
                    E.tensor_mul(o[:, :], a[:, :], s[:, :]), waits=[t_sel, t_ex])
        t_rd = vec(lambda E, o=rde, i=den: E.reciprocal(o, i), waits=[t_ex])
        t_C = vec(lambda E, o=C_tm.ap()[:, tt * 8:(tt + 1) * 8], i=ex_sb, r=rde:
                  E.tensor_scalar_mul(o, i[:, :], r), waits=[t_num, t_rd])
        ex_rd[0] = t_C
        return t_C, t_sg


    stt_done = [[None] * 4 for _ in range(8)]
    pA_rd = pA_rd[:2]
    pS_bvp = [[pS_lnrd[0]]]
    t_bvp = {}
    wp_last = [None] * 4

    def emit_bvp(fg):
        wb = wpbufs[fg]
        for cc in range(CC):
            w0 = [wp_tok[fg][cc // 4]]
            if cc == 0:
                w0 += [t_bvbf] + pS_bvp[0]
            tkr = mm(pS[0:1, 0:512], bv_bf[:, cc:cc + 1],
                     wb[:, cc * 512:(cc + 1) * 512],
                     start=(cc == 0), stop=(cc == CC - 1),
                     inc="pe" if cc == CC - 1 else None,
                     waits=w0)
        t_bvp[fg] = vec(lambda E, o=bvp_sb[0:1, fg * 512:(fg + 1) * 512],
                        i=pS[0:1, 0:512],
                        b=bp_row[0:1, fg * 512:(fg + 1) * 512]:
                        E.tensor_add(o, i, b),
                        waits=[tkr, t_bp])
        pS_bvp[0] = [t_bvp[fg]]

    def emit_proj_tile(fg, tt, pb):
        wb = wpbufs[fg]
        for cc in range(CC):
            w0 = [wp_tok[fg][cc // 4], yall_tok[cc][tt // 4]]
            if cc == 0:
                w0.append(pA_rd[pb])
            mm(pA[pb][:, :],
               yallT[:, cc * TOWN + tt * 128:
                     cc * TOWN + (tt + 1) * 128],
               wb[:, cc * 512:(cc + 1) * 512],
               start=(cc == 0), stop=False, waits=w0)
        tk = mm(pA[pb][:, :], ones1b[0:1, :],
                bvp_sb[0:1, fg * 512:(fg + 1) * 512],
                start=False, stop=True, inc="pe",
                waits=[t_bvp[fg], t_ones1b])
        wp_last[fg] = tk
        ev = vec(lambda E,
                 o=acc[:, tt * C + fg * 512: tt * C + (fg + 1) * 512],
                 i=pA[pb]: E.scalar_tensor_tensor(
                     out=o, in0=i[:, :], scalar=1.0, in1=o,
                     op0=ALU.mult, op1=ALU.add),
                 waits=[tk, acc_init[tt]])
        pA_rd[pb] = ev
        stt_done[tt][fg] = ev

    for fg in (0, 1):
        emit_bvp(fg)
        for tt in range(8):
            emit_proj_tile(fg, tt, tt % 2)
        if fg == 0:
            wp_tok[3] = [
                dma_w(wv4_b[:, 2048 * j: 2048 * (j + 1)],
                      d_wp[3][:, 2048 * j: 2048 * (j + 1)], f"dP3_{j}",
                      waits=([wp_last[0]] if j == 0 else []))
                for j in range(4)]
    emit_bvp(2)
    emit_proj_tile(2, 0, 0)
    emit_bvp(3)
    emit_proj_tile(3, 0, 1)
    ln2_part1(0, stt_done[0][3])
    for tt in range(1, 8):
        emit_proj_tile(2, tt, 0)
        emit_proj_tile(3, tt, 1)
        if tt > 1:
            C_done[tt - 2], S_done[tt - 2] = ln2_router(tt - 2)
        ln2_part1(tt, stt_done[tt][3])
    for _t in (6, 7):
        C_done[_t], S_done[_t] = ln2_router(_t)

    acc1_tok = [None] * 8
    if "acc1" in DEBUG_TAPS:
        d_dbg_a1 = nc.dram_tensor("dbg_acc1", [TOWN, C], F32, kind="ExternalOutput").ap()
        for tt in range(8):
            acc1_tok[tt] = dma_o(d_dbg_a1[tt * 128:(tt + 1) * 128, :],
                                 acc[:, tt * C:(tt + 1) * C],
                                 waits=[stt_done[tt][3]])
    if "x2T" in DEBUG_TAPS:
        d_dbg_x2 = nc.dram_tensor("dbg_x2T", [128, CC * TOWN], BF16, kind="ExternalOutput").ap()
        dma_o(d_dbg_x2, x2T[:, :], waits=[x2_done[tt] for tt in range(8)])
    if "Ctm" in DEBUG_TAPS:
        d_dbg_ct = nc.dram_tensor("dbg_Ctm", [128, 64], F32, kind="ExternalOutput").ap()
        d_dbg_st = nc.dram_tensor("dbg_stm", [128, 8], F32, kind="ExternalOutput").ap()
        dma_o(d_dbg_ct, C_tm[:, :], waits=[C_done[tt] for tt in range(8)])
        dma_o(d_dbg_st, s_tm[:, :], waits=[S_done[tt] for tt in range(8)])

    ffn_barrier = em.snapshot()

    sPp.close()
    sP.close()
    wg_b[1] = sF.enter_context(nc.sbuf_tensor("wg1", [128, CC * 128], BF16))
    wu_b[1] = sF.enter_context(nc.sbuf_tensor("wu1", [128, CC * 128], BF16))
    wg_tok[(0, 1)] = dma_w(wg_b[1][:, :], d_wg[1], "dG1", waits=ffn_barrier)
    wu_tok[(0, 1)] = dma_w(wu_b[1][:, :], d_wu[1], "dU1")
    wd_b = [wv4_b,
            sF.enter_context(nc.sbuf_tensor("wd1", [128, FC * 512], BF16))]
    H_sb = sF.enter_context(nc.sbuf_tensor("H_sb", [128, FC * TOWN], BF16))
    sgl = sF.enter_context(nc.sbuf_tensor("sgl", [128, TOWN], F32))
    pG = sF.enter_context(nc.psum_tensor("pG", [128, 1024], F32))
    pU = sF.enter_context(nc.psum_tensor("pU", [128, 1024], F32))

    wd_tok = {}
    wd_rd = [None, None]
    x2_all = list(x2_done)
    pG_rd = None
    pU_rd = None
    sgl_rd = None
    H_wr = [None] * FC
    acc_fin = [None] * 8
    first_ffn = [True]

    nu = 0
    for e in range(NE):
        for fg in (0, 1):
            j = (e * 4 + fg) % 2
            wd_tok[(e, fg)] = dma_w(
                wd_b[j][:, 0:FC * 512], d_wd[e * 4 + fg], f"dD{j}",
                waits=([wd_rd[j]] + (ffn_barrier if (e, fg) == (0, 0) else [])))
        for fc in range(FC):
            j = nu % 2
            w0 = [wg_tok[(e, fc)], pG_rd, x2_done[3]]
            if first_ffn[0]:
                w0 += ffn_barrier
                first_ffn[0] = False
            for half in range(2):
                for cc in range(CC):
                    tkG = mm(pG[:, half * 512:(half + 1) * 512],
                             wg_b[j][:, cc * 128:(cc + 1) * 128],
                             x2T_v[:, cc, half * 512:(half + 1) * 512],
                             start=(cc == 0), stop=(cc == CC - 1),
                             inc="pe" if (half == 1 and cc == CC - 1) else None,
                             waits=(w0 if half == 0 else [x2_done[7]])
                             if cc == 0 else [])
            for half in range(2):
                for cc in range(CC):
                    tkU = mm(pU[:, half * 512:(half + 1) * 512],
                             wu_b[j][:, cc * 128:(cc + 1) * 128],
                             x2T_v[:, cc, half * 512:(half + 1) * 512],
                             start=(cc == 0), stop=(cc == CC - 1),
                             inc="pe" if (half == 1 and cc == CC - 1) else None,
                             waits=[wu_tok[(e, fc)], pU_rd]
                             if (half == 0 and cc == 0) else [])
            ne_, nf_ = (e, fc + 1) if fc + 1 < FC else (e + 1, 0)
            if ne_ < NE and (ne_, nf_) not in wg_tok:
                wg_tok[(ne_, nf_)] = dma_w(wg_b[1 - j][:, :],
                                           d_wg[ne_ * FC + nf_], f"dG{1 - j}",
                                           waits=[wg_rd[1 - j]])
                wu_tok[(ne_, nf_)] = dma_w(wu_b[1 - j][:, :],
                                           d_wu[ne_ * FC + nf_], f"dU{1 - j}",
                                           waits=[wu_rd[1 - j]])
            wg_rd[j] = tkG
            wu_rd[j] = tkU
            t_si = act(sgl[:, :], pG[:, :], AF.Silu, waits=[tkG, sgl_rd])
            pG_rd = t_si
            t_H = vec(lambda E, o=H_sb[:, fc * TOWN:(fc + 1) * TOWN], s=sgl, u=pU:
                      E.tensor_mul(o, s[:, :], u[:, :]),
                      waits=[t_si, tkU])
            pU_rd = t_H
            sgl_rd = t_H
            H_wr[fc] = t_H
            nu += 1
        h_all = list(H_wr)
        for fg in range(4):
            j = (e * 4 + fg) % 2
            for tt in range(8):
                pb = tt % 2
                for kc in range(FC):
                    tk = mm(pA[pb][:, :],
                            H_sb[:, kc * TOWN + tt * 128:
                                 kc * TOWN + (tt + 1) * 128],
                            wd_b[j][:, kc * 512:(kc + 1) * 512],
                            start=(kc == 0), stop=(kc == FC - 1),
                            inc="pe" if kc == FC - 1 else None,
                            waits=([wd_tok[(e, fg)], pA_rd[pb]] + h_all)
                            if kc == 0 else [])
                gate = (C_tm.ap()[:, tt * 8 + e: tt * 8 + e + 1] if e < 8
                        else s_tm.ap()[:, tt:tt + 1])
                gw = [tk, C_done[tt] if e < 8 else S_done[tt]]
                if e == 0:
                    gw.append(stt_done[tt][fg])
                    gw.append(acc1_tok[tt])
                ev = vec(lambda E,
                         o=acc[:, tt * C + fg * 512: tt * C + (fg + 1) * 512],
                         i=pA[pb], g=gate: E.scalar_tensor_tensor(
                             out=o, in0=i[:, :], scalar=g, in1=o,
                             op0=ALU.mult, op1=ALU.add),
                         waits=gw)
                pA_rd[pb] = ev
                if e == NE - 1:
                    acc_fin[tt] = ev
            wd_rd[j] = tk
            if fg < 2:
                j2 = (e * 4 + fg + 2) % 2
                wd_tok[(e, fg + 2)] = dma_w(wd_b[j2][:, 0:FC * 512],
                                            d_wd[e * 4 + fg + 2],
                                            f"dD{j2}", waits=[wd_rd[j2] or tk])

    for tt in range(8):
        em.getsem(f"dO{tt}")
        em.op("sync",
              lambda E, o=d_y[tt * 128:(tt + 1) * 128, :],
              i=acc[:, tt * C:(tt + 1) * C]: E.dma_start(out=o, in_=i),
              inc=f"dO{tt}", inc_n=16, waits=[acc_fin[tt]])
    for tt in range(8):
        em.prog["sync"].append(("w", f"dO{tt}", 16))

    with ExitStack() as semstack:
        semh = {}
        for s in em.count:
            semh[s] = semstack.enter_context(nc.semaphore(f"{s}_s"))
        with nc.Block() as block:

            def replay(E, name):
                for it in em.prog[name]:
                    if it[0] == "w":
                        E.wait_ge(semh[it[1]], it[2])
                    else:
                        _, fn, inc, inc_n = it
                        ins = fn(E)
                        if inc is not None:
                            ins.then_inc(semh[inc], inc_n)

            @block.tensor
            def _(t):
                replay(t, "tensor")

            @block.vector
            def _(v):
                replay(v, "vector")

            @block.scalar
            def _(sc):
                replay(sc, "scalar")

            @block.gpsimd
            def _(g):
                replay(g, "gpsimd")

            @block.sync
            def _(s):
                replay(s, "sync")

    sF.close()
    S.close()
    return nc


def prep_inputs(inputs):
    bf = ml_dtypes.bfloat16
    Wa = np.asarray(inputs["L0_attn_W_attn"], dtype=np.float32)
    ba = np.asarray(inputs["L0_attn_b_attn"], dtype=np.float32)
    Wp = np.asarray(inputs["L0_attn_W_proj"], dtype=np.float32)
    bp = np.asarray(inputs["L0_attn_b_proj"], dtype=np.float32)
    Wr = np.asarray(inputs["L0_ffn_Wr"], dtype=np.float32)
    Wg = np.asarray(inputs["L0_ffn_Wg"], dtype=np.float32)
    Wu = np.asarray(inputs["L0_ffn_Wu"], dtype=np.float32)
    Wd = np.asarray(inputs["L0_ffn_Wd"], dtype=np.float32)
    Sg = np.asarray(inputs["L0_ffn_Sg"], dtype=np.float32)
    Su = np.asarray(inputs["L0_ffn_Su"], dtype=np.float32)
    Sd = np.asarray(inputs["L0_ffn_Sd"], dtype=np.float32)
    Wsg = np.asarray(inputs["L0_ffn_Wsg"], dtype=np.float32)

    w5 = Wa.reshape(CC, 128, 3, NH, 128)
    wqk = np.ascontiguousarray(
        w5[:, :, 0:2].transpose(3, 1, 0, 2, 4).reshape(NH, 128, CC * 256))
    wv4 = np.ascontiguousarray(
        w5[:, :, 2].reshape(CC, 128, 4, 4, 128).transpose(2, 1, 0, 3, 4)
        .reshape(4, 128, CC * 512))
    battn = np.ascontiguousarray(ba.reshape(48, 128).T)
    wp = np.ascontiguousarray(
        Wp.reshape(CC, 128, 4, 512).transpose(2, 1, 0, 3).reshape(4, 128, CC * 512))
    bp_row = np.ascontiguousarray(bp.reshape(1, C))
    wr = np.ascontiguousarray(
        Wr.reshape(CC, 128, 8).transpose(1, 0, 2).reshape(128, 128))
    wsg = np.ascontiguousarray(Wsg.reshape(CC, 128).T)
    wg_r = Wg.reshape(8, CC, 128, FC, 128).transpose(0, 3, 2, 1, 4)
    sg_r = Sg.reshape(CC, 128, 4, FC, 128).transpose(2, 3, 1, 0, 4)
    wg_all = np.ascontiguousarray(
        np.concatenate([wg_r, sg_r], axis=0).reshape(NE * FC, 128, CC * 128))
    wu_r = Wu.reshape(8, CC, 128, FC, 128).transpose(0, 3, 2, 1, 4)
    su_r = Su.reshape(CC, 128, 4, FC, 128).transpose(2, 3, 1, 0, 4)
    wu_all = np.ascontiguousarray(
        np.concatenate([wu_r, su_r], axis=0).reshape(NE * FC, 128, CC * 128))
    wd_r = Wd.reshape(8, FC, 128, 4, 512).transpose(0, 3, 2, 1, 4)
    sd_r = Sd.reshape(4, FC, 128, 4, 512).transpose(0, 3, 2, 1, 4)
    wd_all = np.ascontiguousarray(
        np.concatenate([wd_r, sd_r], axis=0).reshape(NE * 4, 128, FC * 512))
    ident = np.eye(128, dtype=bf)

    x = np.asarray(inputs["L0_attn_x"], dtype=np.float32)
    in_maps = []
    for core in range(8):
        b, h = core // 2, core % 2
        xb = x[b]
        own = np.ascontiguousarray(
            xb.reshape(CC, 128, C)[:, 64 * h:64 * h + 64, :].reshape(TOWN, C))
        dmask_np = ((np.arange(64)[None, :] + 64 * h)
                    >= np.arange(128)[:, None]).astype(bf)
        in_maps.append(dict(
            x_kv=np.ascontiguousarray(xb),
            x_own=own,
            dmask=np.ascontiguousarray(dmask_np),
            ident=ident,
            wqk=wqk, wv4=wv4, battn=battn, wp=wp, bp_row=bp_row,
            wr=wr, wsg=wsg, wg=wg_all, wu=wu_all, wd=wd_all,
        ))
    return in_maps


def kernel(**inputs):
    in_maps = prep_inputs(inputs)
    nc = build_nc()
    res = run_bass_kernel_spmd(nc, in_maps, core_ids=list(range(8)))
    out = np.empty((4, 2048, C), dtype=np.float32)
    for core in range(8):
        b, h = core // 2, core % 2
        yc = np.asarray(res.results[core]["y"]).reshape(CC, 64, C)
        out[b].reshape(CC, 128, C)[:, 64 * h:64 * h + 64, :] = yc
    return out


# version 19
# speedup vs baseline: 1.2041x; 1.0020x over previous
import numpy as np
import ml_dtypes
from contextlib import ExitStack

import concourse.bass as bass
import concourse.mybir as mybir
from concourse.bass_utils import run_bass_kernel_spmd

F32 = mybir.dt.float32
BF16 = mybir.dt.bfloat16
AF = mybir.ActivationFunctionType
ALU = mybir.AluOpType
AX = mybir.AxisListType

C = 2048
CC = 16
NH = 16
TKV = 2048
TOWN = 1024
NE = 12
FC = 11
ISQ = float(1.0 / np.sqrt(128.0))
LN_EPS = 1e-5

DEBUG_TAPS = ()

ENGN = ("tensor", "vector", "scalar", "gpsimd", "sync")
SEMS = ("pe", "ve", "se", "po")


class Emit:

    def __init__(self):
        self.prog = {e: [] for e in ENGN}
        self.count = {s: 0 for s in SEMS}
        self.lastw = {e: {} for e in ENGN}

    def getsem(self, s):
        if s not in self.count:
            self.count[s] = 0

    def wait(self, eng, tok):
        if tok is None:
            return
        if isinstance(tok, (list, tuple)) and tok and isinstance(tok[0], tuple):
            for t in tok:
                self.wait(eng, t)
            return
        sem, v = tok
        assert v <= self.count[sem], f"wait on future token {sem}:{v}"
        if self.lastw[eng].get(sem, 0) >= v:
            return
        self.lastw[eng][sem] = v
        self.prog[eng].append(("w", sem, v))

    _CHAIN = {"vector": "ve", "scalar": "se", "gpsimd": "po"}

    def op(self, eng, fn, inc=None, inc_n=1, waits=(), chain=True):
        ch = self._CHAIN.get(eng)
        if chain and ch is not None and inc == ch and self.count[ch] > 0:
            self.wait(eng, (ch, self.count[ch]))
        for t in waits:
            self.wait(eng, t)
        tok = None
        if inc is not None:
            self.count[inc] += inc_n
            tok = (inc, self.count[inc])
        self.prog[eng].append(("i", fn, inc, inc_n))
        return tok

    def snapshot(self):
        return [(s, self.count[s]) for s in ("pe", "ve", "se", "po")
                if self.count[s] > 0]


def build_nc():
    nc = bass.Bass("TRN2", target_bir_lowering=False)

    d_xkv = nc.dram_tensor("x_kv", [TKV, C], F32, kind="ExternalInput").ap()
    d_xown = nc.dram_tensor("x_own", [TOWN, C], F32, kind="ExternalInput").ap()
    d_dmask = nc.dram_tensor("dmask", [128, 64], BF16, kind="ExternalInput").ap()
    d_ident = nc.dram_tensor("ident", [128, 128], BF16, kind="ExternalInput").ap()
    d_wqk = nc.dram_tensor("wqk", [NH, 128, CC * 256], F32, kind="ExternalInput").ap()
    d_wv4 = nc.dram_tensor("wv4", [4, 128, CC * 512], F32, kind="ExternalInput").ap()
    d_battn = nc.dram_tensor("battn", [128, 48], F32, kind="ExternalInput").ap()
    d_wp = nc.dram_tensor("wp", [4, 128, CC * 512], F32, kind="ExternalInput").ap()
    d_bp = nc.dram_tensor("bp_row", [1, C], F32, kind="ExternalInput").ap()
    d_wr = nc.dram_tensor("wr", [128, 128], F32, kind="ExternalInput").ap()
    d_wsg = nc.dram_tensor("wsg", [128, 16], F32, kind="ExternalInput").ap()
    d_wg = nc.dram_tensor("wg", [NE * FC, 128, CC * 128], F32, kind="ExternalInput").ap()
    d_wu = nc.dram_tensor("wu", [NE * FC, 128, CC * 128], F32, kind="ExternalInput").ap()
    d_wd = nc.dram_tensor("wd", [NE * 4, 128, FC * 512], F32, kind="ExternalInput").ap()
    d_y = nc.dram_tensor("y", [TOWN, C], F32, kind="ExternalOutput").ap()

    def row3(ap3, i):
        return ap3[i:i + 1, :, :].rearrange("o p f -> p (o f)")

    em = Emit()

    def mm(out, lhsT, rhs, start, stop, inc=None, waits=()):
        return em.op(
            "tensor",
            lambda E, o=out, l=lhsT, r=rhs, st=start, sp=stop: E.matmul(
                o, l, r, start=st, stop=sp),
            inc=inc, waits=waits)

    def pe_tr(out, in_, ident, inc=None, waits=()):
        return em.op(
            "tensor",
            lambda E, o=out, i=in_, d=ident: E.transpose(o, i, d),
            inc=inc, waits=waits)

    def act(out, in_, func, bias=0.0, scale=1.0, accum_out=None, waits=(),
            chain=True):
        return em.op(
            "scalar",
            lambda E, o=out, i=in_, f=func, b=bias, s=scale, a=accum_out:
            E.activation(o, i, f, bias=b, scale=s, accum_out=a),
            inc="se", waits=waits, chain=chain)

    def vec(fn, waits=(), chain=True):
        return em.op("vector", fn, inc="ve", waits=waits, chain=chain)

    def pool(fn, waits=()):
        return em.op("gpsimd", fn, inc="po", waits=waits)

    def _dma(eng, sem, out, in_, waits):
        em.getsem(sem)
        pre = em.count[sem]
        if pre > 0:
            waits = list(waits) + [(sem, pre)]
        return em.op(eng,
                     lambda E, o=out, i=in_: E.dma_start(out=o, in_=i),
                     inc=sem, inc_n=16, waits=waits)

    def dma_w(out, in_, sem, waits=()):
        return _dma("gpsimd", sem, out, in_, waits)

    def dma_x(out, in_, sem, waits=()):
        return _dma("sync", sem, out, in_, waits)

    def dma_o(out, in_, waits=()):
        return _dma("sync", "dO", out, in_, waits)

    S = ExitStack()
    sA = ExitStack()
    sL = ExitStack()
    sA2 = ExitStack()
    sP = ExitStack()
    sPp = ExitStack()
    sF = ExitStack()

    id_bf = S.enter_context(nc.sbuf_tensor("id_bf", [128, 128], BF16))
    dmask = S.enter_context(nc.sbuf_tensor("dmask_sb", [128, 64], BF16))
    ones_bf = S.enter_context(nc.sbuf_tensor("ones_bf", [128, 1], BF16))
    ones1f = S.enter_context(nc.sbuf_tensor("ones1f", [1, 128], F32))
    ones1b = S.enter_context(nc.sbuf_tensor("ones1b", [1, 128], BF16))
    bv_bf = S.enter_context(nc.sbuf_tensor("bv_bf", [128, 16], BF16))
    battn = S.enter_context(nc.sbuf_tensor("battn_sb", [128, 48], F32))
    wr_f = S.enter_context(nc.sbuf_tensor("wr_f", [128, 128], F32))
    wr_bf = S.enter_context(nc.sbuf_tensor("wr_bf", [128, 128], BF16))
    wsg_bf = S.enter_context(nc.sbuf_tensor("wsg_bf", [128, 16], BF16))
    stats = S.enter_context(nc.sbuf_tensor("stats", [128, 48], F32))
    bnst = S.enter_context(nc.sbuf_tensor("bnst", [128, 96], F32))
    C_tm = S.enter_context(nc.sbuf_tensor("C_tm", [128, 64], F32))
    s_tm = S.enter_context(nc.sbuf_tensor("s_tm", [128, 8], F32))
    yallT = S.enter_context(nc.sbuf_tensor("yallT", [128, NH * TOWN], BF16))
    wv4_b = S.enter_context(nc.sbuf_tensor("wv4_b", [128, CC * 512], BF16))

    pA0 = S.enter_context(nc.psum_tensor("pA0", [128, 512], F32))
    pA1 = S.enter_context(nc.psum_tensor("pA1", [128, 512], F32))
    pS = S.enter_context(nc.psum_tensor("pS", [128, 512], F32))
    pA = (pA0, pA1)

    x1T = sA.enter_context(nc.sbuf_tensor("x1T", [128, CC * TKV], BF16))
    xqT = sA.enter_context(nc.sbuf_tensor("xqT", [128, CC * TOWN], BF16))
    wqk_b = sA.enter_context(nc.sbuf_tensor("wqk_b", [128, CC * 256], BF16))
    K_sb = sA.enter_context(nc.sbuf_tensor("K_sb", [128, TKV], BF16))
    Q_sb = sA.enter_context(nc.sbuf_tensor("Q_sb", [128, TOWN], BF16))
    V4_sb = sA.enter_context(nc.sbuf_tensor("V4_sb", [128, 4 * TKV], BF16))
    pA2 = sA.enter_context(nc.psum_tensor("pA2", [128, 512], F32))
    pYa = [sA.enter_context(nc.psum_tensor(f"pY{i}", [128, 512], F32))
           for i in range(2)]


    pT0 = sL.enter_context(nc.psum_tensor("pT0", [128, 1024], BF16))
    pT1 = sL.enter_context(nc.psum_tensor("pT1", [128, 1024], BF16))
    pT = (pT0, pT1)
    lnx = [sL.enter_context(nc.sbuf_tensor(f"lnx{i}", [128, C], F32)) for i in range(2)]
    lntm = [sL.enter_context(nc.sbuf_tensor(f"lntm{i}", [128, C], BF16)) for i in range(2)]

    x1T_v = x1T.ap().rearrange("p (cc t) -> p cc t", cc=CC)
    xqT_v = xqT.ap().rearrange("p (cc t) -> p cc t", cc=CC)

    dma_x(id_bf[:, :], d_ident[:, :], "dC")
    dma_x(dmask[:, :], d_dmask[:, :], "dC")
    dma_x(battn[:, :], d_battn[:, :], "dC")
    dma_x(wr_f[:, :], d_wr[:, :], "dC")
    t_id = t_dm = t_ba = t_wr = ("dC", 64)
    t_ones = vec(lambda E: E.memset(ones_bf[:, :], 1.0))
    t_ones1 = vec(lambda E: E.memset(ones1f[:, :], 1.0))
    t_ones1b = vec(lambda E: E.memset(ones1b[:, :], 1.0))
    t_wrbf = vec(lambda E: E.tensor_copy(out=wr_bf[:, :], in_=wr_f[:, :]),
                 waits=[t_wr])
    t_bvbf = vec(lambda E: E.tensor_copy(out=bv_bf[:, :], in_=battn[:, 32:48]),
                 waits=[t_ba])
    t_wsg = dma_w(wsg_bf[:, :], d_wsg[:, :], "dC2")
    t_wqk = dma_w(wqk_b[:, :], d_wqk[0], "dQK")
    t_wv4 = dma_w(wv4_b[:, :], d_wv4[0], "dV4")

    ln_cons = [[[] for _ in range(4)] for _ in range(2)]
    tm_cons = [None] * 2
    pT_rd = [None, None]
    x1T_tok = [None] * CC
    xqT_tok = [None] * 8

    def ln_stats_norm(b, src_f32_ap, dst_bf_ap, extra_norm_waits=(),
                      dma_toks=None):
        st = stats.ap()
        col = lambda k: st[:, b * 16 + k: b * 16 + k + 1]
        var_c, t1_c, r_c, nmr_c = [col(k) for k in range(4, 8)]
        nm_c = col(3)
        bn = bnst.ap()[:, b * 32: b * 32 + 24].rearrange("p (c k) -> p c k", c=4)
        mv = bnst.ap()[:, b * 32 + 24: b * 32 + 26]
        for ch in range(4):
            t_bn = vec(lambda E, o=bn[:, ch, :],
                       i=src_f32_ap.rearrange("p (c n) -> p c n", c=4)[:, ch, :]:
                       E.bn_stats(o, i), chain=(ch == 0),
                       waits=([dma_toks[ch]] if dma_toks else []))
        t_ag = vec(lambda E, o=mv, i=bnst.ap()[:, b * 32: b * 32 + 24]:
                   E.bn_aggr(o, i), waits=[t_bn])
        vec(lambda E, o=var_c, i=mv: E.tensor_scalar_add(o, i[:, 1:2], LN_EPS),
            waits=[t_ag])
        tok_nm_ = vec(lambda E, o=nm_c, i=mv: E.tensor_scalar_mul(o, i[:, 0:1],
                                                                  -1.0))
        t_l1 = act(t1_c, var_c, AF.Ln, waits=[tok_nm_])
        act(r_c, t1_c, AF.Exp, scale=-0.5, waits=[t_l1])
        em.op("scalar", lambda E, o=nmr_c, i=nm_c, r=r_c: E.mul(o, i, r),
              inc="se")
        toks = []
        for ch in range(4):
            toks.append(act(dst_bf_ap[:, ch * 512:(ch + 1) * 512],
                            src_f32_ap[:, ch * 512:(ch + 1) * 512],
                            AF.Identity, bias=nmr_c, scale=r_c,
                            waits=list(extra_norm_waits), chain=(ch == 0)))
        return toks, None

    pend = [None]

    def ln_transpose_flush(idx):
        b, tok_nm, destT_v, dest_tok_list, dest_i = pend[0]
        last_ev = None
        for g in range(4):
            pb = (idx * 4 + g) % 2
            for j in range(4):
                cc = 4 * g + j
                tk = pe_tr(pT[pb][:, j * 128:(j + 1) * 128],
                           lntm[b][:, cc * 128:(cc + 1) * 128], id_bf[:, :],
                           inc="pe" if j == 3 else None,
                           waits=([tok_nm[g], t_id, pT_rd[pb]]
                                  if j == 0 else []))
            dv = destT_v[:, 4 * g:4 * g + 4, dest_i * 128:(dest_i + 1) * 128]
            sv = pT[pb].ap()[:, 0:512].rearrange("p (c n) -> p c n", c=4)
            last_ev = vec(lambda E, o=dv, i=sv: E.tensor_copy(out=o, in_=i),
                          waits=[tk], chain=(g == 0))
            pT_rd[pb] = last_ev
        tm_cons[b] = last_ev
        dest_tok_list[dest_i] = last_ev
        pend[0] = None

    def ln_tile(idx, src_ap, destT_v, dest_tok_list, dest_i):
        b = idx % 2
        tok_d = [dma_x(lnx[b][:, ch * 512:(ch + 1) * 512],
                       src_ap[:, ch * 512:(ch + 1) * 512], f"dL{b}_{ch}",
                       waits=ln_cons[b][ch])
                 for ch in range(4)]
        tok_nm, _ = ln_stats_norm(b, lnx[b][:, :], lntm[b][:, :],
                                  extra_norm_waits=[tm_cons[b]],
                                  dma_toks=tok_d)
        ln_cons[b] = [[tok_nm[ch]] for ch in range(4)]
        if pend[0] is not None:
            ln_transpose_flush(idx)
        pend[0] = (b, tok_nm, destT_v, dest_tok_list, dest_i)

    pAh = (pA0, pA1, pA2)
    pA_rd = [None, None, None]
    k_ev0 = [None] * 4
    q_ev0 = [None] * 2
    nu0 = [0]
    v4_v0 = V4_sb.ap().rearrange("p (hr t) -> p hr t", hr=4)
    hoist = {}

    def hoist_after_tile(t):
        if t < 0:
            return
        if t == 23:
            v_kts = [13, 14, 15]
        elif t >= 10:
            v_kts = [t - 10]
        else:
            v_kts = []
        for kt in v_kts:
            pb = nu0[0] % 3
            nu0[0] += 1
            for cc in range(CC):
                tkv = mm(pAh[pb][:, :], x1T_v[:, cc, kt * 128:(kt + 1) * 128],
                         wv4_b[:, cc * 512:(cc + 1) * 512],
                         start=(cc == 0), stop=(cc == CC - 1),
                         inc="pe" if cc == CC - 1 else None,
                         waits=([t_wv4, pA_rd[pb], x1T_tok[kt]]
                                if cc == 0 else []))
            dv = v4_v0[:, :, kt * 128:(kt + 1) * 128]
            sv = pAh[pb].ap().rearrange("p (c n) -> p c n", c=4)
            ev = vec(lambda E, o=dv, i=sv: E.tensor_copy(out=o, in_=i),
                     waits=[tkv], chain=False)
            pA_rd[pb] = ev
            if kt == CC - 1:
                hoist["wv41"] = dma_w(wv4_b[:, :], d_wv4[1], "dV4",
                                      waits=[tkv])
        if t in (3, 7, 11, 15):
            u = (t - 3) // 4
            pb = nu0[0] % 3
            nu0[0] += 1
            for cc in range(CC):
                tkk = mm(pAh[pb][:, :],
                         wqk_b[:, cc * 256 + 128: cc * 256 + 256],
                         x1T_v[:, cc, u * 512:(u + 1) * 512],
                         start=(cc == 0), stop=(cc == CC - 1),
                         inc="pe" if cc == CC - 1 else None,
                         waits=([t_wqk, pA_rd[pb], x1T_tok[4 * u + 3]]
                                if cc == 0 else []))
            k_ev0[u] = vec(lambda E, o=K_sb[:, u * 512:(u + 1) * 512],
                           i=pAh[pb], b=battn[:, 16:17]:
                           E.tensor_scalar_add(o, i[:, :], b),
                           waits=[tkk, t_ba], chain=False)
            pA_rd[pb] = k_ev0[u]
        if t in (19, 23):
            u = (t - 19) // 4
            pb = nu0[0] % 3
            nu0[0] += 1
            for cc in range(CC):
                tkq = mm(pAh[pb][:, :],
                         wqk_b[:, cc * 256: cc * 256 + 128],
                         xqT_v[:, cc, u * 512:(u + 1) * 512],
                         start=(cc == 0), stop=(cc == CC - 1),
                         inc="pe" if cc == CC - 1 else None,
                         waits=([t_wqk, pA_rd[pb], xqT_tok[4 * u + 3]]
                                if cc == 0 else []))
            q_ev0[u] = vec(lambda E, o=Q_sb[:, u * 512:(u + 1) * 512],
                           i=pAh[pb], b=battn[:, 0:1]:
                           E.tensor_scalar_add(o, i[:, :], b),
                           waits=[tkq, t_ba], chain=False)
            pA_rd[pb] = q_ev0[u]
            if t == 23:
                hoist["prod"] = ("pe", em.count["pe"])
                hoist["wqk1"] = dma_w(wqk_b[:, :], d_wqk[1], "dQK",
                                      waits=[tkq])

    for i in range(CC):
        ln_tile(i, d_xkv[i * 128:(i + 1) * 128, :], x1T_v, x1T_tok, i)
        hoist_after_tile(i - 1)
    for i in range(8):
        ln_tile(CC + i, d_xown[i * 128:(i + 1) * 128, :], xqT_v, xqT_tok, i)
        hoist_after_tile(CC + i - 1)
    ln_transpose_flush(CC + 8)
    hoist_after_tile(23)

    ln1_end = em.snapshot()
    if "x1T" in DEBUG_TAPS:
        d_dbg_x1 = nc.dram_tensor("dbg_x1T", [128, CC * TKV], BF16, kind="ExternalOutput").ap()
        d_dbg_xq = nc.dram_tensor("dbg_xqT", [128, CC * TOWN], BF16, kind="ExternalOutput").ap()
        dma_o(d_dbg_x1, x1T[:, :], waits=[x1T_tok[CC - 1]] + ln1_end)
        dma_o(d_dbg_xq, xqT[:, :], waits=[xqT_tok[7]])
    sL.close()
    PT_sb = sA2.enter_context(nc.sbuf_tensor("PT_sb", [128, 16 * 512], BF16))
    dg_scr = sA2.enter_context(nc.sbuf_tensor("dg_scr", [128, 64], BF16))
    ytmp2 = [sA2.enter_context(nc.sbuf_tensor(f"ytmp{i}", [128, 512], F32))
             for i in range(2)]
    linv2 = [sA2.enter_context(nc.sbuf_tensor(f"linv{i}", [128, 512], F32))
             for i in range(2)]
    rrow = sA2.enter_context(nc.sbuf_tensor("rrow", [1, 1024], F32))
    pYb = [sA2.enter_context(nc.psum_tensor(f"pYb{i}", [128, 512], F32))
           for i in range(2)]

    for _eng in ("tensor", "vector", "scalar", "gpsimd"):
        em.wait(_eng, ln1_end)

    wp_tok = [None] * 4
    pYq = [[pYa[0], pYb[0]], [pYa[1], pYb[1]]]
    pS_rd = []
    pS_lnrd = []
    yall_tok = [[None, None] for _ in range(NH)]
    pY_rd = [[None, None], [None, None]]
    PT_rd_tok = [None] * 16
    wqk_tok = t_wqk
    wv4_tok = t_wv4
    ytmp_rd = [None, None]
    linv_rd = [None, None]
    dg_rd = None
    rrow_rd = [None]
    deferred = None
    pool_deferred = None

    for h in range(NH):
        hr = h % 4
        bq = battn[:, h:h + 1]
        bk = battn[:, 16 + h:16 + h + 1]
        bv = battn[:, 32 + h:32 + h + 1]

        if h == 0:
            k_ev = k_ev0
            q_ev = q_ev0
            prod_last = hoist["prod"]
            wqk_tok = hoist["wqk1"]
            wv4_tok = hoist["wv41"]
        else:
            k_ev = [None] * 4
            nu_p = 0
            for u in range(4):
                pb = nu_p % 3
                nu_p += 1
                for cc in range(CC):
                    tk = mm(pAh[pb][:, :],
                            wqk_b[:, cc * 256 + 128: cc * 256 + 256],
                            x1T_v[:, cc, u * 512:(u + 1) * 512],
                            start=(cc == 0), stop=(cc == CC - 1),
                            inc="pe" if cc == CC - 1 else None,
                            waits=([wqk_tok, pA_rd[pb], x1T_tok[4 * u + 3]]
                                   if cc == 0 else []))
                k_ev[u] = vec(lambda E, o=K_sb[:, u * 512:(u + 1) * 512],
                              i=pAh[pb], b=bk:
                              E.tensor_scalar_add(o, i[:, :], b),
                              waits=[tk, t_ba], chain=False)
                pA_rd[pb] = k_ev[u]
            q_ev = [None] * 2
            for u in range(2):
                pb = nu_p % 3
                nu_p += 1
                for cc in range(CC):
                    tk = mm(pAh[pb][:, :],
                            wqk_b[:, cc * 256: cc * 256 + 128],
                            xqT_v[:, cc, u * 512:(u + 1) * 512],
                            start=(cc == 0), stop=(cc == CC - 1),
                            inc="pe" if cc == CC - 1 else None,
                            waits=([wqk_tok, pA_rd[pb], xqT_tok[4 * u + 3]]
                                   if cc == 0 else []))
                q_ev[u] = vec(lambda E, o=Q_sb[:, u * 512:(u + 1) * 512],
                              i=pAh[pb], b=bq:
                              E.tensor_scalar_add(o, i[:, :], b),
                              waits=[tk, t_ba], chain=False)
                pA_rd[pb] = q_ev[u]
            wqk_mm_done = tk
            prod_last = ("pe", em.count["pe"])
            if h + 1 < NH:
                wqk_tok = dma_w(wqk_b[:, :], d_wqk[h + 1], "dQK",
                                waits=[wqk_mm_done])

            if hr == 0:
                v4_v = V4_sb.ap().rearrange("p (hr t) -> p hr t", hr=4)
                for u in range(4):
                    for j in range(4):
                        kt = 4 * u + j
                        pb = nu_p % 3
                        nu_p += 1
                        for cc in range(CC):
                            tk = mm(pAh[pb][:, :],
                                    x1T_v[:, cc, kt * 128:(kt + 1) * 128],
                                    wv4_b[:, cc * 512:(cc + 1) * 512],
                                    start=(cc == 0), stop=(cc == CC - 1),
                                    inc="pe" if cc == CC - 1 else None,
                                    waits=([wv4_tok, pA_rd[pb], x1T_tok[kt]]
                                           if cc == 0 else []))
                        dv = v4_v[:, :, kt * 128:(kt + 1) * 128]
                        sv = pAh[pb].ap().rearrange("p (c n) -> p c n", c=4)
                        ev = vec(lambda E, o=dv, i=sv:
                                 E.tensor_copy(out=o, in_=i),
                                 waits=[tk], chain=False)
                        pA_rd[pb] = ev
                if h + 4 < NH:
                    wv4_tok = dma_w(wv4_b[:, :], d_wv4[h // 4 + 1], "dV4",
                                    waits=[tk])
                else:
                    wp_tok[0] = [dma_w(wv4_b[:, 2048 * j: 2048 * (j + 1)],
                                       d_wp[0][:, 2048 * j: 2048 * (j + 1)],
                                       f"dPf0_{j}",
                                       waits=([tk] if j == 0 else []))
                                 for j in range(4)]

        if deferred is not None:
            deferred()
            deferred = None

        ldone_q = [None, None]
        for qh in range(2):
            exp_tok = [None] * 16
            kcs = list(range(8 * qh + 8))
            lrow = pS[32 * qh: 32 * qh + 1, :]

            def emit_st(kc):
                pb = (kc + 2) % 3
                voff = max(0, kc - 8 * qh) * 64
                N = 512 - voff
                mm(pAh[pb][:, 0:N],
                   K_sb[:, kc * 128:(kc + 1) * 128],
                   Q_sb[:, qh * 512 + voff: (qh + 1) * 512],
                   start=True, stop=True, inc="pe",
                   waits=[k_ev[kc // 4], q_ev[qh], pA_rd[pb]])
                st_tok[kc] = ("pe", em.count["pe"])

            st_tok = [None] * 16
            LOOK = 3
            for kc in kcs[:LOOK]:
                emit_st(kc)
            for kc in kcs:
                pb = (kc + 2) % 3
                voff = max(0, kc - 8 * qh) * 64
                N = 512 - voff
                diag = kc >= 8 * qh
                msk = None
                if diag:
                    tdg = act(dg_scr[:, :], pAh[pb][:, 0:64], AF.Exp, scale=ISQ,
                              waits=[st_tok[kc], dg_rd])
                    msk = pool(lambda E,
                               o=PT_sb[:, kc * 512: kc * 512 + 64],
                               a=dg_scr, m=dmask:
                               E.tensor_mul(o, a[:, :], m[:, :]),
                               waits=[tdg, t_dm, PT_rd_tok[kc]])
                    dg_rd = msk
                    if N > 64:
                        exp_tok[kc] = act(PT_sb[:, kc * 512 + 64: kc * 512 + N],
                                          pAh[pb][:, 64:N], AF.Exp, scale=ISQ,
                                          waits=[PT_rd_tok[kc]])
                    else:
                        exp_tok[kc] = tdg
                else:
                    exp_tok[kc] = act(PT_sb[:, kc * 512: kc * 512 + N],
                                      pAh[pb][:, 0:N], AF.Exp, scale=ISQ,
                                      waits=[st_tok[kc], PT_rd_tok[kc]])
                pA_rd[pb] = exp_tok[kc]
                wl = [exp_tok[kc], t_ones]
                if msk is not None:
                    wl.append(msk)
                if kc == 0:
                    wl += pS_rd + [pY_rd[qh][h % 2]]
                mm(lrow[:, voff:512], ones_bf[:, :],
                   PT_sb[:, kc * 512: kc * 512 + N],
                   start=(kc == 0), stop=(kc == kcs[-1]), waits=wl)
                tv = mm(pYq[qh][h % 2][:, voff:512],
                        V4_sb[:, hr * TKV + kc * 128: hr * TKV + (kc + 1) * 128],
                        PT_sb[:, kc * 512: kc * 512 + N],
                        start=(kc == 0), stop=(kc == kcs[-1]), inc="pe")
                PT_rd_tok[kc] = tv
                if kc + LOOK in kcs:
                    emit_st(kc + LOOK)
            ldone_q[qh] = ("pe", em.count["pe"])
            if h == NH - 1:
                make_deferred(h=h, hr=hr, ldone=tuple(ldone_q), qhs=(qh,))()

        def make_deferred(h=h, hr=hr, ldone=tuple(ldone_q), qhs=(0, 1)):
            def run():
                nonlocal pS_rd, pS_lnrd
                toks = [] if 0 in qhs else list(pS_rd)
                ln_toks = [] if 0 in qhs else list(pS_lnrd)
                for qh in qhs:
                    lrow = pS[32 * qh: 32 * qh + 1, :]
                    t_ln = act(rrow[0:1, 0:512], lrow, AF.Ln,
                               waits=[ldone[qh], rrow_rd[0]])
                    ln_toks.append(t_ln)
                    t_li = act(rrow[0:1, 512:1024],
                               rrow[0:1, 0:512], AF.Exp, scale=-1.0,
                               waits=[t_ln])
                    toks.append(t_li)
                    t_bc = mm(pAh[qh][:, :], ones1f[0:1, :],
                              rrow[0:1, 512:1024],
                              start=True, stop=True, inc="pe",
                              waits=[t_li, t_ones1, pA_rd[qh]])
                    rrow_rd[0] = t_bc
                    t_lc = vec(lambda E, o=linv2[qh], i=pAh[qh]:
                               E.tensor_copy(out=o[:, :], in_=i[:, :]),
                               waits=[t_bc, linv_rd[qh]])
                    pA_rd[qh] = t_lc
                    dst = yallT[:, h * TOWN + qh * 512: h * TOWN + (qh + 1) * 512]
                    t_ym = vec(lambda E, o=dst, i=pYq[qh][h % 2],
                               l=linv2[qh]:
                               E.tensor_mul(o, i[:, :], l[:, :]),
                               waits=[t_lc, ldone[qh]])
                    pY_rd[qh][h % 2] = t_ym
                    linv_rd[qh] = t_ym
                    yall_tok[h][qh] = t_ym
                pS_rd = toks
                pS_lnrd = ln_toks
            return run

        if h < NH - 1:
            deferred = make_deferred()

        if h == 13:
            wp1_j2 = dma_w(V4_sb[:, 1024:3072], d_wp[1][:, 4096:6144],
                           "dP1_2", waits=[("pe", em.count["pe"])])
        if h == 14:
            wp1_j3 = dma_w(V4_sb[:, 3072:5120], d_wp[1][:, 6144:8192],
                           "dP1_3", waits=[("pe", em.count["pe"])])

    prod15 = prod_last

    sA2.close()
    sA.close()
    heads_end = em.snapshot()
    if "yallT" in DEBUG_TAPS:
        d_dbg_ya = nc.dram_tensor("dbg_yallT", [128, NH * TOWN], BF16, kind="ExternalOutput").ap()
        dma_o(d_dbg_ya, yallT[:, :], waits=heads_end)

    acc = S.enter_context(nc.sbuf_tensor("acc", [128, 8 * C], F32))
    x2T = S.enter_context(nc.sbuf_tensor("x2T", [128, CC * TOWN], BF16))
    x2T_v = x2T.ap().rearrange("p (cc t) -> p cc t", cc=CC)
    wg_b = [sF.enter_context(nc.sbuf_tensor("wg0", [128, CC * 128], BF16)),
            None]
    wu_b = [sF.enter_context(nc.sbuf_tensor("wu0", [128, CC * 128], BF16)),
            None]
    wp_b = [sP.enter_context(nc.sbuf_tensor(f"wp{i}", [128, CC * 512], BF16))
            for i in range(2)]
    acc_init = [None] * 8
    for tt in range(8):
        acc_init[tt] = dma_x(acc[:, tt * C:(tt + 1) * C],
                             d_xown[tt * 128:(tt + 1) * 128, :], f"dA{tt}",
                             waits=[prod15] if tt == 0 else [])
    wg_tok, wu_tok = {}, {}
    wg_rd = [None, None]
    wu_rd = [None, None]
    wg_tok[(0, 0)] = dma_w(wg_b[0][:, :], d_wg[0], "dG0", waits=[prod15])
    wu_tok[(0, 0)] = dma_w(wu_b[0][:, :], d_wu[0], "dU0")
    wpbufs = [wv4_b, wp_b[0], wp_b[1], wv4_b]
    wp_tok[1] = [dma_w(wp_b[0][:, 2048 * j: 2048 * (j + 1)],
                       d_wp[1][:, 2048 * j: 2048 * (j + 1)], f"dP1_{j}",
                       waits=(heads_end if j == 0 else []))
                 for j in range(2)] + [wp1_j2, wp1_j3]
    wp_tok[2] = [dma_w(wp_b[1][:, 2048 * j: 2048 * (j + 1)],
                       d_wp[2][:, 2048 * j: 2048 * (j + 1)], f"dP2_{j}",
                       waits=(heads_end if j == 0 else []))
                 for j in range(4)]
    bp_row = sP.enter_context(nc.sbuf_tensor("bp_sb", [1, C], BF16))
    bvp_sb = sP.enter_context(nc.sbuf_tensor("bvp_sb", [1, 2048], BF16))
    x2tm2 = [sP.enter_context(nc.sbuf_tensor(f"x2tm{i}", [128, C], BF16))
             for i in range(2)]
    lg_sb = sP.enter_context(nc.sbuf_tensor("lg_sb", [128, 8], F32))
    ex_sb = sP.enter_context(nc.sbuf_tensor("ex_sb", [128, 8], F32))
    ge_sb = sP.enter_context(nc.sbuf_tensor("ge_sb", [128, 8], F32))
    pT0b = sPp.enter_context(nc.psum_tensor("pT0b", [128, 1024], BF16))
    pT1b = sPp.enter_context(nc.psum_tensor("pT1b", [128, 1024], BF16))
    pTb = (pT0b, pT1b)

    t_bp = dma_w(bp_row[0:1, :], d_bp[0:1, :], "dBp", waits=heads_end)

    pT_rd2 = [None, None]
    x2_done = [None] * 8
    C_done = [None] * 8
    S_done = [None] * 8
    x2tm_rd = [None, None]
    pS_rd2 = [t for t in pS_rd]
    lg_rd = [None]
    ex_rd = [None]
    first_ln2 = [True]

    ln2_bf = {}

    def ln2_part1(tt, ready_tok):
        b = tt % 2
        src = acc[:, tt * C:(tt + 1) * C]
        if first_ln2[0]:
            em.wait("vector", heads_end)
            first_ln2[0] = False
        em.wait("vector", ready_tok)
        x2tm = x2tm2[tt % 2]
        tok_bf, _ = ln_stats_norm(b, src, x2tm[:, :],
                                  extra_norm_waits=[x2tm_rd[b]])
        ln2_bf[tt] = tok_bf

    def ln2_router(tt):
        b = tt % 2
        st = stats.ap()
        col = lambda k: st[:, b * 16 + k: b * 16 + k + 1]
        mx1, nmx, den, mx2, rde, s1, s2 = [col(k) for k in range(8, 15)]
        x2tm = x2tm2[tt % 2]
        tok_bf = ln2_bf[tt]
        last_ev = None
        for g in range(4):
            pb = (tt * 4 + g) % 2
            for j in range(4):
                cc = 4 * g + j
                tk = pe_tr(pTb[pb][:, j * 128:(j + 1) * 128],
                           x2tm[:, cc * 128:(cc + 1) * 128], id_bf[:, :],
                           inc="pe" if j == 3 else None,
                           waits=([tok_bf[g], pT_rd2[pb]] if j == 0 else []))
                _ = 0
            dv = x2T_v[:, 4 * g:4 * g + 4, tt * 128:(tt + 1) * 128]
            sv = pTb[pb].ap()[:, 0:512].rearrange("p (c n) -> p c n", c=4)
            last_ev = vec(lambda E, o=dv, i=sv: E.tensor_copy(out=o, in_=i),
                          waits=[tk])
            pT_rd2[pb] = last_ev
        x2tm_rd[b] = last_ev
        x2_done[tt] = last_ev
        for cc in range(CC):
            mm(pS[:, 0:8], x2T_v[:, cc, tt * 128:(tt + 1) * 128],
               wr_bf[:, cc * 8:(cc + 1) * 8],
               start=(cc == 0), stop=(cc == CC - 1),
               waits=([last_ev, t_wrbf] + pS_rd2) if cc == 0 else [])
        for cc in range(CC):
            tk = mm(pS[:, 8:9], x2T_v[:, cc, tt * 128:(tt + 1) * 128],
                    wsg_bf[:, cc:cc + 1],
                    start=(cc == 0), stop=(cc == CC - 1),
                    inc="pe" if cc == CC - 1 else None,
                    waits=[t_wsg] if cc == 0 else [])
        t_s1 = act(s1, pS[:, 8:9], AF.Exp, scale=-1.0, waits=[tk])
        t_s1b = vec(lambda E, o=s2, i=s1: E.tensor_scalar_add(o, i, 1.0),
                    waits=[t_s1])
        t_sln = act(s2, s2, AF.Ln, waits=[t_s1b])
        t_sg = act(s_tm.ap()[:, tt:tt + 1], s2, AF.Exp, scale=-1.0, waits=[t_sln])
        t_lg = vec(lambda E, o=lg_sb, i=pS: E.tensor_copy(out=o[:, :], in_=i[:, 0:8]),
                   waits=[tk, lg_rd[0]])
        pS_rd2[:] = [t_lg, t_s1]
        t_mx1 = vec(lambda E, o=mx1, i=lg_sb: E.reduce_max(o, i[:, :], axis=AX.X),
                    waits=[t_lg])
        t_nmx = vec(lambda E, o=nmx, i=mx1: E.tensor_scalar_mul(o, i, -1.0),
                    waits=[t_mx1])
        t_ex = act(ex_sb[:, :], lg_sb[:, :], AF.Exp, bias=nmx, accum_out=den,
                   waits=[t_nmx, ex_rd[0]])
        t_ge = vec(lambda E, o=ge_sb, i=lg_sb, m=mx1: E.tensor_scalar(
            out=o[:, :], in0=i[:, :], scalar1=m, scalar2=None, op0=ALU.is_ge),
            waits=[t_ex, t_mx1])
        t_lg2 = vec(lambda E, o=ge_sb, i=ge_sb, l=lg_sb: E.scalar_tensor_tensor(
            out=o[:, :], in0=i[:, :], scalar=-1e30, in1=l[:, :],
            op0=ALU.mult, op1=ALU.add), waits=[t_ge])
        t_mx2 = vec(lambda E, o=mx2, i=ge_sb: E.reduce_max(o, i[:, :], axis=AX.X),
                    waits=[t_lg2])
        t_sel = vec(lambda E, o=ge_sb, i=lg_sb, m=mx2: E.tensor_scalar(
            out=o[:, :], in0=i[:, :], scalar1=m, scalar2=None, op0=ALU.is_ge),
            waits=[t_mx2])
        lg_rd[0] = t_ex
        t_num = vec(lambda E, o=ex_sb, a=ex_sb, s=ge_sb:
                    E.tensor_mul(o[:, :], a[:, :], s[:, :]), waits=[t_sel, t_ex])
        t_rd = vec(lambda E, o=rde, i=den: E.reciprocal(o, i), waits=[t_ex])
        t_C = vec(lambda E, o=C_tm.ap()[:, tt * 8:(tt + 1) * 8], i=ex_sb, r=rde:
                  E.tensor_scalar_mul(o, i[:, :], r), waits=[t_num, t_rd])
        ex_rd[0] = t_C
        return t_C, t_sg


    stt_done = [[None] * 4 for _ in range(8)]
    pA_rd = pA_rd[:2]
    pS_bvp = [[pS_lnrd[0]]]
    t_bvp = {}
    wp_last = [None] * 4

    def emit_bvp(fg):
        wb = wpbufs[fg]
        for cc in range(CC):
            w0 = [wp_tok[fg][cc // 4]]
            if cc == 0:
                w0 += [t_bvbf] + pS_bvp[0]
            tkr = mm(pS[0:1, 0:512], bv_bf[:, cc:cc + 1],
                     wb[:, cc * 512:(cc + 1) * 512],
                     start=(cc == 0), stop=(cc == CC - 1),
                     inc="pe" if cc == CC - 1 else None,
                     waits=w0)
        t_bvp[fg] = vec(lambda E, o=bvp_sb[0:1, fg * 512:(fg + 1) * 512],
                        i=pS[0:1, 0:512],
                        b=bp_row[0:1, fg * 512:(fg + 1) * 512]:
                        E.tensor_add(o, i, b),
                        waits=[tkr, t_bp])
        pS_bvp[0] = [t_bvp[fg]]

    def emit_proj_tile(fg, tt, pb):
        wb = wpbufs[fg]
        for cc in range(CC):
            w0 = [wp_tok[fg][cc // 4], yall_tok[cc][tt // 4]]
            if cc == 0:
                w0.append(pA_rd[pb])
            mm(pA[pb][:, :],
               yallT[:, cc * TOWN + tt * 128:
                     cc * TOWN + (tt + 1) * 128],
               wb[:, cc * 512:(cc + 1) * 512],
               start=(cc == 0), stop=False, waits=w0)
        tk = mm(pA[pb][:, :], ones1b[0:1, :],
                bvp_sb[0:1, fg * 512:(fg + 1) * 512],
                start=False, stop=True, inc="pe",
                waits=[t_bvp[fg], t_ones1b])
        wp_last[fg] = tk
        ev = vec(lambda E,
                 o=acc[:, tt * C + fg * 512: tt * C + (fg + 1) * 512],
                 i=pA[pb]: E.scalar_tensor_tensor(
                     out=o, in0=i[:, :], scalar=1.0, in1=o,
                     op0=ALU.mult, op1=ALU.add),
                 waits=[tk, acc_init[tt]])
        pA_rd[pb] = ev
        stt_done[tt][fg] = ev

    for fg in (0, 1):
        emit_bvp(fg)
        for tt in range(8):
            emit_proj_tile(fg, tt, tt % 2)
        if fg == 0:
            wp_tok[3] = [
                dma_w(wv4_b[:, 2048 * j: 2048 * (j + 1)],
                      d_wp[3][:, 2048 * j: 2048 * (j + 1)], f"dP3_{j}",
                      waits=([wp_last[0]] if j == 0 else []))
                for j in range(4)]
    emit_bvp(2)
    emit_proj_tile(2, 0, 0)
    emit_bvp(3)
    emit_proj_tile(3, 0, 1)
    ln2_part1(0, stt_done[0][3])
    for tt in range(1, 8):
        emit_proj_tile(2, tt, 0)
        emit_proj_tile(3, tt, 1)
        if tt > 1:
            C_done[tt - 2], S_done[tt - 2] = ln2_router(tt - 2)
        ln2_part1(tt, stt_done[tt][3])
    for _t in (6, 7):
        C_done[_t], S_done[_t] = ln2_router(_t)

    acc1_tok = [None] * 8
    if "acc1" in DEBUG_TAPS:
        d_dbg_a1 = nc.dram_tensor("dbg_acc1", [TOWN, C], F32, kind="ExternalOutput").ap()
        for tt in range(8):
            acc1_tok[tt] = dma_o(d_dbg_a1[tt * 128:(tt + 1) * 128, :],
                                 acc[:, tt * C:(tt + 1) * C],
                                 waits=[stt_done[tt][3]])
    if "x2T" in DEBUG_TAPS:
        d_dbg_x2 = nc.dram_tensor("dbg_x2T", [128, CC * TOWN], BF16, kind="ExternalOutput").ap()
        dma_o(d_dbg_x2, x2T[:, :], waits=[x2_done[tt] for tt in range(8)])
    if "Ctm" in DEBUG_TAPS:
        d_dbg_ct = nc.dram_tensor("dbg_Ctm", [128, 64], F32, kind="ExternalOutput").ap()
        d_dbg_st = nc.dram_tensor("dbg_stm", [128, 8], F32, kind="ExternalOutput").ap()
        dma_o(d_dbg_ct, C_tm[:, :], waits=[C_done[tt] for tt in range(8)])
        dma_o(d_dbg_st, s_tm[:, :], waits=[S_done[tt] for tt in range(8)])

    ffn_barrier = em.snapshot()

    sPp.close()
    sP.close()
    wg_b[1] = sF.enter_context(nc.sbuf_tensor("wg1", [128, CC * 128], BF16))
    wu_b[1] = sF.enter_context(nc.sbuf_tensor("wu1", [128, CC * 128], BF16))
    wg_tok[(0, 1)] = dma_w(wg_b[1][:, :], d_wg[1], "dG1", waits=ffn_barrier)
    wu_tok[(0, 1)] = dma_w(wu_b[1][:, :], d_wu[1], "dU1")
    wd_b = [wv4_b,
            sF.enter_context(nc.sbuf_tensor("wd1", [128, FC * 512], BF16))]
    H_sb = sF.enter_context(nc.sbuf_tensor("H_sb", [128, FC * TOWN], BF16))
    sgl = sF.enter_context(nc.sbuf_tensor("sgl", [128, TOWN], F32))
    pG = sF.enter_context(nc.psum_tensor("pG", [128, 1024], F32))
    pU = sF.enter_context(nc.psum_tensor("pU", [128, 1024], F32))

    wd_tok = {}
    wd_rd = [None, None]
    x2_all = list(x2_done)
    pG_rd = None
    pU_rd = None
    sgl_rd = None
    H_wr = [None] * FC
    acc_fin = [None] * 8
    first_ffn = [True]

    nu = 0
    for e in range(NE):
        for fg in (0, 1):
            j = (e * 4 + fg) % 2
            wd_tok[(e, fg)] = dma_w(
                wd_b[j][:, 0:FC * 512], d_wd[e * 4 + fg], f"dD{j}",
                waits=([wd_rd[j]] + (ffn_barrier if (e, fg) == (0, 0) else [])))
        for fc in range(FC):
            j = nu % 2
            w0 = [wg_tok[(e, fc)], pG_rd, x2_done[3]]
            if first_ffn[0]:
                w0 += ffn_barrier
                first_ffn[0] = False
            for half in range(2):
                for cc in range(CC):
                    tkG = mm(pG[:, half * 512:(half + 1) * 512],
                             wg_b[j][:, cc * 128:(cc + 1) * 128],
                             x2T_v[:, cc, half * 512:(half + 1) * 512],
                             start=(cc == 0), stop=(cc == CC - 1),
                             inc="pe" if (half == 1 and cc == CC - 1) else None,
                             waits=(w0 if half == 0 else [x2_done[7]])
                             if cc == 0 else [])
            for half in range(2):
                for cc in range(CC):
                    tkU = mm(pU[:, half * 512:(half + 1) * 512],
                             wu_b[j][:, cc * 128:(cc + 1) * 128],
                             x2T_v[:, cc, half * 512:(half + 1) * 512],
                             start=(cc == 0), stop=(cc == CC - 1),
                             inc="pe" if (half == 1 and cc == CC - 1) else None,
                             waits=[wu_tok[(e, fc)], pU_rd]
                             if (half == 0 and cc == 0) else [])
            ne_, nf_ = (e, fc + 1) if fc + 1 < FC else (e + 1, 0)
            if ne_ < NE and (ne_, nf_) not in wg_tok:
                wg_tok[(ne_, nf_)] = dma_w(wg_b[1 - j][:, :],
                                           d_wg[ne_ * FC + nf_], f"dG{1 - j}",
                                           waits=[wg_rd[1 - j]])
                wu_tok[(ne_, nf_)] = dma_w(wu_b[1 - j][:, :],
                                           d_wu[ne_ * FC + nf_], f"dU{1 - j}",
                                           waits=[wu_rd[1 - j]])
            wg_rd[j] = tkG
            wu_rd[j] = tkU
            t_si = act(sgl[:, :], pG[:, :], AF.Silu, waits=[tkG, sgl_rd])
            pG_rd = t_si
            t_H = vec(lambda E, o=H_sb[:, fc * TOWN:(fc + 1) * TOWN], s=sgl, u=pU:
                      E.tensor_mul(o, s[:, :], u[:, :]),
                      waits=[t_si, tkU])
            pU_rd = t_H
            sgl_rd = t_H
            H_wr[fc] = t_H
            nu += 1
        h_all = list(H_wr)
        for fg in range(4):
            j = (e * 4 + fg) % 2
            for tt in range(8):
                pb = tt % 2
                for kc in range(FC):
                    tk = mm(pA[pb][:, :],
                            H_sb[:, kc * TOWN + tt * 128:
                                 kc * TOWN + (tt + 1) * 128],
                            wd_b[j][:, kc * 512:(kc + 1) * 512],
                            start=(kc == 0), stop=(kc == FC - 1),
                            inc="pe" if kc == FC - 1 else None,
                            waits=([wd_tok[(e, fg)], pA_rd[pb]] + h_all)
                            if kc == 0 else [])
                gate = (C_tm.ap()[:, tt * 8 + e: tt * 8 + e + 1] if e < 8
                        else s_tm.ap()[:, tt:tt + 1])
                gw = [tk, C_done[tt] if e < 8 else S_done[tt]]
                if e == 0:
                    gw.append(stt_done[tt][fg])
                    gw.append(acc1_tok[tt])
                ev = vec(lambda E,
                         o=acc[:, tt * C + fg * 512: tt * C + (fg + 1) * 512],
                         i=pA[pb], g=gate: E.scalar_tensor_tensor(
                             out=o, in0=i[:, :], scalar=g, in1=o,
                             op0=ALU.mult, op1=ALU.add),
                         waits=gw)
                pA_rd[pb] = ev
                if e == NE - 1:
                    acc_fin[tt] = ev
            wd_rd[j] = tk
            if fg < 2:
                j2 = (e * 4 + fg + 2) % 2
                wd_tok[(e, fg + 2)] = dma_w(wd_b[j2][:, 0:FC * 512],
                                            d_wd[e * 4 + fg + 2],
                                            f"dD{j2}", waits=[wd_rd[j2] or tk])

    for tt in range(8):
        em.getsem(f"dO{tt}")
        em.op("sync",
              lambda E, o=d_y[tt * 128:(tt + 1) * 128, :],
              i=acc[:, tt * C:(tt + 1) * C]: E.dma_start(out=o, in_=i),
              inc=f"dO{tt}", inc_n=16, waits=[acc_fin[tt]])
    for tt in range(8):
        em.prog["sync"].append(("w", f"dO{tt}", 16))

    with ExitStack() as semstack:
        semh = {}
        for s in em.count:
            semh[s] = semstack.enter_context(nc.semaphore(f"{s}_s"))
        with nc.Block() as block:

            def replay(E, name):
                for it in em.prog[name]:
                    if it[0] == "w":
                        E.wait_ge(semh[it[1]], it[2])
                    else:
                        _, fn, inc, inc_n = it
                        ins = fn(E)
                        if inc is not None:
                            ins.then_inc(semh[inc], inc_n)

            @block.tensor
            def _(t):
                replay(t, "tensor")

            @block.vector
            def _(v):
                replay(v, "vector")

            @block.scalar
            def _(sc):
                replay(sc, "scalar")

            @block.gpsimd
            def _(g):
                replay(g, "gpsimd")

            @block.sync
            def _(s):
                replay(s, "sync")

    sF.close()
    S.close()
    return nc


def prep_inputs(inputs):
    bf = ml_dtypes.bfloat16
    Wa = np.asarray(inputs["L0_attn_W_attn"], dtype=np.float32)
    ba = np.asarray(inputs["L0_attn_b_attn"], dtype=np.float32)
    Wp = np.asarray(inputs["L0_attn_W_proj"], dtype=np.float32)
    bp = np.asarray(inputs["L0_attn_b_proj"], dtype=np.float32)
    Wr = np.asarray(inputs["L0_ffn_Wr"], dtype=np.float32)
    Wg = np.asarray(inputs["L0_ffn_Wg"], dtype=np.float32)
    Wu = np.asarray(inputs["L0_ffn_Wu"], dtype=np.float32)
    Wd = np.asarray(inputs["L0_ffn_Wd"], dtype=np.float32)
    Sg = np.asarray(inputs["L0_ffn_Sg"], dtype=np.float32)
    Su = np.asarray(inputs["L0_ffn_Su"], dtype=np.float32)
    Sd = np.asarray(inputs["L0_ffn_Sd"], dtype=np.float32)
    Wsg = np.asarray(inputs["L0_ffn_Wsg"], dtype=np.float32)

    w5 = Wa.reshape(CC, 128, 3, NH, 128)
    wqk = np.ascontiguousarray(
        w5[:, :, 0:2].transpose(3, 1, 0, 2, 4).reshape(NH, 128, CC * 256))
    wv4 = np.ascontiguousarray(
        w5[:, :, 2].reshape(CC, 128, 4, 4, 128).transpose(2, 1, 0, 3, 4)
        .reshape(4, 128, CC * 512))
    battn = np.ascontiguousarray(ba.reshape(48, 128).T)
    wp = np.ascontiguousarray(
        Wp.reshape(CC, 128, 4, 512).transpose(2, 1, 0, 3).reshape(4, 128, CC * 512))
    bp_row = np.ascontiguousarray(bp.reshape(1, C))
    wr = np.ascontiguousarray(
        Wr.reshape(CC, 128, 8).transpose(1, 0, 2).reshape(128, 128))
    wsg = np.ascontiguousarray(Wsg.reshape(CC, 128).T)
    wg_r = Wg.reshape(8, CC, 128, FC, 128).transpose(0, 3, 2, 1, 4)
    sg_r = Sg.reshape(CC, 128, 4, FC, 128).transpose(2, 3, 1, 0, 4)
    wg_all = np.ascontiguousarray(
        np.concatenate([wg_r, sg_r], axis=0).reshape(NE * FC, 128, CC * 128))
    wu_r = Wu.reshape(8, CC, 128, FC, 128).transpose(0, 3, 2, 1, 4)
    su_r = Su.reshape(CC, 128, 4, FC, 128).transpose(2, 3, 1, 0, 4)
    wu_all = np.ascontiguousarray(
        np.concatenate([wu_r, su_r], axis=0).reshape(NE * FC, 128, CC * 128))
    wd_r = Wd.reshape(8, FC, 128, 4, 512).transpose(0, 3, 2, 1, 4)
    sd_r = Sd.reshape(4, FC, 128, 4, 512).transpose(0, 3, 2, 1, 4)
    wd_all = np.ascontiguousarray(
        np.concatenate([wd_r, sd_r], axis=0).reshape(NE * 4, 128, FC * 512))
    ident = np.eye(128, dtype=bf)

    x = np.asarray(inputs["L0_attn_x"], dtype=np.float32)
    in_maps = []
    for core in range(8):
        b, h = core // 2, core % 2
        xb = x[b]
        own = np.ascontiguousarray(
            xb.reshape(CC, 128, C)[:, 64 * h:64 * h + 64, :].reshape(TOWN, C))
        dmask_np = ((np.arange(64)[None, :] + 64 * h)
                    >= np.arange(128)[:, None]).astype(bf)
        in_maps.append(dict(
            x_kv=np.ascontiguousarray(xb),
            x_own=own,
            dmask=np.ascontiguousarray(dmask_np),
            ident=ident,
            wqk=wqk, wv4=wv4, battn=battn, wp=wp, bp_row=bp_row,
            wr=wr, wsg=wsg, wg=wg_all, wu=wu_all, wd=wd_all,
        ))
    return in_maps


def kernel(**inputs):
    in_maps = prep_inputs(inputs)
    nc = build_nc()
    res = run_bass_kernel_spmd(nc, in_maps, core_ids=list(range(8)))
    out = np.empty((4, 2048, C), dtype=np.float32)
    for core in range(8):
        b, h = core // 2, core % 2
        yc = np.asarray(res.results[core]["y"]).reshape(CC, 64, C)
        out[b].reshape(CC, 128, C)[:, 64 * h:64 * h + 64, :] = yc
    return out
